# version 09
# baseline (speedup 1.0000x reference)
import numpy as np
import ml_dtypes
from contextlib import ExitStack

import concourse.bass as bass
import concourse.mybir as mybir
from concourse.bass_utils import run_bass_kernel_spmd

F32 = mybir.dt.float32
BF16 = mybir.dt.bfloat16
AF = mybir.ActivationFunctionType
ALU = mybir.AluOpType
AX = mybir.AxisListType
BF16NP = ml_dtypes.bfloat16

P = 128



class Prog:
    ENGINES = ("sync", "scalar", "tensor", "vector", "gpsimd")
    ESEM = {"sync": "s_sp", "scalar": "s_act", "tensor": "s_pe",
            "vector": "s_dve", "gpsimd": "s_pool"}

    def __init__(self, nc, stack):
        self.nc = nc
        self.stack = stack
        self.ops = {e: [] for e in self.ENGINES}
        self.sems = {}
        self.counts = {}
        for e in self.ENGINES:
            self.sem(self.ESEM[e])

    def sem(self, name):
        if name not in self.sems:
            self.sems[name] = self.stack.enter_context(self.nc.semaphore(name))
            self.counts[name] = 0
        return name

    def op(self, eng, fn, waits=(), inc="auto", inc_amt=1):
        if inc == "auto":
            inc = self.ESEM[eng]
        ws = tuple(w for w in waits if w is not None)
        for (s, v) in ws:
            if s.startswith("q_"):
                assert v == self.counts[s], (
                    f"stale DMA token on stream {s}: tok {v} < issued {self.counts[s]}")
        tok = None
        if inc is not None:
            self.counts[inc] += inc_amt
            tok = (inc, self.counts[inc])
        self.ops[eng].append((ws, fn, tok, inc_amt if inc else 0))
        return tok

    def dma(self, eng, out, in_, waits=(), stream="d"):
        s = self.sem(f"q_{eng}_{stream}")
        return self.op(eng, lambda E: E.dma_start(out=out, in_=in_), waits, s, 16)

    def all_dma_toks(self):
        return [self.tok(s) for s in self.counts if s.startswith("q_") and self.counts[s] > 0]

    def tok(self, sname):
        self.sem(sname)
        return (sname, self.counts[sname])

    def barrier(self):
        toks = []
        dts = self.all_dma_toks()
        for e in self.ENGINES:
            be = self.sem("b_" + e)
            if e == "sync":
                toks.append(self.op(e, lambda E: E.nop(), waits=dts, inc=be))
            else:
                toks.append(self.op(e, lambda E: E.drain(), waits=dts, inc=be))
        for e in self.ENGINES:
            self.op(e, lambda E: E.nop(), waits=toks, inc=None)
        return toks

    def emit(self):
        nc = self.nc
        with nc.Block() as block:
            for eng in self.ENGINES:
                ops = self.ops[eng]

                def body(E, ops=ops):
                    waited = {}
                    for ws, fn, tok, amt in ops:
                        for (s, v) in ws:
                            if v > 0 and waited.get(s, 0) < v:
                                E.wait_ge(self.sems[s], v)
                                waited[s] = v
                        ins = fn(E)
                        if tok is not None:
                            ins.then_inc(self.sems[tok[0]], amt)
                getattr(block, eng)(body)


def ceil_div(a, b):
    return -(-a // b)


class _ArenaView:

    def __init__(self, t, off, nbytes):
        self.t = t
        self.off = off
        self.nbytes = nbytes


class _IntAlias:

    def __init__(self, d):
        self.d = d

    def _k(self, k):
        return f"pb{k}" if isinstance(k, int) else k

    def __getitem__(self, k):
        return self.d[self._k(k)]

    def __setitem__(self, k, v):
        self.d[self._k(k)] = v


class Ctx:

    def __init__(self, name="k", psum_spec=None):
        self.nc = bass.Bass("TRN2", target_bir_lowering=False)
        self.stack = ExitStack()
        self.pg = Prog(self.nc, self.stack)
        if psum_spec == "both":
            base = self.stack.enter_context(self.nc.psum_tensor("PS", [P, 4096], F32))
            self.pt = {
                "ps0": base[:, 0:1024], "ps1": base[:, 1024:2048],
                "av": base[:, 2048:3584], "sp": base[:, 3584:4096],
            }
            for i in range(8):
                self.pt[f"pb{i}"] = base[:, i * 512:(i + 1) * 512]
            self.bank_w = {k: None for k in self.pt}
            self.bank_r = {k: None for k in self.pt}
            self.pb = _IntAlias(self.pt)
            self.bank_w = _IntAlias(self.bank_w)
            self.bank_r = _IntAlias(self.bank_r)
            self.inputs = {}
            return
        if psum_spec is None:
            psum_spec = [(f"pb{i}", [P, 512]) for i in range(8)]
        self.pt = {}
        self.bank_w = {}
        self.bank_r = {}
        for nm, shape in psum_spec:
            self.pt[nm] = self.stack.enter_context(self.nc.psum_tensor(nm, shape, F32))
            self.bank_w[nm] = None
            self.bank_r[nm] = None
        self.pb = _IntAlias(self.pt)
        self.bank_w = _IntAlias(self.bank_w)
        self.bank_r = _IntAlias(self.bank_r)
        self.inputs = {}

    POOL_BYTES = 212800

    def _pool(self):
        if not hasattr(self, "_pool_t"):
            self._pool_t = self.stack.enter_context(
                self.nc.sbuf_tensor("POOL", [P, self.POOL_BYTES], mybir.dt.uint8))
            self._pool_off = 0
            self._pool_base = 0
        return self._pool_t

    def sb(self, name, shape, dtype):
        t = self._pool()
        assert len(shape) == 2
        pp, cols = shape
        esz = 2 if dtype in (BF16, mybir.dt.float16) else 4
        off = (self._pool_off + 63) // 64 * 64
        nb = cols * esz
        assert off + nb <= self.POOL_BYTES, (
            f"pool overflow at {name}: need {nb} at {off}")
        self._pool_off = off + nb
        v = t[:].bitcast(dtype)[:, off // esz: off // esz + cols]
        if pp != P:
            v = v[0:pp, :]
        return v

    def reset_banks(self, tok=None):
        keys = self.pt.keys() if isinstance(self.pt, dict) else self.pt.d.keys()
        for k in list(keys):
            self.bank_w[k] = tok
            self.bank_r[k] = tok

    def mark_persist(self):
        self._pool()
        self._pool_base = (self._pool_off + 63) // 64 * 64

    def reset_pool(self):
        self._pool_off = self._pool_base

    def arena(self, name, nbytes):
        nbytes = (nbytes + 63) // 64 * 64
        t = self._pool()
        off = (self._pool_off + 63) // 64 * 64
        assert off + nbytes <= self.POOL_BYTES, (
            f"pool overflow at arena {name}: need {nbytes} at {off}")
        self._pool_off = off + nbytes
        return _ArenaView(t, off, nbytes)

    @staticmethod
    def view(ar, byte_off, cols, dtype):
        esz = 2 if dtype == BF16 else 4
        assert (ar.off + byte_off) % esz == 0, (ar.off, byte_off, esz)
        base = (ar.off + byte_off) // esz
        return ar.t[:].bitcast(dtype)[:, base: base + cols]

    def inp(self, name, shape, dtype=F32):
        t = self.nc.dram_tensor(name, list(shape), dtype, kind="ExternalInput")
        self.inputs[name] = (tuple(shape), dtype)
        return t

    def outp(self, name, shape, dtype=F32):
        return self.nc.dram_tensor(name, list(shape), dtype, kind="ExternalOutput")

    def scratch(self, name, shape, dtype):
        return self.nc.dram_tensor(name, list(shape), dtype, kind="Internal")

    def pbf(self, i, n=1024):
        t = self.pt[i] if isinstance(i, str) else self.pb[i]
        return t[:].bitcast(BF16)[:, 0:n]

    def setup_consts(self):
        pg = self.pg
        identd = self.inp("c_ident", (P, P), BF16)
        identfd = self.inp("c_identf", (P, P), F32)
        self.ident = self.sb("identb", [P, P], BF16)
        self.identf = self.sb("identf", [P, P], F32)
        self.ones_col_b = self.sb("ones_col_b", [P, 1], BF16)
        self.ones_row_b = self.sb("ones_row_b", [1, P], BF16)
        self.ones_col_f = self.sb("ones_col_f", [P, 1], F32)
        self.ones_row_f = self.sb("ones_row_f", [1, P], F32)
        self.zer_b = self.sb("zer_b", [P, 512], BF16)
        t0 = pg.dma("sync", self.ident[:], identd.ap(), stream="c")
        t0b = pg.dma("sync", self.identf[:], identfd.ap(), stream="c")
        tz = pg.op("gpsimd", lambda E: E.memset(self.zer_b[:], 0.0))
        t1 = pg.op("gpsimd", lambda E: E.memset(self.ones_col_b[:], 1.0))
        t2 = pg.op("gpsimd", lambda E: E.memset(self.ones_row_b[:], 1.0),
                   waits=[t1])
        t3 = pg.op("gpsimd", lambda E: E.memset(self.ones_col_f[:], 1.0),
                   waits=[t2])
        t4 = pg.op("gpsimd", lambda E: E.memset(self.ones_row_f[:], 1.0),
                   waits=[t3])
        pg.barrier()

    def transpose(self, src, dst, bank, waits=(), evac="vector", evac_waits=()):
        pg = self.pg
        pp, m = src.shape[0], src.shape[1]
        if src.dtype == BF16:
            view = self.pbf(bank, P)
            idn = self.ident
        else:
            t = self.pt[bank] if isinstance(bank, str) else self.pb[bank]
            view = t[:]
            idn = self.identf
        wts = list(waits) + [self.bank_r[bank]]
        tw = pg.op("tensor",
                   lambda E: E.transpose(view[0:m, 0:pp], src, idn[0:pp, 0:pp]),
                   waits=wts)
        self.bank_w[bank] = tw
        tr = pg.op(evac, lambda E: E.tensor_copy(out=dst, in_=view[0:m, 0:pp]),
                   waits=[tw] + list(evac_waits))
        self.bank_r[bank] = tr
        return tr

    def finish(self, final_waits):
        pg = self.pg
        pg.op("gpsimd", lambda E: E.nop(), waits=list(final_waits), inc=None)
        pg.emit()
        self.stack.close()
        return self.nc


def emit_layernorm(ctx, sumbuf, D, stat, out_ap, out_dtype_note=None, waits=(),
                   sq_scratch=None):
    pg = ctx.pg
    t_mean = pg.op("vector",
                   lambda E: E.reduce_sum(stat[:, 0:1], sumbuf[:, 0:D], axis=AX.X),
                   waits=list(waits))
    t_sq = pg.op("scalar",
                 lambda E: E.activation(sq_scratch[:, 0:D], sumbuf[:, 0:D], AF.Square,
                                        accum_out=stat[:, 1:2]),
                 waits=list(waits) + [pg.tok("s_act")])
    t = pg.op("vector",
              lambda E: E.tensor_scalar_mul(stat[:, 2:3], stat[:, 0:1], 1.0 / D),
              waits=[t_mean, t_sq])
    t = pg.op("vector", lambda E: E.tensor_mul(stat[:, 3:4], stat[:, 2:3], stat[:, 2:3]),
              waits=[t])
    t = pg.op("vector",
              lambda E: E.tensor_scalar(stat[:, 4:5], stat[:, 1:2], 1.0 / D, None, ALU.mult),
              waits=[t])
    t = pg.op("vector", lambda E: E.tensor_sub(stat[:, 5:6], stat[:, 4:5], stat[:, 3:4]),
              waits=[t])
    t = pg.op("vector", lambda E: E.tensor_scalar_add(stat[:, 6:7], stat[:, 5:6], 1e-5),
              waits=[t])
    t_rt = pg.op("scalar", lambda E: E.sqrt(stat[:, 7:8], stat[:, 6:7]),
                 waits=[t, t_sq])
    t_rc = pg.op("vector", lambda E: E.reciprocal(stat[:, 7:8], stat[:, 7:8]),
                 waits=[t_rt, t])
    t_o = pg.op("vector",
                lambda E: E.tensor_scalar(out_ap, sumbuf[:, 0:D],
                                          stat[:, 2:3], stat[:, 7:8],
                                          ALU.subtract, ALU.mult),
                waits=[t_rc])
    return t_o



def build_l0_ffn(D=2048, HID=5120, T=2048, ctx=None, ext=None, x2bf=None):
    standalone = ctx is None
    if standalone:
        ctx = Ctx()
    pg = ctx.pg
    EX = (lambda n, sh: ctx.inp(n, sh)) if ext is None else (lambda n, sh: ext[n])
    DCH, HG, TT = D // P, HID // P, T // P
    TOKH = min(T, 1024)
    NH = T // TOKH
    TH_T = TOKH // P

    xs = EX("xslab", (T, D))
    wfc = EX("w_fc", (HID, D))
    wpr = EX("w_proj", (D, HID))
    out = ctx.outp("out", (T, D)) if ext is None else ext["out"]
    wfcT = ctx.scratch("wfcT", [HG, P, D], BF16)
    wprT = ctx.scratch("wprT", [HG, P, D], BF16)

    if standalone:
        ctx.setup_consts()

    wbuf = [ctx.sb("wbuf_0", [P, HID], F32)] * 2
    tern = ctx.sb("tern", [P, D], BF16)
    stage = ctx.sb("stage", [P, D], BF16)
    red = ctx.sb("red", [P, HG + 2], F32)
    scal = ctx.sb("scal", [P, 8], F32)
    tiny = ctx.sb("tiny", [1, 8], F32)
    x_fm = ctx.sb("x_fm", [P, DCH * TOKH], BF16)
    act = ctx.sb("act", [P, HG * TOKH], BF16)
    wrow = [ctx.sb(f"wrow{i}", [P, D], BF16) for i in range(2)]
    xbrow = [ctx.sb(f"xbrow{i}", [P, D], BF16) for i in range(2)]
    xf32 = [ctx.sb("xf32_0", [P, D], F32)] * 2
    sumbuf = [ctx.sb(f"sumbuf{i}", [P, D], F32) for i in range(2)]
    sqbuf = ctx.sb("sqbuf", [P, D], F32)
    q1 = sumbuf[0]
    q2 = sqbuf
    stat = ctx.sb("stat", [P, 8], F32)
    outbuf = [ctx.sb("outbuf_0", [P, D], F32)] * 2
    relu_t = ctx.sb("relu_t", [P, 512], F32)

    def absmean(w, ncols, which):
        ng = (w.shape[0]) // P
        toks = []
        for g in range(ng):
            b = wbuf[g % 2]
            w_prev = toks[g - 1] if g >= 1 else pg.tok("s_dve")
            t_l = pg.dma("sync", b[:, 0:ncols], w[g * P:(g + 1) * P, :], waits=[w_prev],
                         stream="w0")
            t_r = pg.op("vector",
                        lambda E, b=b, g=g: E.reduce_sum(
                            red[:, g:g + 1], b[:, 0:ncols],
                            axis=AX.X, apply_absolute_value=True),
                        waits=[t_l, pg.tok("s_dve")])
            toks.append(t_r)
        t_s = pg.op("vector",
                    lambda E, ng=ng: E.reduce_sum(red[:, ng:ng + 1], red[:, 0:ng], axis=AX.X),
                    waits=[toks[-1]])
        t_m = pg.op("tensor",
                    lambda E, which=which, ng=ng: E.matmul(
                        ctx.pb[7][0:1, which:which + 1],
                        ctx.ones_col_f[:],
                        red[:, ng:ng + 1],
                        start=True, stop=True),
                    waits=[t_s, ctx.bank_r[7]])
        ctx.bank_w[7] = t_m
        return t_m

    tm0 = absmean(wfc, D, 0)
    tm1 = absmean(wpr, HID, 2)
    t1 = pg.op("vector",
               lambda E: E.tensor_scalar(tiny[0:1, 0:1], ctx.pb[7][0:1, 0:1],
                                         1.0 / (HID * D), 1e-5, ALU.mult, ALU.max),
               waits=[tm0, tm1])
    t2 = pg.op("vector",
               lambda E: E.tensor_scalar(tiny[0:1, 1:2], ctx.pb[7][0:1, 2:3],
                                         1.0 / (HID * D), 1e-5, ALU.mult, ALU.max),
               waits=[tm0, tm1, t1])
    ctx.bank_r[7] = t2
    t_bc = pg.op("tensor",
                 lambda E: E.matmul(ctx.pb[6][:, 0:2], ctx.ones_row_f[:],
                                    tiny[0:1, 0:2],
                                    start=True, stop=True),
                 waits=[t2, ctx.bank_r[6]])
    ctx.bank_w[6] = t_bc
    t = pg.op("vector", lambda E: E.tensor_copy(scal[:, 0:1], ctx.pb[6][:, 0:1]), waits=[t_bc])
    t = pg.op("vector", lambda E: E.tensor_scalar_mul(scal[:, 1:2], ctx.pb[6][:, 0:1], 0.7),
              waits=[t_bc, t])
    t = pg.op("vector", lambda E: E.tensor_scalar_mul(scal[:, 2:3], ctx.pb[6][:, 0:1], -0.7),
              waits=[t_bc, t])
    t = pg.op("vector", lambda E: E.tensor_copy(scal[:, 3:4], ctx.pb[6][:, 1:2]),
              waits=[t_bc, t])
    t = pg.op("vector", lambda E: E.tensor_scalar_mul(scal[:, 4:5], ctx.pb[6][:, 1:2], 0.7),
              waits=[t_bc, t])
    t = pg.op("vector", lambda E: E.tensor_scalar_mul(scal[:, 5:6], ctx.pb[6][:, 1:2], -0.7),
              waits=[t_bc, t])
    ctx.bank_r[6] = t
    pg.barrier()

    def tern_transpose(w, wT, rows_are_h, scol):
        am = scal[:, scol:scol + 1]
        thr = scal[:, scol + 1:scol + 2]
        nthr = scal[:, scol + 2:scol + 3]
        prev_out = pg.tok(pg.sem("q_gpsimd_wT"))
        for hg in range(HG):
            b = wbuf[hg % 2]
            if rows_are_h:
                src = w[hg * P:(hg + 1) * P, :]
                dst = b[:, 0:D]
            else:
                src = w[:, hg * P:(hg + 1) * P].rearrange("(c p) h -> p c h", p=P)
                dst = b[:, 0:D].rearrange("p (c h) -> p c h", h=P)
            t_l = pg.dma("sync", dst, src,
                         waits=[prev_out, pg.tok("s_dve")], stream=f"w{hg % 2}")
            t_q = pg.op("vector",
                        lambda E, b=b: E.tensor_single_scalar(q1[:, 0:D], b[:, 0:D], thr, ALU.is_gt),
                        waits=[t_l, pg.tok("s_dve")])
            t_q = pg.op("vector",
                        lambda E, b=b: E.tensor_single_scalar(q2[:, 0:D], b[:, 0:D], nthr, ALU.is_lt),
                        waits=[t_l, t_q])
            t_q = pg.op("vector", lambda E: E.tensor_sub(q1[:, 0:D], q1[:, 0:D], q2[:, 0:D]),
                        waits=[t_q])
            t_q = pg.op("vector", lambda E: E.tensor_scalar_mul(tern[:, 0:D], q1[:, 0:D], am),
                        waits=[t_q])
            ev_toks = []
            for dc in range(DCH):
                ev_toks.append(ctx.transpose(tern[:, dc * P:(dc + 1) * P],
                                             stage[:, dc * P:(dc + 1) * P],
                                             bank=dc % 2, waits=[t_q],
                                             evac_waits=[prev_out]))
            prev_out = pg.dma("gpsimd", wT[hg], stage[:, 0:D], waits=ev_toks[-2:],
                              stream="wT")
        return prev_out

    tern_transpose(wfc, wfcT, True, 0)
    tern_transpose(wpr, wprT, False, 3)
    pg.barrier()

    out_toks = []
    for h in range(NH):
        for tt in range(TH_T):
            g_tt = h * TH_T + tt
            b = xbrow[tt % 2]
            t_l = pg.dma("gpsimd", b[:, 0:D], xs[g_tt * P:(g_tt + 1) * P, :],
                         waits=[pg.tok("s_dve") if tt >= 2 else None],
                         stream=f"x{tt % 2}")
            for dc in range(DCH):
                ctx.transpose(b[:, dc * P:(dc + 1) * P],
                              x_fm[:, dc * TOKH + tt * P: dc * TOKH + (tt + 1) * P],
                              bank=dc % 2, waits=[t_l])
        pg.barrier()

        for hg in range(HG):
            b = wrow[hg % 2]
            t_l = pg.dma("sync", b[:, 0:D], wfcT[hg],
                         waits=[pg.tok("s_pe") if hg >= 2 else None],
                         stream=f"w{hg % 2}")
            TB = min(512, TOKH)
            for tb in range(TOKH // TB):
                bank = tb % 2
                pt = ctx.pb[bank]
                mm = None
                for dc in range(DCH):
                    st, sp = dc == 0, dc == DCH - 1
                    wts = [t_l]
                    if st:
                        wts.append(ctx.bank_r[bank])
                    mm = pg.op("tensor",
                               lambda E, pt=pt, b=b, dc=dc, tb=tb, st=st, sp=sp:
                               E.matmul(pt[:, 0:TB], b[:, dc * P:(dc + 1) * P],
                                        x_fm[:, dc * TOKH + tb * TB: dc * TOKH + (tb + 1) * TB],
                                        start=st, stop=sp),
                               waits=wts)
                ctx.bank_w[bank] = mm
                t_r = pg.op("scalar",
                            lambda E, pt=pt: E.activation(relu_t[:, 0:TB], pt[:, 0:TB], AF.Relu),
                            waits=[mm, pg.tok("s_dve")])
                t_a = pg.op("vector",
                            lambda E, hg=hg, tb=tb: E.tensor_mul(
                                act[:, hg * TOKH + tb * TB: hg * TOKH + (tb + 1) * TB],
                                relu_t[:, 0:TB], relu_t[:, 0:TB]),
                            waits=[t_r])
                ctx.bank_r[bank] = t_a
        pg.barrier()

        NDB = ceil_div(D, 512)
        DBW = min(512, D)
        BLKT = min(256, TOKH)
        NSUB = BLKT // P
        assert NSUB * NDB <= 8
        for blk in range(TOKH // BLKT):
            mm_last = [[None] * NDB for _ in range(NSUB)]
            for hg in range(HG):
                b = wrow[hg % 2]
                t_l = pg.dma("sync", b[:, 0:D], wprT[hg],
                             waits=[pg.tok("s_pe") if hg >= 2 else None],
                             stream=f"w{hg % 2}")
                for sub in range(NSUB):
                    tt = h * TH_T + blk * NSUB + sub
                    lt = blk * BLKT + sub * P
                    for db in range(NDB):
                        bank = sub * NDB + db
                        st = hg == 0
                        wts = [t_l]
                        if st:
                            wts.append(ctx.bank_r[bank])
                        mm_last[sub][db] = pg.op(
                            "tensor",
                            lambda E, bank=bank, b=b, db=db, hg=hg, lt=lt, st=st:
                            E.matmul(ctx.pb[bank][:, 0:DBW],
                                     act[:, hg * TOKH + lt: hg * TOKH + lt + P],
                                     b[:, db * DBW:(db + 1) * DBW],
                                     start=st, stop=(hg == HG - 1)),
                            waits=wts)
            for sub in range(NSUB):
                tt = h * TH_T + blk * NSUB + sub
                g_tt = tt
                bx = xf32[sub]
                t_rx = pg.dma("sync", bx[:, 0:D], xs[g_tt * P:(g_tt + 1) * P, :],
                              waits=[pg.tok("s_dve")], stream=f"r{sub}")
                sb_ = sumbuf[sub]
                t_s = None
                for db in range(NDB):
                    bank = sub * NDB + db
                    ctx.bank_w[bank] = mm_last[sub][db]
                    t_s = pg.op("vector",
                                lambda E, bank=bank, db=db, bx=bx, sb_=sb_: E.tensor_add(
                                    sb_[:, db * DBW:(db + 1) * DBW],
                                    ctx.pb[bank][:, 0:DBW],
                                    bx[:, db * DBW:(db + 1) * DBW]),
                                waits=[mm_last[sub][db], t_rx, pg.tok("s_dve")])
                    ctx.bank_r[bank] = t_s
                ob = outbuf[sub]
                t_o = emit_layernorm(ctx, sb_, D, stat, ob[:, 0:D],
                                     waits=[t_s,
                                            out_toks[-1] if out_toks else None],
                                     sq_scratch=sqbuf)
                t_out = pg.dma("sync", out[g_tt * P:(g_tt + 1) * P, :], ob[:, 0:D],
                               waits=[t_o], stream=f"o{sub % 2}")
                out_toks.append(t_out)
                if x2bf is not None:
                    out_toks.append(pg.dma(
                        "gpsimd", x2bf[g_tt * P:(g_tt + 1) * P, :], ob[:, 0:D],
                        waits=[t_o], stream=f"ob{sub % 2}"))
        pg.barrier()

    if standalone:
        return ctx.finish(pg.all_dma_toks())
    return None





def emit_rope(pg, kn3, kr3, e3, cs, sn, t_n, HALF, ROPE, HD):
    t_r1 = pg.op("vector", lambda E: E.tensor_tensor(
        kr3[:, :, 0:HALF], kn3[:, :, 0:HALF], cs, ALU.mult), waits=[t_n, pg.tok("s_dve")])
    t_r2 = pg.op("vector", lambda E: E.tensor_tensor(
        e3, kn3[:, :, HALF:ROPE], sn, ALU.mult), waits=[t_r1])
    t_r3 = pg.op("vector", lambda E: E.tensor_tensor(
        kr3[:, :, 0:HALF], kr3[:, :, 0:HALF], e3, ALU.add), waits=[t_r2])
    t_r4 = pg.op("vector", lambda E: E.tensor_tensor(
        kr3[:, :, HALF:ROPE], kn3[:, :, HALF:ROPE], cs, ALU.mult), waits=[t_r3])
    t_r5 = pg.op("vector", lambda E: E.tensor_tensor(
        e3, kn3[:, :, 0:HALF], sn, ALU.mult), waits=[t_r4])
    t_r6 = pg.op("vector", lambda E: E.tensor_tensor(
        kr3[:, :, HALF:ROPE], kr3[:, :, HALF:ROPE], e3, ALU.subtract), waits=[t_r5])
    t_r7 = pg.op("vector", lambda E: E.tensor_copy(
        kr3[:, :, ROPE:HD], kn3[:, :, ROPE:HD]), waits=[t_r6])
    return t_r7



def build_l0_attn(D=2048, NH=16, NKV=8, HD=128, S=8192, T=2048, R=32,
                  ROPE=64, EPS=1.1920929e-07, ctx=None, ext=None):
    KVD = NKV * HD
    DCH = D // P
    KT = S // P
    TT = T // P
    QG = min(1024, T)
    NQG = T // QG
    QGT = QG // P
    HPH = NH // 2
    HPK = NH // NKV
    HALF = ROPE // 2
    SC = 1.0 / np.sqrt(HD)
    NB = float(np.sqrt(HD))

    standalone = ctx is None
    if standalone:
        ctx = Ctx(psum_spec=[("ps0", [P, 1024]), ("ps1", [P, 1024]),
                             ("av", [P, 1536]), ("sp", [P, 512])])
    pg = ctx.pg
    EX = (lambda n, sh: ctx.inp(n, sh)) if ext is None else (lambda n, sh: ext[n])

    xb = EX("xb", (S, D))
    Wq = EX("W_q", (D, D)); Wk = EX("W_k", (KVD, D))
    Wv = EX("W_v", (KVD, D)); Wo = EX("W_o", (D, D))
    Aq = EX("A_q", (R, D)); Bq = EX("B_q", (D, R))
    Ak = EX("A_k", (R, D)); Bk = EX("B_k", (KVD, R))
    Av = EX("A_v", (R, D)); Bv = EX("B_v", (KVD, R))
    Ao = EX("A_o", (R, D)); Bo = EX("B_o", (D, R))
    sq = EX("s_q", (1,)); sk = EX("s_k", (1,))
    sv = EX("s_v", (1,)); so = EX("s_o", (1,))
    qg_in = EX("q_gain", (NH,))
    cosk = EX("cosk", (S, HALF)); sink = EX("sink", (S, HALF))
    kbias = EX("kbias", (P, KT))
    bandm = EX("bandm", (P, 2 * QG - P))
    out = ctx.outp("out", (T, D)) if ext is None else ext["out"]
    kscr = ctx.scratch("kscr", [NKV, KT, P, P], BF16)
    vscr = ctx.scratch("vscr", [KT, P, NKV * 129], BF16)
    qscr = ctx.scratch("qscr", [NH, TT, P, P], BF16)
    yscr = ctx.scratch("yscr", [NH, TT, P, P], BF16)

    if standalone:
        ctx.setup_consts()

    QW = (NH // 2) * HD
    AR1B = 2 * DCH * max(T, D)
    AR2B = max(2 * DCH * 2 * KVD, 2 * DCH * QW, 2 * (KT * P + KT * 129 + T),
               8192 + 16 * D + 2 * NH * P)
    AR1 = ctx.arena("AR1", AR1B)
    AR2 = ctx.arena("AR2", AR2B)
    x_own = ctx.view(AR1, 0, DCH * T, BF16)
    wo_big = ctx.view(AR1, 0, DCH * D, BF16)
    wbig = ctx.view(AR2, 0, DCH * 2 * KVD, BF16)
    KH0 = ctx.view(AR2, 0, KT * P, BF16)
    VH0 = ctx.view(AR2, 2 * KT * P, KT * 129, BF16)
    QH0 = ctx.view(AR2, 2 * (KT * P + KT * 129), T, BF16)
    cos_sb = ctx.sb("cos_sb", [P, KT * HALF], F32)
    sin_sb = ctx.sb("sin_sb", [P, KT * HALF], F32)
    kb_sb = ctx.sb("kb_sb", [P, KT], F32)
    bias_hk = ctx.sb("bias_hk", [P, NH * KT], BF16)
    bandm_sb = ctx.sb("bandm_sb", [P, 2 * QG - P], BF16)
    gbc = ctx.sb("gbc", [P, NH], F32)
    mneg = ctx.sb("mneg", [P, NH], F32)
    lA = {}
    for nm in ("q", "k", "v", "o"):
        lA[nm] = ctx.sb(f"lA_{nm}", [P, DCH * R], BF16)
    lB = {}
    for nm, wdt in (("q", D), ("k", KVD), ("v", KVD), ("o", D)):
        lB[nm] = ctx.sb(f"lB_{nm}", [R, wdt], BF16)
    wld = [ctx.sb("wld0", [P, D], F32)] * 2
    wldb = ctx.sb("wldb", [P, D], BF16)
    xstg = [ctx.sb("xstg0", [P, DCH * P], BF16)] * 2
    xrow = [ctx.sb("xrow0", [P, D], BF16)] * 2
    epi = ctx.sb("epi", [P, max(KVD, 1024)], F32)
    epi2 = ctx.sb("epi2", [P, 32], F32)
    KNB = 4 * max(KVD, 1024)
    AR3 = ctx.arena("AR3", max(2 * KNB + 2 * NKV * 129 + 2 * NKV * P,
                               2 * KNB + (NH // 2) * P * 2,
                               2 * QG + 2 * P + 2 * (min(1024, T) // P) * P) + 64)
    knorm = ctx.view(AR3, 0, max(KVD, 1024), F32)
    krot = ctx.view(AR3, KNB, max(KVD, 1024), F32)
    vstg = ctx.view(AR3, 2 * KNB, NKV * 129, BF16)
    ktr = ctx.view(AR3, 2 * KNB + 2 * NKV * 129, NKV * P, BF16)
    tlora = ctx.sb("tlora", [R, 2 * P], BF16)
    qstg = ctx.view(AR3, 2 * KNB, (NH // 2) * P, BF16)
    PT = ctx.view(AR3, 0, QG, BF16)
    ystg = ctx.view(AR3, 2 * QG, P, BF16)
    ystag = ctx.view(AR3, 2 * QG + 2 * P, min(1024, T) // P * P, BF16)
    rcp = ctx.sb("rcp", [P, 1], F32)
    tiny = ctx.sb("tinyf", [1, max(NH, 8)], F32)
    xres = [ctx.view(AR2, 8192, D, F32)] * 2
    sumb = ctx.view(AR2, 8192 + 4 * D, D, F32)
    lnsq = ctx.view(AR2, 8192 + 8 * D, D, F32)
    stat = ctx.sb("stat", [P, 8], F32)
    obuf = [ctx.view(AR2, 8192 + 12 * D, D, F32)] * 2

    pg.dma("sync", cos_sb[:].rearrange("p (t c) -> p t c", c=HALF),
           cosk.ap().rearrange("(t p) c -> p t c", p=P), stream="c")
    pg.dma("sync", sin_sb[:].rearrange("p (t c) -> p t c", c=HALF),
           sink.ap().rearrange("(t p) c -> p t c", p=P), stream="c")
    pg.dma("sync", kb_sb[:], kbias.ap(), stream="c")
    pg.dma("gpsimd", bandm_sb[:], bandm.ap(), stream="c")
    t_g = pg.dma("sync", tiny[0:1, 0:NH], qg_in.ap().unsqueeze(0), stream="c")
    sc_t = ctx.sb("sc_t", [1, 4], F32)
    pg.dma("sync", sc_t[0:1, 0:1], sq.ap().unsqueeze(0), stream="c")
    pg.dma("sync", sc_t[0:1, 1:2], sk.ap().unsqueeze(0), stream="c")
    pg.dma("sync", sc_t[0:1, 2:3], sv.ap().unsqueeze(0), stream="c")
    t_sc = pg.dma("sync", sc_t[0:1, 3:4], so.ap().unsqueeze(0), stream="c")
    pg.barrier()
    t_mm = pg.op("tensor", lambda E: E.matmul(ctx.pt["sp"][:, 0:NH], ctx.ones_row_f[:],
                                              tiny[0:1, 0:NH], start=True, stop=True),
                 waits=[pg.tok("q_sync_c")])
    t2 = pg.op("vector", lambda E: E.tensor_copy(gbc[:, 0:NH], ctx.pt["sp"][:, 0:NH]),
               waits=[t_mm])
    t3 = pg.op("scalar", lambda E: E.activation(mneg[:, 0:NH], gbc[:, 0:NH], AF.Abs),
               waits=[t2])
    t4 = pg.op("vector", lambda E: E.tensor_scalar_mul(mneg[:, 0:NH], mneg[:, 0:NH], -NB),
               waits=[t3])
    t_mm2 = pg.op("tensor", lambda E: E.matmul(ctx.pt["sp"][:, 64:68], ctx.ones_row_f[:],
                                               sc_t[0:1, 0:4], start=True, stop=True),
                  waits=[t_sc, t2, t4])
    scbc = ctx.sb("scbc", [P, 4], F32)
    t6 = pg.op("vector", lambda E: E.tensor_copy(scbc[:, 0:4], ctx.pt["sp"][:, 64:68]),
               waits=[t_mm2])
    tb = t6
    for h in range(NH):
        tb = pg.op("vector",
                   lambda E, h=h: E.tensor_scalar_add(bias_hk[:, h * KT:(h + 1) * KT],
                                                      kb_sb[:, 0:KT], mneg[:, h:h + 1]),
                   waits=[tb, t4])
    pg.barrier()

    def prep_lora(Amat, Bmat, nm, outw, scol=None):
        t_l = pg.dma("sync", wld[0][0:R, 0:D], Amat.ap(), waits=[pg.tok("s_dve")],
                     stream="w0")
        t_c = pg.op("vector", lambda E: E.tensor_copy(wldb[0:R, 0:D], wld[0][0:R, 0:D]),
                    waits=[t_l])
        for dc in range(DCH):
            ctx.transpose(wldb[0:R, dc * P:(dc + 1) * P],
                          lA[nm][:, dc * R:(dc + 1) * R], "sp", waits=[t_c])
        t_l2 = pg.dma("sync", wld[1][:, 0:(outw // P) * R].rearrange(
            "p (c r) -> p c r", r=R),
            Bmat.ap().rearrange("(c p) r -> p c r", p=P),
            waits=[pg.tok("s_dve")], stream="w1")
        t_c2 = pg.op("vector", lambda E: E.tensor_copy(
            wldb[:, 0:(outw // P) * R], wld[1][:, 0:(outw // P) * R]),
            waits=[t_l2, pg.tok("s_pe")])
        for c in range(outw // P):
            ctx.transpose(wldb[:, c * R:(c + 1) * R],
                          lB[nm][0:R, c * P:(c + 1) * P], "sp", waits=[t_c2])

    prep_lora(Aq, Bq, "q", D)
    prep_lora(Ak, Bk, "k", KVD)
    prep_lora(Av, Bv, "v", KVD)
    prep_lora(Ao, Bo, "o", D)
    pg.barrier()

    KV2 = 2 * KVD
    def prep_w(Wmat, outw, col_off, stride, scale_col, dest, dest_onecol=None):
        for oc in range(outw // P):
            b = wld[oc % 2]
            t_l = pg.dma("sync", b[:, 0:D], Wmat[oc * P:(oc + 1) * P, :],
                         waits=[pg.tok("s_dve")], stream=f"w{oc % 2}")
            t_c = pg.op("vector",
                        lambda E, b=b: E.tensor_scalar_mul(wldb[:, 0:D], b[:, 0:D],
                                                           scbc[:, scale_col:scale_col + 1]),
                        waits=[t_l, pg.tok("s_dve")])
            for dc in range(DCH):
                ctx.transpose(wldb[:, dc * P:(dc + 1) * P],
                              dest[:, dc * stride + col_off + oc * P:
                                   dc * stride + col_off + (oc + 1) * P],
                              "sp", waits=[t_c])

    prep_w(Wk, KVD, 0, KV2, 1, wbig)
    prep_w(Wv, KVD, KVD, KV2, 2, wbig)
    pg.barrier()

    vst_init = pg.op("gpsimd", lambda E: E.memset(vstg[:], 1.0))
    rms_c = 1.0 / HD
    for kt in range(KT):
        own = kt < TT
        xr = xrow[kt % 2]
        t_lx = pg.dma("gpsimd", xr[:, 0:D], xb[kt * P:(kt + 1) * P, :],
                      waits=[pg.tok("s_pe")], stream=f"x{kt % 2}")
        if own:
            xfm = lambda dc, kt=kt: x_own[:, dc * T + kt * P: dc * T + (kt + 1) * P]
        else:
            stg = xstg[kt % 2]
            xfm = lambda dc, stg=stg: stg[:, dc * P:(dc + 1) * P]
        for dc in range(DCH):
            ctx.transpose(xr[:, dc * P:(dc + 1) * P], xfm(dc), "sp",
                          waits=[t_lx], evac_waits=[pg.tok("s_pe")])
        t_x = pg.tok("s_dve")
        for j, nm in enumerate(("k", "v")):
            mm = None
            for dc in range(DCH):
                mm = pg.op("tensor",
                           lambda E, dc=dc, j=j, nm=nm, xfm=xfm, st=(dc == 0), sp_=(dc == DCH - 1):
                           E.matmul(ctx.pt["av"][0:R, j * P:j * P + P],
                                    lA[nm][:, dc * R:(dc + 1) * R],
                                    xfm(dc), start=st, stop=sp_),
                           waits=[t_x, ctx.bank_r["av"] if dc == 0 else None])
            ctx.bank_w["av"] = mm
            tev = pg.op("vector",
                        lambda E, j=j: E.tensor_copy(tlora[0:R, j * P:(j + 1) * P],
                                                     ctx.pt["av"][0:R, j * P:j * P + P]),
                        waits=[mm, pg.tok("s_act")])
            ctx.bank_r["av"] = tev
        t_tl = pg.tok("s_dve")
        for j, (nm, ps) in enumerate((("k", "ps0"), ("v", "ps1"))):
            co = j * KVD
            mm = None
            for dc in range(DCH):
                for nb in range(ceil_div(KVD, 512)):
                    st = dc == 0
                    w_ = [t_x]
                    if st and nb == 0:
                        w_.append(ctx.bank_r[ps])
                    mm = pg.op("tensor",
                               lambda E, dc=dc, nb=nb, co=co, ps=ps, st=st, xfm=xfm:
                               E.matmul(ctx.pt[ps][:, nb * 512:min(KVD, (nb + 1) * 512)],
                                        xfm(dc),
                                        wbig[:, dc * KV2 + co + nb * 512:
                                             dc * KV2 + co + min(KVD, (nb + 1) * 512)],
                                        start=st, stop=False),
                               waits=w_)
            for nb in range(ceil_div(KVD, 512)):
                mm = pg.op("tensor",
                           lambda E, nb=nb, j=j, nm=nm, ps=ps:
                           E.matmul(ctx.pt[ps][:, nb * 512:min(KVD, (nb + 1) * 512)],
                                    tlora[0:R, j * P:(j + 1) * P],
                                    lB[nm][0:R, nb * 512:min(KVD, (nb + 1) * 512)],
                                    start=False, stop=True),
                           waits=[t_tl])
            ctx.bank_w[ps] = mm
        t_sq = pg.op("scalar",
                     lambda E: E.activation(epi[:, 0:KVD], ctx.pt["ps0"][:, 0:KVD],
                                            AF.Square),
                     waits=[ctx.bank_w["ps0"], pg.tok("s_dve")])
        t_red = pg.op("vector",
                      lambda E: E.reduce_sum(
                          epi2[:, 0:NKV],
                          epi[:, 0:KVD].rearrange("p (h d) -> p h d", d=HD), axis=AX.X),
                      waits=[t_sq])
        t_st = pg.op("vector",
                     lambda E: E.tensor_scalar(epi2[:, 8:8 + NKV], epi2[:, 0:NKV],
                                               rms_c, EPS, ALU.mult, ALU.add),
                     waits=[t_red])
        t_rt = pg.op("scalar", lambda E: E.sqrt(epi2[:, 16:16 + NKV], epi2[:, 8:8 + NKV]),
                     waits=[t_st, t_sq])
        t_rc = pg.op("vector", lambda E: E.reciprocal(epi2[:, 16:16 + NKV],
                                                      epi2[:, 16:16 + NKV]),
                     waits=[t_rt])
        t_n = pg.op("vector",
                    lambda E: E.tensor_tensor(
                        knorm[:, 0:KVD].rearrange("p (h d) -> p h d", d=HD),
                        ctx.pt["ps0"][:, 0:KVD].rearrange("p (h d) -> p h d", d=HD),
                        epi2[:, 16:16 + NKV].unsqueeze(2).broadcast_to(
                            [P, NKV, HD]),
                        ALU.mult),
                    waits=[t_rc])
        ctx.bank_r["ps0"] = t_n
        kn3 = knorm[:, 0:KVD].rearrange("p (h d) -> p h d", d=HD)
        kr3 = krot[:, 0:KVD].rearrange("p (h d) -> p h d", d=HD)
        cs = cos_sb[:, kt * HALF:(kt + 1) * HALF].unsqueeze(1).broadcast_to([P, NKV, HALF])
        sn = sin_sb[:, kt * HALF:(kt + 1) * HALF].unsqueeze(1).broadcast_to([P, NKV, HALF])
        e3 = epi[:, 0:NKV * HALF].rearrange("p (h c) -> p h c", c=HALF)
        t_r7 = emit_rope(pg, kn3, kr3, e3, cs, sn, t_n, HALF, ROPE, HD)
        tr_toks = []
        for hh in range(NKV):
            tr_toks.append(ctx.transpose(krot[:, hh * HD:hh * HD + P],
                                         ktr[:, hh * P:(hh + 1) * P], "sp",
                                         waits=[t_r7],
                                         evac_waits=[pg.tok("q_gpsimd_ks")
                                                     if kt > 0 else None]))
        pg.dma("gpsimd", kscr[:, kt].rearrange("h p n -> p h n"),
               ktr[:].rearrange("p (h n) -> p h n", n=P),
               waits=[tr_toks[-1]], stream="ks")
        t_v = pg.op("vector",
                    lambda E: E.tensor_copy(
                        vstg[:].rearrange("p (h n) -> p h n", n=129)[:, :, 0:P],
                        ctx.pt["ps1"][:, 0:KVD].rearrange("p (h d) -> p h d", d=HD)),
                    waits=[ctx.bank_w["ps1"], vst_init,
                           pg.tok("q_gpsimd_vs") if kt > 0 else None])
        ctx.bank_r["ps1"] = t_v
        pg.dma("gpsimd", vscr[kt], vstg[:], waits=[t_v], stream="vs")
    pg.barrier()

    for half in range(2):
        QW = HPH * HD
        for oc in range(QW // P):
            b = wld[oc % 2]
            goc = half * (QW // P) + oc
            t_l = pg.dma("sync", b[:, 0:D], Wq[goc * P:(goc + 1) * P, :],
                         waits=[pg.tok("s_dve")], stream=f"w{oc % 2}")
            t_c = pg.op("vector",
                        lambda E, b=b: E.tensor_scalar_mul(wldb[:, 0:D], b[:, 0:D],
                                                           scbc[:, 0:1]),
                        waits=[t_l, pg.tok("s_dve")])
            for dc in range(DCH):
                ctx.transpose(wldb[:, dc * P:(dc + 1) * P],
                              wbig[:, dc * QW + oc * P: dc * QW + (oc + 1) * P],
                              "sp", waits=[t_c], evac_waits=[pg.tok("s_pe")])
        pg.barrier()
        for tt in range(TT):
            xfm = lambda dc, tt=tt: x_own[:, dc * T + tt * P: dc * T + (tt + 1) * P]
            mm = None
            for dc in range(DCH):
                mm = pg.op("tensor",
                           lambda E, dc=dc, xfm=xfm, st=(dc == 0), sp_=(dc == DCH - 1):
                           E.matmul(ctx.pt["av"][0:R, 0:P],
                                    lA["q"][:, dc * R:(dc + 1) * R],
                                    xfm(dc), start=st, stop=sp_),
                           waits=[ctx.bank_r["av"] if dc == 0 else None])
            ctx.bank_w["av"] = mm
            tev = pg.op("vector",
                        lambda E: E.tensor_copy(tlora[0:R, 0:P], ctx.pt["av"][0:R, 0:P]),
                        waits=[mm, pg.tok("s_act")])
            ctx.bank_r["av"] = tev
            NBQ = ceil_div(QW, 512)
            QBW = min(QW, 512)
            mm = None
            for dc in range(DCH):
                for nb in range(NBQ):
                    st = dc == 0
                    w_ = []
                    if st and nb == 0:
                        w_.append(ctx.bank_r["ps0"])
                    mm = pg.op("tensor",
                               lambda E, dc=dc, nb=nb, st=st, xfm=xfm:
                               E.matmul(ctx.pt["ps0"][:, nb * QBW:(nb + 1) * QBW],
                                        xfm(dc),
                                        wbig[:, dc * QW + nb * QBW:
                                             dc * QW + (nb + 1) * QBW],
                                        start=st, stop=False),
                               waits=w_)
            for nb in range(NBQ):
                mm = pg.op("tensor",
                           lambda E, nb=nb, half=half:
                           E.matmul(ctx.pt["ps0"][:, nb * QBW:(nb + 1) * QBW],
                                    tlora[0:R, 0:P],
                                    lB["q"][0:R, half * QW + nb * QBW:
                                            half * QW + (nb + 1) * QBW],
                                    start=False, stop=True),
                           waits=[tev])
            ctx.bank_w["ps0"] = mm
            t_sq = pg.op("scalar",
                         lambda E: E.activation(epi[:, 0:QW], ctx.pt["ps0"][:, 0:QW],
                                                AF.Square),
                         waits=[mm, pg.tok("s_dve")])
            t_red = pg.op("vector",
                          lambda E: E.reduce_sum(
                              epi2[:, 0:HPH],
                              epi[:, 0:QW].rearrange("p (h d) -> p h d", d=HD),
                              axis=AX.X),
                          waits=[t_sq])
            t_st = pg.op("vector",
                         lambda E: E.tensor_scalar(epi2[:, 8:8 + HPH], epi2[:, 0:HPH],
                                                   rms_c, EPS, ALU.mult, ALU.add),
                         waits=[t_red])
            t_rt = pg.op("scalar", lambda E: E.sqrt(epi2[:, 16:16 + HPH],
                                                    epi2[:, 8:8 + HPH]),
                         waits=[t_st, t_sq])
            t_rc = pg.op("vector", lambda E: E.reciprocal(epi2[:, 16:16 + HPH],
                                                          epi2[:, 16:16 + HPH]),
                         waits=[t_rt])
            t_gm = pg.op("vector",
                         lambda E, half=half: E.tensor_mul(
                             epi2[:, 24:24 + HPH], epi2[:, 16:16 + HPH],
                             gbc[:, half * HPH:(half + 1) * HPH]),
                         waits=[t_rc])
            t_n = pg.op("vector",
                        lambda E: E.tensor_tensor(
                            knorm[:, 0:QW].rearrange("p (h d) -> p h d", d=HD),
                            ctx.pt["ps0"][:, 0:QW].rearrange("p (h d) -> p h d", d=HD),
                            epi2[:, 24:24 + HPH].unsqueeze(2).broadcast_to(
                                [P, HPH, HD]),
                            ALU.mult),
                        waits=[t_gm])
            ctx.bank_r["ps0"] = t_n
            kn3 = knorm[:, 0:QW].rearrange("p (h d) -> p h d", d=HD)
            kr3 = krot[:, 0:QW].rearrange("p (h d) -> p h d", d=HD)
            cs = cos_sb[:, tt * HALF:(tt + 1) * HALF].unsqueeze(1).broadcast_to([P, HPH, HALF])
            sn = sin_sb[:, tt * HALF:(tt + 1) * HALF].unsqueeze(1).broadcast_to([P, HPH, HALF])
            e3 = epi[:, 0:HPH * HALF].rearrange("p (h c) -> p h c", c=HALF)
            t_r7 = emit_rope(pg, kn3, kr3, e3, cs, sn, t_n, HALF, ROPE, HD)
            for hh in range(HPH):
                ctx.transpose(krot[:, hh * HD:hh * HD + P],
                              qstg[:, hh * P:(hh + 1) * P], "sp",
                              waits=[t_r7],
                              evac_waits=[pg.tok("q_gpsimd_qs")])
            pg.dma("gpsimd", qscr[half * HPH:(half + 1) * HPH, tt].rearrange(
                "h p n -> p h n"),
                qstg[:].rearrange("p (h n) -> p h n", n=P),
                waits=[pg.tok("s_dve")], stream="qs")
        pg.barrier()

    def kt_list(qgi):
        own_band = list(range(qgi * QG // P, (qgi + 1) * QG // P))
        own_full = list(range(0, qgi * QG // P))
        tail = list(range(TT, KT))
        return own_full, own_band, tail

    avbase = lambda i: (i // 3) * 512 + (i % 3) * 129

    for kv in range(NKV):
        t_kh = pg.dma("sync", KH0.rearrange("p (t n) -> p t n", n=P),
                      kscr[kv].rearrange("t p n -> p t n"),
                      waits=[pg.tok("s_pe")], stream="kh")
        t_vh = pg.dma("sync", VH0.rearrange("p (t n) -> p t n", n=129),
                      vscr[:, :, kv * 129:(kv + 1) * 129].rearrange("t p n -> p t n"),
                      waits=[pg.tok("s_pe")], stream="vh")
        for hh in range(HPK):
            h = kv * HPK + hh
            t_qh = pg.dma("sync", QH0.rearrange("p (t n) -> p t n", n=P),
                          qscr[h].rearrange("t p n -> p t n"),
                          waits=[pg.tok("s_pe")], stream="qh")
            for qgi in range(NQG):
                own_full, own_band, tail = kt_list(qgi)
                kts = own_full + own_band + tail
                first_kt = kts[0]
                last_kt = kts[-1]
                psn = ["ps0", "ps1"]
                nav_banks = ceil_div(avbase(QGT - 1) + 129, 512)
                mmz = None
                for bkk in range(nav_banks):
                    mmz = pg.op("tensor",
                                lambda E, bkk=bkk: E.matmul(
                                    ctx.pt["av"][:, bkk * 512:(bkk + 1) * 512],
                                    ctx.ident[:], ctx.zer_b[:, 0:512],
                                    start=True, stop=True),
                                waits=[ctx.bank_r["av"] if bkk == 0 else None])
                ctx.bank_w["av"] = mmz
                for ki, kt in enumerate(kts):
                    ps = psn[ki % 2]
                    mm = None
                    for nb in range(QG // 512 if QG >= 512 else 1):
                        NBW = min(512, QG)
                        w_ = [t_kh, t_qh, ctx.bank_r[ps] if nb == 0 else None]
                        mm = pg.op("tensor",
                                   lambda E, ps=ps, kt=kt, nb=nb, NBW=NBW, h=h, qgi=qgi:
                                   E.matmul(ctx.pt[ps][:, nb * NBW:(nb + 1) * NBW],
                                            KH0[:, kt * P:(kt + 1) * P],
                                            QH0[:, qgi * QG + nb * NBW:
                                                qgi * QG + (nb + 1) * NBW],
                                            start=True, stop=True),
                                   waits=w_)
                    ctx.bank_w[ps] = mm
                    t_pre = mm
                    if kt in own_band:
                        s = kt * P - qgi * QG
                        off = (QG - P) - s
                        t_pre = pg.op("vector",
                                      lambda E, ps=ps, off=off: E.tensor_add(
                                          ctx.pt[ps][:, 0:QG],
                                          ctx.pt[ps][:, 0:QG],
                                          bandm_sb[:, off:off + QG]),
                                      waits=[mm, pg.tok("s_dve")])
                    t_e = pg.op("scalar",
                                lambda E, ps=ps, h=h, kt=kt: E.activation(
                                    PT[:, 0:QG], ctx.pt[ps][:, 0:QG], AF.Exp,
                                    bias=bias_hk[:, h * KT + kt:h * KT + kt + 1],
                                    scale=SC),
                                waits=[t_pre, pg.tok("s_pe"), pg.tok("s_act")])
                    ctx.bank_r[ps] = t_e
                    for i in range(QGT):
                        w_ = [t_e, t_vh, t_qh]
                        mm2 = pg.op("tensor",
                                    lambda E, i=i, kt=kt: E.matmul(
                                        ctx.pt["av"][:, avbase(i):avbase(i) + 129],
                                        PT[:, i * P:(i + 1) * P],
                                        VH0[:, kt * 129:(kt + 1) * 129],
                                        start=False, stop=(kt == last_kt),
                                        skip_group_check=True),
                                    waits=w_)
                    ctx.bank_w["av"] = mm2
                last_r = None
                for i in range(QGT):
                    t_rc2 = pg.op("vector",
                                  lambda E, i=i: E.reciprocal(
                                      rcp[:, 0:1],
                                      ctx.pt["av"][:, avbase(i) + P:avbase(i) + 129]),
                                  waits=[ctx.bank_w["av"], pg.tok("s_dve")])
                    t_y = pg.op("vector",
                                lambda E, i=i: E.tensor_scalar_mul(
                                    ystg[:, 0:P],
                                    ctx.pt["av"][:, avbase(i):avbase(i) + P],
                                    rcp[:, 0:1]),
                                waits=[t_rc2, pg.tok("s_pe")])
                    last_r = ctx.transpose(
                        ystg[:, 0:P],
                        ystag[:, i * P:(i + 1) * P],
                        "sp", waits=[t_y],
                        evac_waits=[pg.tok("q_gpsimd_ys")])
                ctx.bank_r["av"] = last_r
                pg.dma("gpsimd",
                       yscr[h, qgi * QGT:(qgi + 1) * QGT].rearrange("t p n -> p t n"),
                       ystag[:, 0:QGT * P].rearrange("p (t n) -> p t n", n=P),
                       waits=[last_r], stream="ys")
    pg.barrier()

    for oc in range(D // P):
        b = wld[oc % 2]
        t_l = pg.dma("sync", b[:, 0:D], Wo[oc * P:(oc + 1) * P, :],
                     waits=[pg.tok("s_dve")], stream=f"w{oc % 2}")
        t_c = pg.op("vector",
                    lambda E, b=b: E.tensor_scalar_mul(wldb[:, 0:D], b[:, 0:D],
                                                       scbc[:, 3:4]),
                    waits=[t_l, pg.tok("s_dve")])
        for dc in range(DCH):
            ctx.transpose(wldb[:, dc * P:(dc + 1) * P],
                          wo_big[:, dc * D + oc * P: dc * D + (oc + 1) * P],
                          "sp", waits=[t_c], evac_waits=[pg.tok("s_pe")])
    pg.barrier()
    out_toks = []
    OH = D // 2
    ybufs = [ctx.view(AR2, 8192 + 16 * D, NH * P, BF16)] * 2
    for tt in range(TT):
        yb = ybufs[tt % 2]
        t_yl = pg.dma("sync", yb[:].rearrange("p (h n) -> p h n", n=P),
                      yscr[:, tt].rearrange("h p n -> p h n"),
                      waits=[pg.tok("s_pe")], stream=f"yl{tt % 2}")
        yfm = lambda dc, yb=yb: yb[:, dc * P:(dc + 1) * P]
        mm = None
        for dc in range(DCH):
            mm = pg.op("tensor",
                       lambda E, dc=dc, yfm=yfm, st=(dc == 0), sp_=(dc == DCH - 1):
                       E.matmul(ctx.pt["av"][0:R, 0:P],
                                lA["o"][:, dc * R:(dc + 1) * R],
                                yfm(dc), start=st, stop=sp_),
                       waits=[t_yl, ctx.bank_r["av"] if dc == 0 else None])
        ctx.bank_w["av"] = mm
        tev = pg.op("vector",
                    lambda E: E.tensor_copy(tlora[0:R, 0:P], ctx.pt["av"][0:R, 0:P]),
                    waits=[mm, pg.tok("s_act")])
        ctx.bank_r["av"] = tev
        NBO = ceil_div(OH, 512)
        OBW = min(OH, 512)
        for halfo, ps in enumerate(("ps0", "ps1")):
            mm = None
            for dc in range(DCH):
                for nb in range(NBO):
                    st = dc == 0
                    w_ = [t_yl]
                    if st and nb == 0:
                        w_.append(ctx.bank_r[ps])
                    mm = pg.op("tensor",
                               lambda E, dc=dc, nb=nb, halfo=halfo, ps=ps, st=st, yfm=yfm:
                               E.matmul(ctx.pt[ps][:, nb * OBW:(nb + 1) * OBW],
                                        yfm(dc),
                                        wo_big[:, dc * D + halfo * OH + nb * OBW:
                                               dc * D + halfo * OH + (nb + 1) * OBW],
                                        start=st, stop=False),
                               waits=w_)
            for nb in range(NBO):
                mm = pg.op("tensor",
                           lambda E, nb=nb, halfo=halfo, ps=ps:
                           E.matmul(ctx.pt[ps][:, nb * OBW:(nb + 1) * OBW],
                                    tlora[0:R, 0:P],
                                    lB["o"][0:R, halfo * OH + nb * OBW:
                                            halfo * OH + (nb + 1) * OBW],
                                    start=False, stop=True),
                           waits=[tev])
            ctx.bank_w[ps] = mm
        bx = xres[tt % 2]
        t_rx = pg.dma("sync", bx[:, 0:D], xb[tt * P:(tt + 1) * P, :],
                      waits=[pg.tok("s_dve")], stream=f"r{tt % 2}")
        t_s = None
        for halfo, ps in enumerate(("ps0", "ps1")):
            t_s = pg.op("vector",
                        lambda E, ps=ps, halfo=halfo, bx=bx: E.tensor_add(
                            sumb[:, halfo * OH:(halfo + 1) * OH],
                            ctx.pt[ps][:, 0:OH],
                            bx[:, halfo * OH:(halfo + 1) * OH]),
                        waits=[ctx.bank_w[ps], t_rx, pg.tok("s_dve")])
            ctx.bank_r[ps] = t_s
        ob = obuf[tt % 2]
        t_o = emit_layernorm(ctx, sumb, D, stat, ob[:, 0:D],
                             waits=[t_s, out_toks[-1] if out_toks else None],
                             sq_scratch=lnsq)
        t_out = pg.dma("sync", out[tt * P:(tt + 1) * P, :], ob[:, 0:D],
                       waits=[t_o], stream=f"o{tt % 2}")
        out_toks.append(t_out)

    if standalone:
        return ctx.finish(pg.all_dma_toks())
    return None





def build_l1_attn(D=2048, NH=16, HD=128, KVC=512, QC=1024, S=8192, T=2048,
                  ctx=None, ext=None, split=False):
    DCH = D // P
    KT = S // P
    TT = T // P
    LTC = KVC // P
    QCC = QC // P
    QG = min(1024, T)
    NQG = T // QG
    QGT = QG // P
    HALF1 = HD // 2
    SC = 1.0 / float(np.sqrt(2 * HD))
    NOB = ceil_div(D, 512)

    standalone = ctx is None
    if standalone:
        ctx = Ctx(psum_spec=[("ps0", [P, 1024]), ("ps1", [P, 1024]),
                             ("av", [P, 1536]), ("sp", [P, 512])])
    pg = ctx.pg
    EX = (lambda n, sh: ctx.inp(n, sh)) if ext is None else (lambda n, sh: ext[n])

    if split:
        xloc = ext["xloc"]
        xg = ext["xg"]
        xg_bf = True
        cos1q = ext["cos1q"]
        sin1q = ext["sin1q"]
    else:
        xb = EX("xb", (S, D))
        xloc = xb
        xg = xb
        xg_bf = False
        cos1q = None
    Wkvd = EX("w_kv_down", (D, KVC))
    Wupk = EX("w_kv_up_k", (KVC, D))
    Wupv = EX("w_kv_up_v", (KVC, D))
    Wqd = EX("w_q_down", (D, QC))
    Wqu = EX("w_q_up", (QC, D))
    Wkr = EX("w_k_rope", (D, HD))
    Wqr = EX("w_q_rope", (D, HD))
    Wout = EX("w_out", (D, D))
    cos1 = EX("cos1", (S, HALF1))
    sin1 = EX("sin1", (S, HALF1))
    out = ctx.outp("out", (T, D)) if ext is None else ext["out"]
    yscr = ctx.scratch("yscr1", [NH, TT, P, P], BF16)

    if standalone:
        ctx.setup_consts()

    ARXB = max(2 * DCH * QC + 2 * DCH * P,
               2 * DCH * KVC + 2 * DCH * HD + 2 * DCH * 512 + 2048 + 4 * HALF1 + 64,
               2 * KT * P + 2 * KT * 129 + 2 * (2 * LTC + QCC) * HD
               + 2 * QG + 2 * P + 2 * QGT * P + 256,
               16 * D + 2 * NH * P + 256)
    ARX = ctx.arena("ARX1", ARXB)
    ARY = ctx.arena("ARY1", 2 * max(LTC * S, DCH * D))
    lat_fm = ctx.view(ARY, 0, LTC * S, BF16)
    wo_sb = ctx.view(ARY, 0, DCH * D, BF16)
    wqd_sb = ctx.view(ARX, 0, DCH * QC, BF16)
    xstg_c1 = ctx.view(ARX, 2 * DCH * QC, DCH * P, BF16)
    wkvd_sb = ctx.view(ARX, 0, DCH * KVC, BF16)
    wkr_sb = ctx.view(ARX, 2 * DCH * KVC, DCH * HD, BF16)
    xstg_c2 = ctx.view(ARX, 2 * DCH * (KVC + HD), DCH * 512, BF16)
    KH0 = ctx.view(ARX, 0, KT * P, BF16)
    VH0 = ctx.view(ARX, 2 * KT * P, KT * 129, BF16)
    upk_h = ctx.view(ARX, 2 * (KT * P + KT * 129), LTC * HD, BF16)
    upv_h = ctx.view(ARX, 2 * (KT * P + KT * 129) + 2 * LTC * HD, LTC * HD, BF16)
    wqu_h = ctx.view(ARX, 2 * (KT * P + KT * 129) + 4 * LTC * HD, QCC * HD, BF16)
    _c3t = 2 * (KT * P + KT * 129) + 4 * LTC * HD + 2 * QCC * HD
    xres = ctx.view(ARX, 0, D, F32)
    sumb = ctx.view(ARX, 4 * D, D, F32)
    lnsq = ctx.view(ARX, 8 * D, D, F32)
    obuf = ctx.view(ARX, 12 * D, D, F32)
    Y_t = ctx.view(ARX, 16 * D, NH * P, BF16)

    t1fm = ctx.sb("t1fm1", [P, QCC * T], BF16)
    kr_fm = ctx.sb("kr_fm1", [P, S], BF16)
    qr_fm = ctx.sb("qr_fm1", [P, T], BF16)
    qch = ctx.sb("qch1", [P, T], BF16)
    cos_sb = ctx.sb("cos1_sb", [P, KT * HALF1], F32)
    sin_sb = ctx.sb("sin1_sb", [P, KT * HALF1], F32)
    cosq_sb, sinq_sb = cos_sb, sin_sb
    wqr_sb = ctx.sb("wqr_sb1", [P, DCH * HD], BF16)
    xrow = [ctx.sb("xrow1_0", [P, D], BF16)] * 2
    t1tok = ctx.sb("t1tok1", [P, QC], BF16)
    qrtok = ctx.sb("qrtok1", [P, HD], F32)
    qrrot = ctx.sb("qrrot1", [P, HD], F32)
    krtok = qrtok
    krrot = qrrot
    krstg = ctx.view(ARX, 2 * DCH * (KVC + HD) + 2 * DCH * 512, 512, F32)
    _rop_off = ((max(2 * DCH * QC + 2 * DCH * P,
                     2 * DCH * (KVC + HD) + 2 * DCH * 512 + 2048) + 63) // 64) * 64
    ropes = ctx.view(ARX, _rop_off, HALF1, F32)
    sqrow = ctx.sb("sqrow1", [P, 512], BF16)
    sca = ctx.sb("sca1", [1, 16], F32)
    mb_h = ctx.sb("mb_h1", [P, 1], F32)
    PT = ctx.view(ARX, _c3t, QG, BF16)
    ystg = ctx.view(ARX, _c3t + 2 * QG, P, BF16)
    ystag = ctx.view(ARX, _c3t + 2 * QG + 2 * P, QGT * P, BF16)
    rcp = ctx.sb("rcp1", [P, 1], F32)
    stat = ctx.sb("stat1", [P, 8], F32)

    pg.dma("gpsimd", wqd_sb.rearrange("p (c n) -> p c n", n=QC),
           Wqd.ap().rearrange("(c p) n -> p c n", p=P), stream="w0")
    pg.dma("gpsimd", wqr_sb.rearrange("p (c n) -> p c n", n=HD),
           Wqr.ap().rearrange("(c p) n -> p c n", p=P), stream="w1")
    if split:
        pg.dma("sync", cos_sb[:, 0:TT * HALF1].rearrange("p (t c) -> p t c", c=HALF1),
               cos1q.ap().rearrange("(t p) c -> p t c", p=P), stream="w2")
        pg.dma("sync", sin_sb[:, 0:TT * HALF1].rearrange("p (t c) -> p t c", c=HALF1),
               sin1q.ap().rearrange("(t p) c -> p t c", p=P), stream="w3")
    else:
        pg.dma("sync", cos_sb[:].rearrange("p (t c) -> p t c", c=HALF1),
               cos1.ap().rearrange("(t p) c -> p t c", p=P), stream="w2")
        pg.dma("sync", sin_sb[:].rearrange("p (t c) -> p t c", c=HALF1),
               sin1.ap().rearrange("(t p) c -> p t c", p=P), stream="w3")
    pg.barrier()

    def rope1(src, dst, tl, t_in, tabs=None):
        cs_t, sn_t = tabs if tabs is not None else (cos_sb, sin_sb)
        s2 = src.rearrange("p (c two) -> p c two", two=2)
        d2 = dst.rearrange("p (c two) -> p c two", two=2)
        cs = cs_t[:, tl * HALF1:(tl + 1) * HALF1]
        sn = sn_t[:, tl * HALF1:(tl + 1) * HALF1]
        e2 = ropes[:, 0:HALF1]
        t = pg.op("vector", lambda E: E.tensor_tensor(
            d2[:, :, 0], s2[:, :, 0], cs, ALU.mult), waits=[t_in, pg.tok("s_dve")])
        t = pg.op("vector", lambda E: E.tensor_tensor(
            e2, s2[:, :, 1], sn, ALU.mult), waits=[t])
        t = pg.op("vector", lambda E: E.tensor_tensor(
            d2[:, :, 0], d2[:, :, 0], e2, ALU.subtract), waits=[t])
        t = pg.op("vector", lambda E: E.tensor_tensor(
            d2[:, :, 1], s2[:, :, 1], cs, ALU.mult), waits=[t])
        t = pg.op("vector", lambda E: E.tensor_tensor(
            e2, s2[:, :, 0], sn, ALU.mult), waits=[t])
        t = pg.op("vector", lambda E: E.tensor_tensor(
            d2[:, :, 1], d2[:, :, 1], e2, ALU.add), waits=[t])
        return t

    def max_norm_pass(src_bf, width, out_col, t_in, init):
        t = t_in
        if init:
            t = pg.op("vector", lambda E: E.memset(sca[0:1, out_col:out_col + 1], 0.0),
                      waits=[t_in, pg.tok("s_dve")])
        for off in range(0, width, 512):
            w = min(512, width - off)
            t_sq = pg.op("scalar",
                         lambda E, off=off, w=w: E.activation(
                             sqrow[:, 0:w], src_bf[:, off:off + w], AF.Square),
                         waits=[t, pg.tok("s_act"), pg.tok("s_dve")])
            t_mm = pg.op("tensor",
                         lambda E, w=w: E.matmul(ctx.pt["sp"][0:1, 0:w],
                                                 ctx.ones_col_b[:], sqrow[:, 0:w],
                                                 start=True, stop=True),
                         waits=[t_sq, ctx.bank_r["sp"]])
            ctx.bank_w["sp"] = t_mm
            t_mx = pg.op("vector",
                         lambda E, w=w: E.reduce_max(sca[0:1, 15:16],
                                                     ctx.pt["sp"][0:1, 0:w], axis=AX.X),
                         waits=[t_mm])
            ctx.bank_r["sp"] = t_mx
            t = pg.op("vector",
                      lambda E: E.tensor_max(sca[0:1, out_col:out_col + 1],
                                             sca[0:1, out_col:out_col + 1],
                                             sca[0:1, 15:16]),
                      waits=[t_mx])
        return t

    for tt in range(TT):
        xr = xrow[tt % 2]
        t_lx = pg.dma("gpsimd", xr[:, 0:D], xloc[tt * P:(tt + 1) * P, :],
                      waits=[pg.tok("s_pe")], stream=f"x{tt % 2}")
        for dc in range(DCH):
            ctx.transpose(xr[:, dc * P:(dc + 1) * P], xstg_c1[:, dc * P:(dc + 1) * P],
                          "sp", waits=[t_lx], evac_waits=[pg.tok("s_pe")])
        t_x = pg.tok("s_dve")
        NBQ = ceil_div(QC, 512)
        QBW = min(QC, 512)
        mm = None
        for dc in range(DCH):
            for nb in range(NBQ):
                st = dc == 0
                w_ = [t_x]
                if st and nb == 0:
                    w_.append(ctx.bank_r["ps0"])
                mm = pg.op("tensor",
                           lambda E, dc=dc, nb=nb, st=st:
                           E.matmul(ctx.pt["ps0"][:, nb * QBW:(nb + 1) * QBW],
                                    xstg_c1[:, dc * P:(dc + 1) * P],
                                    wqd_sb[:, dc * QC + nb * QBW: dc * QC + (nb + 1) * QBW],
                                    start=st, stop=(dc == DCH - 1)),
                           waits=w_)
        ctx.bank_w["ps0"] = mm
        t_ev = pg.op("vector", lambda E: E.tensor_copy(t1tok[:, 0:QC],
                                                       ctx.pt["ps0"][:, 0:QC]),
                     waits=[mm, pg.tok("s_dve")])
        ctx.bank_r["ps0"] = t_ev
        for c in range(QCC):
            ctx.transpose(t1tok[:, c * P:(c + 1) * P],
                          t1fm[:, c * T + tt * P: c * T + (tt + 1) * P],
                          "sp", waits=[t_ev], evac_waits=[pg.tok("s_pe")])
        mm = None
        for dc in range(DCH):
            st = dc == 0
            w_ = [t_x]
            if st:
                w_.append(ctx.bank_r["av"])
            mm = pg.op("tensor",
                       lambda E, dc=dc, st=st:
                       E.matmul(ctx.pt["av"][:, 0:HD], xstg_c1[:, dc * P:(dc + 1) * P],
                                wqr_sb[:, dc * HD:(dc + 1) * HD],
                                start=st, stop=(dc == DCH - 1)),
                       waits=w_)
        ctx.bank_w["av"] = mm
        t_qr = pg.op("vector", lambda E: E.tensor_copy(qrtok[:, 0:HD],
                                                       ctx.pt["av"][:, 0:HD]),
                     waits=[mm, pg.tok("s_dve")])
        ctx.bank_r["av"] = t_qr
        t_ro = rope1(qrtok[:, 0:HD], qrrot[:, 0:HD], tt, t_qr,
                     tabs=(cosq_sb, sinq_sb))
        ctx.transpose(qrrot[:, 0:HD], qr_fm[:, tt * P:(tt + 1) * P], "sp",
                      waits=[t_ro])
    t_q2m = max_norm_pass(qr_fm[:], T, 0, pg.tok("s_dve"), True)
    pg.barrier()

    pg.dma("gpsimd", wkvd_sb.rearrange("p (c n) -> p c n", n=KVC),
           Wkvd.ap().rearrange("(c p) n -> p c n", p=P), stream="w0")
    pg.dma("gpsimd", wkr_sb.rearrange("p (c n) -> p c n", n=HD),
           Wkr.ap().rearrange("(c p) n -> p c n", p=P), stream="w1")
    if split:
        pg.dma("sync", cos_sb[:].rearrange("p (t c) -> p t c", c=HALF1),
               cos1.ap().rearrange("(t p) c -> p t c", p=P), stream="w2")
        pg.dma("sync", sin_sb[:].rearrange("p (t c) -> p t c", c=HALF1),
               sin1.ap().rearrange("(t p) c -> p t c", p=P), stream="w3")
    pg.barrier()
    for blk in range(S // 512):
        for j in range(4):
            kt = blk * 4 + j
            xr = xrow[j % 2]
            t_lx = pg.dma("sync" if xg_bf else "gpsimd",
                          xr[:, 0:D], xg[kt * P:(kt + 1) * P, :],
                          waits=[pg.tok("s_pe")], stream=f"x{j % 2}")
            for dc in range(DCH):
                ctx.transpose(xr[:, dc * P:(dc + 1) * P],
                              xstg_c2[:, dc * 512 + j * P: dc * 512 + (j + 1) * P],
                              "sp", waits=[t_lx], evac_waits=[pg.tok("s_pe")])
        t_x = pg.tok("s_dve")
        for lt in range(LTC):
            psn, off = [("ps0", 0), ("ps0", 512), ("ps1", 0), ("ps1", 512)][lt % 4]
            mm = None
            for dc in range(DCH):
                st = dc == 0
                w_ = [t_x]
                if st:
                    w_.append(ctx.bank_r[psn])
                mm = pg.op("tensor",
                           lambda E, dc=dc, lt=lt, psn=psn, off=off, st=st:
                           E.matmul(ctx.pt[psn][:, off:off + 512],
                                    wkvd_sb[:, dc * KVC + lt * P: dc * KVC + (lt + 1) * P],
                                    xstg_c2[:, dc * 512:(dc + 1) * 512],
                                    start=st, stop=(dc == DCH - 1)),
                           waits=w_)
            ctx.bank_w[psn] = mm
            t_ev = pg.op("vector",
                         lambda E, lt=lt, blk=blk, psn=psn, off=off: E.tensor_copy(
                             lat_fm[:, lt * S + blk * 512:lt * S + (blk + 1) * 512],
                             ctx.pt[psn][:, off:off + 512]),
                         waits=[mm, pg.tok("s_dve")])
            ctx.bank_r[psn] = t_ev
        mm = None
        for dc in range(DCH):
            st = dc == 0
            w_ = [t_x]
            if st:
                w_.append(ctx.bank_r["av"])
            mm = pg.op("tensor",
                       lambda E, dc=dc, st=st:
                       E.matmul(ctx.pt["av"][:, 0:512],
                                wkr_sb[:, dc * HD:(dc + 1) * HD],
                                xstg_c2[:, dc * 512:(dc + 1) * 512],
                                start=st, stop=(dc == DCH - 1)),
                       waits=w_)
        ctx.bank_w["av"] = mm
        t_evk = pg.op("vector",
                      lambda E: E.tensor_copy(krstg[:, 0:512], ctx.pt["av"][:, 0:512]),
                      waits=[mm, pg.tok("s_dve"), pg.tok("s_pe")])
        ctx.bank_r["av"] = t_evk
        for j in range(4):
            kt = blk * 4 + j
            t_t = ctx.transpose(krstg[:, j * P:(j + 1) * P], krtok[:, 0:P], "sp",
                                waits=[t_evk], evac_waits=[pg.tok("s_pe")])
            t_ro = rope1(krtok[:, 0:P], krrot[:, 0:P], kt, t_t)
            ctx.transpose(krrot[:, 0:P], kr_fm[:, kt * P:(kt + 1) * P], "sp",
                          waits=[t_ro], evac_waits=[pg.tok("s_pe")])
    t_k2m = max_norm_pass(kr_fm[:], S, 1, pg.tok("s_dve"), True)
    pg.barrier()

    avbase = lambda i: (i // 3) * 512 + (i % 3) * 129
    for h in range(NH):
        t_wq = pg.dma("gpsimd", wqu_h.rearrange("p (c n) -> p c n", n=HD),
                      Wqu[:, h * HD:(h + 1) * HD].rearrange("(c p) n -> p c n", p=P),
                      waits=[pg.tok("s_pe")], stream="wh0")
        t_uk = pg.dma("gpsimd", upk_h.rearrange("p (c n) -> p c n", n=HD),
                      Wupk[:, h * HD:(h + 1) * HD].rearrange("(c p) n -> p c n", p=P),
                      waits=[pg.tok("s_pe")], stream="wh1")
        t_uv = pg.dma("gpsimd", upv_h.rearrange("p (c n) -> p c n", n=HD),
                      Wupv[:, h * HD:(h + 1) * HD].rearrange("(c p) n -> p c n", p=P),
                      waits=[pg.tok("s_pe")], stream="wh2")
        QB2 = min(512, T)
        for qb in range(ceil_div(T, 512)):
            psn = ["ps0", "ps1"][qb % 2]
            mm = None
            for c in range(QCC):
                st = c == 0
                w_ = [t_wq]
                if st:
                    w_.append(ctx.bank_r[psn])
                mm = pg.op("tensor",
                           lambda E, c=c, qb=qb, psn=psn, st=st:
                           E.matmul(ctx.pt[psn][:, 0:QB2],
                                    wqu_h[:, c * HD:(c + 1) * HD],
                                    t1fm[:, c * T + qb * QB2: c * T + (qb + 1) * QB2],
                                    start=st, stop=(c == QCC - 1)),
                           waits=w_)
            ctx.bank_w[psn] = mm
            t_ev = pg.op("vector",
                         lambda E, qb=qb, psn=psn: E.tensor_copy(
                             qch[:, qb * QB2:(qb + 1) * QB2], ctx.pt[psn][:, 0:QB2]),
                         waits=[mm, pg.tok("s_dve")])
            ctx.bank_r[psn] = t_ev
        t_qn = max_norm_pass(qch[:], T, 2, pg.tok("s_dve"), True)
        for sb_ in range(S // 512):
            psn = ["ps0", "ps1"][sb_ % 2]
            mm = None
            for lt in range(LTC):
                st = lt == 0
                w_ = [t_uk]
                if st:
                    w_.append(ctx.bank_r[psn])
                mm = pg.op("tensor",
                           lambda E, lt=lt, sb_=sb_, psn=psn, st=st:
                           E.matmul(ctx.pt[psn][:, 0:512],
                                    upk_h[:, lt * HD:(lt + 1) * HD],
                                    lat_fm[:, lt * S + sb_ * 512: lt * S + (sb_ + 1) * 512],
                                    start=st, stop=(lt == LTC - 1)),
                           waits=w_)
            ctx.bank_w[psn] = mm
            t_ev = pg.op("vector",
                         lambda E, sb_=sb_, psn=psn: E.tensor_copy(
                             KH0[:, sb_ * 512:(sb_ + 1) * 512], ctx.pt[psn][:, 0:512]),
                         waits=[mm, pg.tok("s_dve")])
            ctx.bank_r[psn] = t_ev
        t_kn = max_norm_pass(KH0, S, 3, pg.tok("s_dve"), True)
        t_m = pg.op("vector", lambda E: E.tensor_add(sca[0:1, 4:5], sca[0:1, 2:3],
                                                     sca[0:1, 0:1]),
                    waits=[t_qn, t_q2m])
        t_m = pg.op("vector", lambda E: E.tensor_add(sca[0:1, 5:6], sca[0:1, 3:4],
                                                     sca[0:1, 1:2]),
                    waits=[t_kn, t_k2m, t_m])
        t_m = pg.op("scalar", lambda E: E.sqrt(sca[0:1, 6:7], sca[0:1, 4:5]),
                    waits=[t_m, pg.tok("s_act")])
        t_m = pg.op("scalar", lambda E: E.sqrt(sca[0:1, 7:8], sca[0:1, 5:6]),
                    waits=[t_m])
        t_m = pg.op("vector", lambda E: E.tensor_mul(sca[0:1, 8:9], sca[0:1, 6:7],
                                                     sca[0:1, 7:8]),
                    waits=[t_m, pg.tok("s_dve")])
        t_m = pg.op("vector", lambda E: E.tensor_scalar_mul(sca[0:1, 9:10],
                                                            sca[0:1, 8:9], -SC),
                    waits=[t_m])
        t_mb = pg.op("tensor", lambda E: E.matmul(ctx.pt["sp"][:, 0:1],
                                                  ctx.ones_row_f[:], sca[0:1, 9:10],
                                                  start=True, stop=True),
                     waits=[t_m, ctx.bank_r["sp"]])
        ctx.bank_w["sp"] = t_mb
        t_mb2 = pg.op("vector", lambda E: E.tensor_copy(mb_h[:, 0:1],
                                                        ctx.pt["sp"][:, 0:1]),
                      waits=[t_mb])
        ctx.bank_r["sp"] = t_mb2
        t_vi = pg.op("gpsimd", lambda E: E.memset(VH0, 1.0),
                     waits=[pg.tok("s_pe"), pg.tok("q_gpsimd_ys")])
        for kt in range(KT):
            mm = None
            for lt in range(LTC):
                st = lt == 0
                w_ = [t_uv, t_vi]
                if st:
                    w_.append(ctx.bank_r["av"])
                mm = pg.op("tensor",
                           lambda E, lt=lt, kt=kt, st=st:
                           E.matmul(ctx.pt["av"][:, 0:HD],
                                    lat_fm[:, lt * S + kt * P: lt * S + (kt + 1) * P],
                                    upv_h[:, lt * HD:(lt + 1) * HD],
                                    start=st, stop=(lt == LTC - 1)),
                           waits=w_)
            ctx.bank_w["av"] = mm
            t_ev = pg.op("vector",
                         lambda E, kt=kt: E.tensor_copy(
                             VH0[:, kt * 129:kt * 129 + P], ctx.pt["av"][:, 0:HD]),
                         waits=[mm, pg.tok("s_dve")])
            ctx.bank_r["av"] = t_ev
        for qgi in range(NQG):
            nav_banks = ceil_div(avbase(QGT - 1) + 129, 512)
            mmz = None
            for bkk in range(nav_banks):
                mmz = pg.op("tensor",
                            lambda E, bkk=bkk: E.matmul(
                                ctx.pt["av"][:, bkk * 512:(bkk + 1) * 512],
                                ctx.ident[:], ctx.zer_b[:, 0:512],
                                start=True, stop=True),
                            waits=[ctx.bank_r["av"] if bkk == 0 else None])
            ctx.bank_w["av"] = mmz
            psn = ["ps0", "ps1"]
            for kt in range(KT):
                ps = psn[kt % 2]
                mm = None
                for nb in range(ceil_div(QG, 512)):
                    NBW = min(512, QG)
                    w_ = [pg.tok("s_dve")]
                    if nb == 0:
                        w_.append(ctx.bank_r[ps])
                    mm = pg.op("tensor",
                               lambda E, ps=ps, kt=kt, nb=nb, NBW=NBW, qgi=qgi:
                               E.matmul(ctx.pt[ps][:, nb * NBW:(nb + 1) * NBW],
                                        KH0[:, kt * P:(kt + 1) * P],
                                        qch[:, qgi * QG + nb * NBW:
                                            qgi * QG + (nb + 1) * NBW],
                                        start=True, stop=False),
                               waits=w_)
                    mm = pg.op("tensor",
                               lambda E, ps=ps, kt=kt, nb=nb, NBW=NBW, qgi=qgi:
                               E.matmul(ctx.pt[ps][:, nb * NBW:(nb + 1) * NBW],
                                        kr_fm[:, kt * P:(kt + 1) * P],
                                        qr_fm[:, qgi * QG + nb * NBW:
                                              qgi * QG + (nb + 1) * NBW],
                                        start=False, stop=True),
                               waits=[])
                ctx.bank_w[ps] = mm
                t_e = pg.op("scalar",
                            lambda E, ps=ps: E.activation(
                                PT[:, 0:QG], ctx.pt[ps][:, 0:QG], AF.Exp,
                                bias=mb_h[:, 0:1], scale=SC),
                            waits=[mm, t_mb2, pg.tok("s_pe"), pg.tok("s_act")])
                ctx.bank_r[ps] = t_e
                for i in range(QGT):
                    mm2 = pg.op("tensor",
                                lambda E, i=i, kt=kt: E.matmul(
                                    ctx.pt["av"][:, avbase(i):avbase(i) + 129],
                                    PT[:, i * P:(i + 1) * P],
                                    VH0[:, kt * 129:(kt + 1) * 129],
                                    start=False, stop=(kt == KT - 1),
                                    skip_group_check=True),
                                waits=[t_e])
                ctx.bank_w["av"] = mm2
            last_r = None
            for i in range(QGT):
                t_rc2 = pg.op("vector",
                              lambda E, i=i: E.reciprocal(
                                  rcp[:, 0:1],
                                  ctx.pt["av"][:, avbase(i) + P:avbase(i) + 129]),
                              waits=[ctx.bank_w["av"], pg.tok("s_dve")])
                t_y = pg.op("vector",
                            lambda E, i=i: E.tensor_scalar_mul(
                                ystg[:, 0:P],
                                ctx.pt["av"][:, avbase(i):avbase(i) + P],
                                rcp[:, 0:1]),
                            waits=[t_rc2, pg.tok("s_pe")])
                last_r = ctx.transpose(
                    ystg[:, 0:P], ystag[:, i * P:(i + 1) * P],
                    "sp", waits=[t_y], evac_waits=[pg.tok("q_gpsimd_ys")])
            ctx.bank_r["av"] = last_r
            pg.dma("gpsimd",
                   yscr[h, qgi * QGT:(qgi + 1) * QGT].rearrange("t p n -> p t n"),
                   ystag[:, 0:QGT * P].rearrange("p (t n) -> p t n", n=P),
                   waits=[last_r], stream="ys")
    pg.barrier()

    t_wo = pg.dma("gpsimd", wo_sb.rearrange("p (c n) -> p c n", n=D),
                  Wout.ap().rearrange("(c p) n -> p c n", p=P), stream="wo")
    out_toks = []
    for tt in range(TT):
        t_yl = pg.dma("sync", Y_t.rearrange("p (h n) -> p h n", n=P),
                      yscr[:, tt].rearrange("h p n -> p h n"),
                      waits=[pg.tok("s_pe")], stream="yl")
        mm = None
        OBW = min(512, D)
        for dc in range(DCH):
            for nb in range(NOB):
                psn, off = ("ps0", nb * OBW) if nb * OBW < 1024 else ("ps1", nb * OBW - 1024)
                st = dc == 0
                w_ = [t_yl, t_wo]
                if st:
                    w_.append(ctx.bank_r[psn])
                mm = pg.op("tensor",
                           lambda E, dc=dc, nb=nb, psn=psn, off=off, st=st, OBW=OBW:
                           E.matmul(ctx.pt[psn][:, off:off + OBW],
                                    Y_t[:, dc * P:(dc + 1) * P],
                                    wo_sb[:, dc * D + nb * OBW: dc * D + (nb + 1) * OBW],
                                    start=st, stop=(dc == DCH - 1)),
                           waits=w_)
        ctx.bank_w["ps0"] = mm
        if D > 1024:
            ctx.bank_w["ps1"] = mm
        t_rx = pg.dma("sync", xres, xloc[tt * P:(tt + 1) * P, :],
                      waits=[pg.tok("s_dve")], stream="r0")
        t_s = None
        HW = min(D, 1024)
        for halfo in range(ceil_div(D, 1024)):
            psn = ("ps0", "ps1")[halfo]
            t_s = pg.op("vector",
                        lambda E, psn=psn, halfo=halfo: E.tensor_add(
                            sumb[:, halfo * HW:(halfo + 1) * HW],
                            ctx.pt[psn][:, 0:HW],
                            xres[:, halfo * HW:(halfo + 1) * HW]),
                        waits=[mm, t_rx, pg.tok("s_dve")])
            ctx.bank_r[psn] = t_s
        t_o = emit_layernorm(ctx, sumb, D, stat, obuf[:, 0:D],
                             waits=[t_s, out_toks[-1] if out_toks else None],
                             sq_scratch=lnsq)
        t_out = pg.dma("sync", out[tt * P:(tt + 1) * P, :], obuf[:, 0:D],
                       waits=[t_o], stream=f"o{tt % 2}")
        out_toks.append(t_out)

    if standalone:
        return ctx.finish(pg.all_dma_toks())
    return None





def build_l1_ffn(D=2048, NE=32, FE=768, TOPK=6, FSH=1536, T=2048, ctx=None, ext=None):
    DCH = D // P
    TT = T // P
    TOKH = min(1024, T)
    NH_ = T // TOKH
    HT = TOKH // P
    FEC = FE // P
    FSC = FSH // P
    NB = ceil_div(D, 512)
    DBW = min(512, D)
    TB = min(512, TOKH)
    NTB = TOKH // TB

    standalone = ctx is None
    if standalone:
        ctx = Ctx()
    pg = ctx.pg
    EX = (lambda n, sh: ctx.inp(n, sh)) if ext is None else (lambda n, sh: ext[n])

    xs = EX("xslab", (T, D))
    gw = EX("gate_w", (D, NE))
    wg = EX("wg", (NE, D, FE))
    wu = EX("wu", (NE, D, FE))
    wd = EX("wd", (NE, FE, D))
    sg = EX("sg", (D, FSH))
    su = EX("su", (D, FSH))
    sd = EX("sd", (FSH, D))
    out = ctx.outp("out", (T, D)) if ext is None else ext["out"]

    if standalone:
        ctx.setup_consts()

    acc = ctx.sb("acc", [P, HT * D], F32)
    x_fm = ctx.sb("x_fm", [P, DCH * TOKH], BF16)
    ARW = ctx.arena("ARW", max(2 * (DCH * FE) * 3,
                               2 * (FSC * TOKH) + 4 * D * 2 + 2 * DCH * P * 2,
                               16 * D) + 256)
    wgbuf = ctx.view(ARW, 0, DCH * FE, BF16)
    wubuf = ctx.view(ARW, 2 * DCH * FE, DCH * FE, BF16)
    wdbuf = ctx.view(ARW, 4 * DCH * FE, FEC * D, BF16)
    actsh = ctx.view(ARW, 0, FSC * TOKH, BF16)
    shw = [ctx.view(ARW, 2 * FSC * TOKH + i * 2 * DCH * P, DCH * P, BF16)
           for i in range(2)]
    sdrow = [ctx.view(ARW, 2 * FSC * TOKH + 4 * DCH * P + i * 2 * D, D, BF16)
             for i in range(2)]
    xres = ctx.view(ARW, 0, D, F32)
    sumb = ctx.view(ARW, 4 * D, D, F32)
    lnsq = ctx.view(ARW, 8 * D, D, F32)
    obuf = ctx.view(ARW, 12 * D, D, F32)
    act_e = ctx.sb("act_e", [P, FEC * TOKH], BF16)
    gwb = ctx.sb("gwb", [P, DCH * NE], BF16)
    comb = ctx.sb("comb", [P, HT * NE], F32)
    scores = ctx.sb("scores", [P, NE], F32)
    top8 = ctx.sb("top8", [P, 8], F32)
    msk = ctx.sb("msk", [P, NE], F32)
    dnm = ctx.sb("dnm", [P, 2], F32)
    xrow = [ctx.sb(f"xrow2_{i}", [P, D], BF16) for i in range(2)]
    silu_g = ctx.sb("silu_g", [P, TB], BF16)
    stat = ctx.sb("stat2", [P, 8], F32)

    pg.dma("gpsimd", gwb[:].rearrange("p (c n) -> p c n", n=NE),
           gw.ap().rearrange("(c p) n -> p c n", p=P), stream="w0")
    pg.barrier()

    out_toks = []
    for hf in range(NH_):
        for tt in range(HT):
            g_tt = hf * HT + tt
            xr = xrow[tt % 2]
            t_lx = pg.dma("gpsimd", xr[:, 0:D], xs[g_tt * P:(g_tt + 1) * P, :],
                          waits=[pg.tok("s_pe")], stream=f"x{tt % 2}")
            for dc in range(DCH):
                ctx.transpose(xr[:, dc * P:(dc + 1) * P],
                              x_fm[:, dc * TOKH + tt * P: dc * TOKH + (tt + 1) * P],
                              "pb7", waits=[t_lx], evac_waits=[pg.tok("s_pe")])
        t_xfm = pg.tok("s_dve")
        t_ai = pg.op("gpsimd", lambda E: E.memset(acc[:], 0.0),
                     waits=[pg.tok("s_dve"), pg.tok("q_sync_o0"), pg.tok("q_sync_o1")])
        for tt in range(HT):
            mm = None
            for dc in range(DCH):
                st = dc == 0
                w_ = [t_xfm]
                if st:
                    w_.append(ctx.bank_r["pb6"])
                mm = pg.op("tensor",
                           lambda E, dc=dc, tt=tt, st=st:
                           E.matmul(ctx.pt["pb6"][:, 0:NE],
                                    x_fm[:, dc * TOKH + tt * P: dc * TOKH + (tt + 1) * P],
                                    gwb[:, dc * NE:(dc + 1) * NE],
                                    start=st, stop=(dc == DCH - 1)),
                           waits=w_)
            ctx.bank_w["pb6"] = mm
            t_sg = pg.op("scalar",
                         lambda E: E.activation(scores[:, 0:NE], ctx.pt["pb6"][:, 0:NE],
                                                AF.Sigmoid),
                         waits=[mm, pg.tok("s_dve"), pg.tok("s_act")])
            ctx.bank_r["pb6"] = t_sg
            t_t8 = pg.op("vector", lambda E: E.max(out=top8[:, 0:8], in_=scores[:, 0:NE]),
                         waits=[t_sg])
            t_mk = pg.op("vector",
                         lambda E: E.tensor_single_scalar(msk[:, 0:NE], scores[:, 0:NE],
                                                          top8[:, TOPK - 1:TOPK], ALU.is_ge),
                         waits=[t_t8])
            t_ws = pg.op("vector", lambda E: E.tensor_mul(msk[:, 0:NE], msk[:, 0:NE],
                                                          scores[:, 0:NE]),
                         waits=[t_mk])
            t_dn = pg.op("vector", lambda E: E.reduce_sum(dnm[:, 0:1], msk[:, 0:NE],
                                                          axis=AX.X),
                         waits=[t_ws])
            t_dn = pg.op("vector", lambda E: E.tensor_scalar_add(dnm[:, 0:1], dnm[:, 0:1],
                                                                 1e-20),
                         waits=[t_dn])
            t_dn = pg.op("vector", lambda E: E.reciprocal(dnm[:, 1:2], dnm[:, 0:1]),
                         waits=[t_dn])
            t_cb = pg.op("vector",
                         lambda E, tt=tt: E.tensor_scalar_mul(
                             comb[:, tt * NE:(tt + 1) * NE], msk[:, 0:NE], dnm[:, 1:2]),
                         waits=[t_dn])
        t_comb = pg.tok("s_dve")
        for ft in range(FSC):
            t_lg = pg.dma("gpsimd", shw[0].rearrange("p (c n) -> p c n", n=P),
                          sg[:, ft * P:(ft + 1) * P].rearrange("(c p) n -> p c n", p=P),
                          waits=[pg.tok("s_pe")], stream="sh0")
            t_lu = pg.dma("gpsimd", shw[1].rearrange("p (c n) -> p c n", n=P),
                          su[:, ft * P:(ft + 1) * P].rearrange("(c p) n -> p c n", p=P),
                          waits=[pg.tok("s_pe")], stream="sh1")
            for tb in range(NTB):
                mmg = mmu = None
                for dc in range(DCH):
                    st = dc == 0
                    wg_ = [t_lg, t_xfm]
                    wu_ = [t_lu]
                    if st:
                        wg_.append(ctx.bank_r["pb0"])
                        wu_.append(ctx.bank_r["pb1"])
                    mmg = pg.op("tensor",
                                lambda E, dc=dc, tb=tb, st=st:
                                E.matmul(ctx.pt["pb0"][:, 0:TB],
                                         shw[0][:, dc * P:(dc + 1) * P],
                                         x_fm[:, dc * TOKH + tb * TB: dc * TOKH + (tb + 1) * TB],
                                         start=st, stop=(dc == DCH - 1)),
                                waits=wg_)
                    mmu = pg.op("tensor",
                                lambda E, dc=dc, tb=tb, st=st:
                                E.matmul(ctx.pt["pb1"][:, 0:TB],
                                         shw[1][:, dc * P:(dc + 1) * P],
                                         x_fm[:, dc * TOKH + tb * TB: dc * TOKH + (tb + 1) * TB],
                                         start=st, stop=(dc == DCH - 1)),
                                waits=wu_)
                ctx.bank_w["pb0"] = mmg
                ctx.bank_w["pb1"] = mmu
                t_si = pg.op("scalar",
                             lambda E: E.activation(silu_g[:, 0:TB], ctx.pt["pb0"][:, 0:TB],
                                                    AF.Silu),
                             waits=[mmg, pg.tok("s_dve"), pg.tok("s_act")])
                ctx.bank_r["pb0"] = t_si
                t_mu = pg.op("vector",
                             lambda E, ft=ft, tb=tb: E.tensor_tensor(
                                 actsh[:, ft * TOKH + tb * TB: ft * TOKH + (tb + 1) * TB],
                                 silu_g[:, 0:TB], ctx.pt["pb1"][:, 0:TB], ALU.mult),
                             waits=[t_si, mmu])
                ctx.bank_r["pb1"] = t_mu
        t_ash = pg.tok("s_dve")
        BLKT = min(256, TOKH)
        NSUB = BLKT // P
        for blk in range(TOKH // BLKT):
            mm_l = [[None] * NB for _ in range(NSUB)]
            for ft in range(FSC):
                sb_ = sdrow[ft % 2]
                t_ld = pg.dma("gpsimd", sb_, sd[ft * P:(ft + 1) * P, :],
                              waits=[pg.tok("s_pe")], stream=f"sd{ft % 2}")
                for sub in range(NSUB):
                    lt = blk * BLKT + sub * P
                    for nb in range(NB):
                        bank = f"pb{sub * NB + nb}"
                        st = ft == 0
                        w_ = [t_ld, t_ash]
                        if st:
                            w_.append(ctx.bank_r[bank])
                        mm_l[sub][nb] = pg.op(
                            "tensor",
                            lambda E, bank=bank, sb_=sb_, nb=nb, lt=lt, ft=ft, st=st:
                            E.matmul(ctx.pt[bank][:, 0:DBW],
                                     actsh[:, ft * TOKH + lt: ft * TOKH + lt + P],
                                     sb_[:, nb * DBW:(nb + 1) * DBW],
                                     start=st, stop=(ft == FSC - 1)),
                            waits=w_)
            for sub in range(NSUB):
                tile = blk * NSUB + sub
                t_a = None
                for nb in range(NB):
                    bank = f"pb{sub * NB + nb}"
                    t_a = pg.op("vector",
                                lambda E, bank=bank, tile=tile, nb=nb: E.tensor_add(
                                    acc[:, tile * D + nb * DBW: tile * D + (nb + 1) * DBW],
                                    ctx.pt[bank][:, 0:DBW],
                                    acc[:, tile * D + nb * DBW: tile * D + (nb + 1) * DBW]),
                                waits=[mm_l[sub][nb], t_ai, pg.tok("s_dve")])
                    ctx.bank_r[bank] = t_a
        pg.barrier()

        t_wg = pg.dma("gpsimd", wgbuf.rearrange("p (c n) -> p c n", n=FE),
                      wg[0].rearrange("(c p) n -> p c n", p=P), stream="we0")
        t_wu = pg.dma("gpsimd", wubuf.rearrange("p (c n) -> p c n", n=FE),
                      wu[0].rearrange("(c p) n -> p c n", p=P), stream="we1")
        t_wd = pg.dma("gpsimd", wdbuf.rearrange("p (c n) -> p c n", n=D),
                      wd[0].rearrange("(c p) n -> p c n", p=P), stream="we2")
        for e in range(NE):
            for ft in range(FEC):
                for tb in range(NTB):
                    mmg = mmu = None
                    for dc in range(DCH):
                        st = dc == 0
                        wg_ = [t_wg]
                        wu_ = [t_wu]
                        if st:
                            wg_.append(ctx.bank_r["pb0"])
                            wu_.append(ctx.bank_r["pb1"])
                        mmg = pg.op("tensor",
                                    lambda E, dc=dc, ft=ft, tb=tb, st=st:
                                    E.matmul(ctx.pt["pb0"][:, 0:TB],
                                             wgbuf[:, dc * FE + ft * P: dc * FE + (ft + 1) * P],
                                             x_fm[:, dc * TOKH + tb * TB: dc * TOKH + (tb + 1) * TB],
                                             start=st, stop=(dc == DCH - 1)),
                                    waits=wg_)
                        mmu = pg.op("tensor",
                                    lambda E, dc=dc, ft=ft, tb=tb, st=st:
                                    E.matmul(ctx.pt["pb1"][:, 0:TB],
                                             wubuf[:, dc * FE + ft * P: dc * FE + (ft + 1) * P],
                                             x_fm[:, dc * TOKH + tb * TB: dc * TOKH + (tb + 1) * TB],
                                             start=st, stop=(dc == DCH - 1)),
                                    waits=wu_)
                    ctx.bank_w["pb0"] = mmg
                    ctx.bank_w["pb1"] = mmu
                    t_si = pg.op("scalar",
                                 lambda E: E.activation(silu_g[:, 0:TB],
                                                        ctx.pt["pb0"][:, 0:TB], AF.Silu),
                                 waits=[mmg, pg.tok("s_dve"), pg.tok("s_act")])
                    ctx.bank_r["pb0"] = t_si
                    t_mu = pg.op("vector",
                                 lambda E, ft=ft, tb=tb: E.tensor_tensor(
                                     act_e[:, ft * TOKH + tb * TB: ft * TOKH + (tb + 1) * TB],
                                     silu_g[:, 0:TB], ctx.pt["pb1"][:, 0:TB], ALU.mult),
                                 waits=[t_si, mmu])
                    ctx.bank_r["pb1"] = t_mu
            t_act = pg.tok("s_dve")
            if e + 1 < NE:
                t_wg = pg.dma("gpsimd", wgbuf.rearrange("p (c n) -> p c n", n=FE),
                              wg[e + 1].rearrange("(c p) n -> p c n", p=P),
                              waits=[pg.tok("s_pe")], stream="we0")
                t_wu = pg.dma("gpsimd", wubuf.rearrange("p (c n) -> p c n", n=FE),
                              wu[e + 1].rearrange("(c p) n -> p c n", p=P),
                              waits=[pg.tok("s_pe")], stream="we1")
            for blk in range(TOKH // BLKT):
                mm_l = [[None] * NB for _ in range(NSUB)]
                for ft in range(FEC):
                    for sub in range(NSUB):
                        lt = blk * BLKT + sub * P
                        for nb in range(NB):
                            bank = f"pb{sub * NB + nb}"
                            st = ft == 0
                            w_ = [t_wd, t_act]
                            if st:
                                w_.append(ctx.bank_r[bank])
                            mm_l[sub][nb] = pg.op(
                                "tensor",
                                lambda E, bank=bank, ft=ft, nb=nb, lt=lt, st=st:
                                E.matmul(ctx.pt[bank][:, 0:DBW],
                                         act_e[:, ft * TOKH + lt: ft * TOKH + lt + P],
                                         wdbuf[:, ft * D + nb * DBW: ft * D + (nb + 1) * DBW],
                                         start=st, stop=(ft == FEC - 1)),
                                waits=w_)
                for sub in range(NSUB):
                    tile = blk * NSUB + sub
                    t_a = None
                    for nb in range(NB):
                        bank = f"pb{sub * NB + nb}"
                        t_a = pg.op("vector",
                                    lambda E, bank=bank, tile=tile, nb=nb, e=e:
                                    E.scalar_tensor_tensor(
                                        acc[:, tile * D + nb * DBW: tile * D + (nb + 1) * DBW],
                                        ctx.pt[bank][:, 0:DBW],
                                        comb[:, tile * NE + e: tile * NE + e + 1],
                                        acc[:, tile * D + nb * DBW: tile * D + (nb + 1) * DBW],
                                        ALU.mult, ALU.add),
                                    waits=[mm_l[sub][nb], t_comb, pg.tok("s_dve")])
                        ctx.bank_r[bank] = t_a
            if e + 1 < NE:
                t_wd = pg.dma("gpsimd", wdbuf.rearrange("p (c n) -> p c n", n=D),
                              wd[e + 1].rearrange("(c p) n -> p c n", p=P),
                              waits=[pg.tok("s_pe")], stream="we2")
        pg.barrier()

        for tt in range(HT):
            g_tt = hf * HT + tt
            t_rx = pg.dma("sync", xres, xs[g_tt * P:(g_tt + 1) * P, :],
                          waits=[pg.tok("s_dve")], stream="r0")
            t_s = pg.op("vector",
                        lambda E, tt=tt: E.tensor_add(sumb[:, 0:D],
                                                      acc[:, tt * D:(tt + 1) * D],
                                                      xres[:, 0:D]),
                        waits=[t_rx, pg.tok("s_dve")])
            t_o = emit_layernorm(ctx, sumb, D, stat, obuf[:, 0:D],
                                 waits=[t_s, out_toks[-1] if out_toks else None],
                                 sq_scratch=lnsq)
            t_out = pg.dma("sync", out[g_tt * P:(g_tt + 1) * P, :], obuf[:, 0:D],
                           waits=[t_o], stream=f"o{tt % 2}")
            out_toks.append(t_out)
        pg.barrier()

    if standalone:
        return ctx.finish(pg.all_dma_toks())
    return None





def build_fused(B=2, S=8192, D=2048, T=2048, NH0=16, NKV=8, R=32, HID=5120,
                NH1=16, KVC=512, QC=1024, NE=32, FE=768, TOPK=6, FSH=1536):
    ctx = Ctx(psum_spec="both")
    pg = ctx.pg
    KVD = NKV * 128
    QG0 = min(1024, T)

    ins = {}
    def inp(name, shape, dtype=F32):
        ins[name] = ctx.inp(name, shape, dtype)
        return ins[name]

    inp("xb", (S, D))
    for nm, sh in (("W_q", (D, D)), ("W_k", (KVD, D)), ("W_v", (KVD, D)),
                   ("W_o", (D, D)), ("A_q", (R, D)), ("B_q", (D, R)),
                   ("A_k", (R, D)), ("B_k", (KVD, R)), ("A_v", (R, D)),
                   ("B_v", (KVD, R)), ("A_o", (R, D)), ("B_o", (D, R)),
                   ("s_q", (1,)), ("s_k", (1,)), ("s_v", (1,)), ("s_o", (1,)),
                   ("q_gain", (NH0,)), ("cosk", (S, 32)), ("sink", (S, 32)),
                   ("kbias", (P, S // P)), ("bandm", (P, 2 * QG0 - P)),
                   ("w_fc", (HID, D)), ("w_proj", (D, HID)),
                   ("w_kv_down", (D, KVC)), ("w_kv_up_k", (KVC, D)),
                   ("w_kv_up_v", (KVC, D)), ("w_q_down", (D, QC)),
                   ("w_q_up", (QC, D)), ("w_k_rope", (D, 128)),
                   ("w_q_rope", (D, 128)), ("w_out", (D, D)),
                   ("cos1", (S, 64)), ("sin1", (S, 64)),
                   ("cos1q", (T, 64)), ("sin1q", (T, 64)),
                   ("gate_w", (D, NE)), ("wg", (NE, D, FE)), ("wu", (NE, D, FE)),
                   ("wd", (NE, FE, D)), ("sg", (D, FSH)), ("su", (D, FSH)),
                   ("sd", (FSH, D))):
        inp(nm, sh)
    out = ctx.outp("out", (T, D))

    x1 = ctx.scratch("fx1", [T, D], F32)
    x2 = ctx.scratch("fx2", [T, D], F32)
    x2bf = ctx.scratch("fx2bf", [T, D], BF16)
    x2g = ctx.scratch("fx2g", [S, D], BF16)
    x3 = ctx.scratch("fx3", [T, D], F32)

    ctx.setup_consts()
    ctx.mark_persist()

    def phase_gap():
        pg.barrier()
        ctx.reset_pool()
        ctx.reset_banks(None)

    extA = {k: ins[k] for k in ("xb", "W_q", "W_k", "W_v", "W_o", "A_q", "B_q",
                                "A_k", "B_k", "A_v", "B_v", "A_o", "B_o",
                                "s_q", "s_k", "s_v", "s_o", "q_gain",
                                "cosk", "sink", "kbias", "bandm")}
    extA["out"] = x1
    build_l0_attn(D=D, NH=NH0, NKV=NKV, S=S, T=T, R=R, ctx=ctx, ext=extA)
    phase_gap()

    extB = {"xslab": x1, "w_fc": ins["w_fc"], "w_proj": ins["w_proj"], "out": x2}
    build_l0_ffn(D=D, HID=HID, T=T, ctx=ctx, ext=extB, x2bf=x2bf)
    phase_gap()

    pg.sem("cc")
    rg = [[0, 1, 2, 3], [4, 5, 6, 7]]
    t_cc = pg.op("gpsimd",
                 lambda E: E.collective_compute(
                     "AllGather", mybir.AluOpType.bypass,
                     replica_groups=rg,
                     ins=[x2bf.ap().opt()], outs=[x2g.ap().opt()]),
                 waits=pg.all_dma_toks(), inc="cc", inc_amt=1)
    for e in Prog.ENGINES:
        pg.op(e, lambda E: E.nop(), waits=[t_cc], inc=None)
    ctx.reset_banks(None)

    extC = {"xloc": x2, "xg": x2g, "out": x3,
            "cos1": ins["cos1"], "sin1": ins["sin1"],
            "cos1q": ins["cos1q"], "sin1q": ins["sin1q"]}
    for k in ("w_kv_down", "w_kv_up_k", "w_kv_up_v", "w_q_down", "w_q_up",
              "w_k_rope", "w_q_rope", "w_out"):
        extC[k] = ins[k]
    build_l1_attn(D=D, NH=NH1, KVC=KVC, QC=QC, S=S, T=T, ctx=ctx, ext=extC,
                  split=True)
    phase_gap()

    extD = {"xslab": x3, "out": out}
    for k in ("gate_w", "wg", "wu", "wd", "sg", "su", "sd"):
        extD[k] = ins[k]
    build_l1_ffn(D=D, NE=NE, FE=FE, TOPK=TOPK, FSH=FSH, T=T, ctx=ctx, ext=extD)

    return ctx.finish(pg.all_dma_toks())



def _ident_np():
    return np.eye(P, dtype=BF16NP)


def make_rope_l1(S, HD=128):
    inv = (1.0 / np.float32(10000.0) ** (np.arange(0, HD, 2, dtype=np.float32) / np.float32(HD))).astype(np.float32)
    ang = (np.arange(S, dtype=np.float32)[:, None] * inv[None, :]).astype(np.float32)
    return np.cos(ang).astype(np.float32), np.sin(ang).astype(np.float32)


def make_rope_l0(S, ROPE=64):
    inv_freq = (1.0 / np.float32(10000.0) ** (np.arange(0, ROPE, 2, dtype=np.float32) / np.float32(ROPE))).astype(np.float32)
    freqs = np.outer(np.arange(S, dtype=np.float32), inv_freq).astype(np.float32)
    return np.cos(freqs).astype(np.float32), np.sin(freqs).astype(np.float32)


def make_kbias(t0, S, T):
    KT = S // P
    kb = np.zeros((P, KT), np.float32)
    for kt in range(KT):
        if kt < T // P:
            continue
        keys = kt * P + np.arange(P)
        kb[:, kt] = np.where(keys >= S - t0, 0.0, -1e9)
    return kb


def make_bandm(QG):
    c = np.arange(2 * QG - P)[None, :]
    p = np.arange(P)[:, None]
    return np.where(c >= p + (QG - P), 0.0, -1e9).astype(np.float32)


_BUILD_CACHE = {}


def _get(name, fn):
    if name not in _BUILD_CACHE:
        _BUILD_CACHE[name] = fn()
    return _BUILD_CACHE[name]


def kernel(**inputs):
    B, S, D = 2, 8192, 2048
    T = S // 4
    NC = 8
    ident = _ident_np()
    identf = np.eye(P, dtype=np.float32)
    f32 = np.float32

    def rolled(xb, t0):
        return np.ascontiguousarray(np.roll(xb, -t0, axis=0), dtype=f32)

    cos0, sin0 = make_rope_l0(S)
    cos1np, sin1np = make_rope_l1(S)
    bandm = make_bandm(1024)
    ncF = _get("F", build_fused)
    x = np.asarray(inputs["L0_attn_x"], f32)
    wmap = {}
    for nm in ("W_q", "W_k", "W_v", "W_o", "A_q", "B_q", "A_k", "B_k",
               "A_v", "B_v", "A_o", "B_o", "s_q", "s_k", "s_v", "s_o", "q_gain"):
        wmap[nm] = np.asarray(inputs["L0_attn_" + nm], f32)
    wmap["w_fc"] = np.asarray(inputs["L0_ffn_w_fc"], f32)
    wmap["w_proj"] = np.asarray(inputs["L0_ffn_w_proj"], f32)
    for nm in ("w_kv_down", "w_kv_up_k", "w_kv_up_v", "w_q_down", "w_q_up",
               "w_k_rope", "w_q_rope", "w_out"):
        wmap[nm] = np.asarray(inputs["L1_attn_" + nm], f32)
    for nm, src in (("gate_w", "L1_ffn_gate_w"), ("wg", "L1_ffn_wg"),
                    ("wu", "L1_ffn_wu"), ("wd", "L1_ffn_wd"),
                    ("sg", "L1_ffn_sg"), ("su", "L1_ffn_su"),
                    ("sd", "L1_ffn_sd")):
        wmap[nm] = np.asarray(inputs[src], f32)
    in_maps = []
    for c in range(NC):
        b, t0 = c // 4, (c % 4) * T
        m = dict(wmap)
        m["xb"] = rolled(x[b], t0)
        m["cosk"] = rolled(cos0, t0)
        m["sink"] = rolled(sin0, t0)
        m["kbias"] = make_kbias(t0, S, T)
        m["bandm"] = bandm
        m["cos1"] = cos1np
        m["sin1"] = sin1np
        m["cos1q"] = np.ascontiguousarray(cos1np[t0:t0 + T])
        m["sin1q"] = np.ascontiguousarray(sin1np[t0:t0 + T])
        m["c_ident"] = ident
        m["c_identf"] = identf
        in_maps.append(m)
    res = run_bass_kernel_spmd(ncF, in_maps, core_ids=list(range(NC)))
    y = np.empty((B, S, D), f32)
    for c in range(NC):
        b, t0 = c // 4, (c % 4) * T
        y[b, t0:t0 + T] = res.results[c]["out"]
    return y


def kernel_multilaunch(**inputs):
    B, S, D = 2, 8192, 2048
    T = S // 4
    NC = 8
    ident = _ident_np()
    identf = np.eye(P, dtype=np.float32)
    f32 = np.float32

    def rolled(xb, t0):
        return np.ascontiguousarray(np.roll(xb, -t0, axis=0), dtype=f32)

    cos0, sin0 = make_rope_l0(S)
    cos1np, sin1np = make_rope_l1(S)
    bandm = make_bandm(1024)

    ncA = _get("A", build_l0_attn)
    x = np.asarray(inputs["L0_attn_x"], f32)
    in_maps = []
    for c in range(NC):
        b, t0 = c // 4, (c % 4) * T
        m = {"xb": rolled(x[b], t0),
             "cosk": rolled(cos0, t0), "sink": rolled(sin0, t0),
             "kbias": make_kbias(t0, S, T), "bandm": bandm, "c_ident": ident,
             "c_identf": identf}
        for nm in ("W_q", "W_k", "W_v", "W_o", "A_q", "B_q", "A_k", "B_k",
                   "A_v", "B_v", "A_o", "B_o", "s_q", "s_k", "s_v", "s_o",
                   "q_gain"):
            m[nm] = np.asarray(inputs["L0_attn_" + nm], f32)
        in_maps.append(m)
    res = run_bass_kernel_spmd(ncA, in_maps, core_ids=list(range(NC)))
    x1 = np.empty((B, S, D), f32)
    for c in range(NC):
        b, t0 = c // 4, (c % 4) * T
        x1[b, t0:t0 + T] = res.results[c]["out"]

    ncB = _get("B", build_l0_ffn)
    in_maps = []
    for c in range(NC):
        b, t0 = c // 4, (c % 4) * T
        in_maps.append({"xslab": np.ascontiguousarray(x1[b, t0:t0 + T]),
                        "w_fc": np.asarray(inputs["L0_ffn_w_fc"], f32),
                        "w_proj": np.asarray(inputs["L0_ffn_w_proj"], f32),
                        "c_ident": ident, "c_identf": identf})
    res = run_bass_kernel_spmd(ncB, in_maps, core_ids=list(range(NC)))
    x2 = np.empty((B, S, D), f32)
    for c in range(NC):
        b, t0 = c // 4, (c % 4) * T
        x2[b, t0:t0 + T] = res.results[c]["out"]

    ncC = _get("C", build_l1_attn)
    in_maps = []
    for c in range(NC):
        b, t0 = c // 4, (c % 4) * T
        m = {"xb": rolled(x2[b], t0),
             "cos1": rolled(cos1np, t0), "sin1": rolled(sin1np, t0),
             "c_ident": ident, "c_identf": identf}
        for nm in ("w_kv_down", "w_kv_up_k", "w_kv_up_v", "w_q_down", "w_q_up",
                   "w_k_rope", "w_q_rope", "w_out"):
            m[nm] = np.asarray(inputs["L1_attn_" + nm], f32)
        in_maps.append(m)
    res = run_bass_kernel_spmd(ncC, in_maps, core_ids=list(range(NC)))
    x3 = np.empty((B, S, D), f32)
    for c in range(NC):
        b, t0 = c // 4, (c % 4) * T
        x3[b, t0:t0 + T] = res.results[c]["out"]

    ncD = _get("D", build_l1_ffn)
    in_maps = []
    for c in range(NC):
        b, t0 = c // 4, (c % 4) * T
        in_maps.append({"xslab": np.ascontiguousarray(x3[b, t0:t0 + T]),
                        "gate_w": np.asarray(inputs["L1_ffn_gate_w"], f32),
                        "wg": np.asarray(inputs["L1_ffn_wg"], f32),
                        "wu": np.asarray(inputs["L1_ffn_wu"], f32),
                        "wd": np.asarray(inputs["L1_ffn_wd"], f32),
                        "sg": np.asarray(inputs["L1_ffn_sg"], f32),
                        "su": np.asarray(inputs["L1_ffn_su"], f32),
                        "sd": np.asarray(inputs["L1_ffn_sd"], f32),
                        "c_ident": ident, "c_identf": identf})
    res = run_bass_kernel_spmd(ncD, in_maps, core_ids=list(range(NC)))
    y = np.empty((B, S, D), f32)
    for c in range(NC):
        b, t0 = c // 4, (c % 4) * T
        y[b, t0:t0 + T] = res.results[c]["out"]
    return y


# version 11
# speedup vs baseline: 1.0407x; 1.0407x over previous
import numpy as np
import ml_dtypes
from contextlib import ExitStack

import concourse.bass as bass
import concourse.mybir as mybir
from concourse.bass_utils import run_bass_kernel_spmd

F32 = mybir.dt.float32
BF16 = mybir.dt.bfloat16
AF = mybir.ActivationFunctionType
ALU = mybir.AluOpType
AX = mybir.AxisListType
BF16NP = ml_dtypes.bfloat16

P = 128



class Prog:
    ENGINES = ("sync", "scalar", "tensor", "vector", "gpsimd")
    ESEM = {"sync": "s_sp", "scalar": "s_act", "tensor": "s_pe",
            "vector": "s_dve", "gpsimd": "s_pool"}

    def __init__(self, nc, stack):
        self.nc = nc
        self.stack = stack
        self.ops = {e: [] for e in self.ENGINES}
        self.sems = {}
        self.counts = {}
        for e in self.ENGINES:
            self.sem(self.ESEM[e])

    def sem(self, name):
        if name not in self.sems:
            self.sems[name] = self.stack.enter_context(self.nc.semaphore(name))
            self.counts[name] = 0
        return name

    def op(self, eng, fn, waits=(), inc="auto", inc_amt=1):
        if inc == "auto":
            inc = self.ESEM[eng]
        ws = tuple(w for w in waits if w is not None)
        for (s, v) in ws:
            if s.startswith("q_"):
                assert v == self.counts[s], (
                    f"stale DMA token on stream {s}: tok {v} < issued {self.counts[s]}")
        tok = None
        if inc is not None:
            self.counts[inc] += inc_amt
            tok = (inc, self.counts[inc])
        self.ops[eng].append((ws, fn, tok, inc_amt if inc else 0))
        return tok

    def dma(self, eng, out, in_, waits=(), stream="d"):
        s = self.sem(f"q_{eng}_{stream}")
        return self.op(eng, lambda E: E.dma_start(out=out, in_=in_), waits, s, 16)

    def all_dma_toks(self):
        return [self.tok(s) for s in self.counts if s.startswith("q_") and self.counts[s] > 0]

    def tok(self, sname):
        self.sem(sname)
        return (sname, self.counts[sname])

    def barrier(self):
        toks = []
        dts = self.all_dma_toks()
        for e in self.ENGINES:
            be = self.sem("b_" + e)
            if e == "sync":
                toks.append(self.op(e, lambda E: E.nop(), waits=dts, inc=be))
            else:
                toks.append(self.op(e, lambda E: E.drain(), waits=dts, inc=be))
        for e in self.ENGINES:
            self.op(e, lambda E: E.nop(), waits=toks, inc=None)
        return toks

    def emit(self):
        nc = self.nc
        with nc.Block() as block:
            for eng in self.ENGINES:
                ops = self.ops[eng]

                def body(E, ops=ops):
                    waited = {}
                    for ws, fn, tok, amt in ops:
                        for (s, v) in ws:
                            if v > 0 and waited.get(s, 0) < v:
                                E.wait_ge(self.sems[s], v)
                                waited[s] = v
                        ins = fn(E)
                        if tok is not None:
                            ins.then_inc(self.sems[tok[0]], amt)
                getattr(block, eng)(body)


def ceil_div(a, b):
    return -(-a // b)


class _ArenaView:

    def __init__(self, t, off, nbytes):
        self.t = t
        self.off = off
        self.nbytes = nbytes


class _IntAlias:

    def __init__(self, d):
        self.d = d

    def _k(self, k):
        return f"pb{k}" if isinstance(k, int) else k

    def __getitem__(self, k):
        return self.d[self._k(k)]

    def __setitem__(self, k, v):
        self.d[self._k(k)] = v


class Ctx:

    def __init__(self, name="k", psum_spec=None):
        self.nc = bass.Bass("TRN2", target_bir_lowering=False)
        self.stack = ExitStack()
        self.pg = Prog(self.nc, self.stack)
        if psum_spec == "both":
            base = self.stack.enter_context(self.nc.psum_tensor("PS", [P, 4096], F32))
            self.pt = {
                "ps0": base[:, 0:1024], "ps1": base[:, 1024:2048],
                "av": base[:, 2048:3584], "sp": base[:, 3584:4096],
            }
            for i in range(8):
                self.pt[f"pb{i}"] = base[:, i * 512:(i + 1) * 512]
            self.bank_w = {k: None for k in self.pt}
            self.bank_r = {k: None for k in self.pt}
            self.pb = _IntAlias(self.pt)
            self.bank_w = _IntAlias(self.bank_w)
            self.bank_r = _IntAlias(self.bank_r)
            self.inputs = {}
            return
        if psum_spec is None:
            psum_spec = [(f"pb{i}", [P, 512]) for i in range(8)]
        self.pt = {}
        self.bank_w = {}
        self.bank_r = {}
        for nm, shape in psum_spec:
            self.pt[nm] = self.stack.enter_context(self.nc.psum_tensor(nm, shape, F32))
            self.bank_w[nm] = None
            self.bank_r[nm] = None
        self.pb = _IntAlias(self.pt)
        self.bank_w = _IntAlias(self.bank_w)
        self.bank_r = _IntAlias(self.bank_r)
        self.inputs = {}

    POOL_BYTES = 212800

    def _pool(self):
        if not hasattr(self, "_pool_t"):
            self._pool_t = self.stack.enter_context(
                self.nc.sbuf_tensor("POOL", [P, self.POOL_BYTES], mybir.dt.uint8))
            self._pool_off = 0
            self._pool_base = 0
        return self._pool_t

    def sb(self, name, shape, dtype):
        t = self._pool()
        assert len(shape) == 2
        pp, cols = shape
        esz = 2 if dtype in (BF16, mybir.dt.float16) else 4
        off = (self._pool_off + 63) // 64 * 64
        nb = cols * esz
        assert off + nb <= self.POOL_BYTES, (
            f"pool overflow at {name}: need {nb} at {off}")
        self._pool_off = off + nb
        v = t[:].bitcast(dtype)[:, off // esz: off // esz + cols]
        if pp != P:
            v = v[0:pp, :]
        return v

    def reset_banks(self, tok=None):
        keys = self.pt.keys() if isinstance(self.pt, dict) else self.pt.d.keys()
        for k in list(keys):
            self.bank_w[k] = tok
            self.bank_r[k] = tok

    def mark_persist(self):
        self._pool()
        self._pool_base = (self._pool_off + 63) // 64 * 64

    def reset_pool(self):
        self._pool_off = self._pool_base

    def arena(self, name, nbytes):
        nbytes = (nbytes + 63) // 64 * 64
        t = self._pool()
        off = (self._pool_off + 63) // 64 * 64
        assert off + nbytes <= self.POOL_BYTES, (
            f"pool overflow at arena {name}: need {nbytes} at {off}")
        self._pool_off = off + nbytes
        return _ArenaView(t, off, nbytes)

    @staticmethod
    def view(ar, byte_off, cols, dtype):
        esz = 2 if dtype == BF16 else 4
        assert (ar.off + byte_off) % esz == 0, (ar.off, byte_off, esz)
        base = (ar.off + byte_off) // esz
        return ar.t[:].bitcast(dtype)[:, base: base + cols]

    def inp(self, name, shape, dtype=F32):
        t = self.nc.dram_tensor(name, list(shape), dtype, kind="ExternalInput")
        self.inputs[name] = (tuple(shape), dtype)
        return t

    def outp(self, name, shape, dtype=F32):
        return self.nc.dram_tensor(name, list(shape), dtype, kind="ExternalOutput")

    def scratch(self, name, shape, dtype):
        return self.nc.dram_tensor(name, list(shape), dtype, kind="Internal")

    def pbf(self, i, n=1024):
        t = self.pt[i] if isinstance(i, str) else self.pb[i]
        return t[:].bitcast(BF16)[:, 0:n]

    def setup_consts(self):
        pg = self.pg
        identd = self.inp("c_ident", (P, P), BF16)
        identfd = self.inp("c_identf", (P, P), F32)
        self.ident = self.sb("identb", [P, P], BF16)
        self.identf = self.sb("identf", [P, P], F32)
        self.ones_col_b = self.sb("ones_col_b", [P, 1], BF16)
        self.ones_row_b = self.sb("ones_row_b", [1, P], BF16)
        self.ones_col_f = self.sb("ones_col_f", [P, 1], F32)
        self.ones_row_f = self.sb("ones_row_f", [1, P], F32)
        self.zer_b = self.sb("zer_b", [P, 512], BF16)
        t0 = pg.dma("sync", self.ident[:], identd.ap(), stream="c")
        t0b = pg.dma("sync", self.identf[:], identfd.ap(), stream="c")
        tz = pg.op("gpsimd", lambda E: E.memset(self.zer_b[:], 0.0))
        t1 = pg.op("gpsimd", lambda E: E.memset(self.ones_col_b[:], 1.0))
        t2 = pg.op("gpsimd", lambda E: E.memset(self.ones_row_b[:], 1.0),
                   waits=[t1])
        t3 = pg.op("gpsimd", lambda E: E.memset(self.ones_col_f[:], 1.0),
                   waits=[t2])
        t4 = pg.op("gpsimd", lambda E: E.memset(self.ones_row_f[:], 1.0),
                   waits=[t3])
        pg.barrier()

    def transpose(self, src, dst, bank, waits=(), evac="vector", evac_waits=()):
        pg = self.pg
        pp, m = src.shape[0], src.shape[1]
        if src.dtype == BF16:
            view = self.pbf(bank, P)
            idn = self.ident
        else:
            t = self.pt[bank] if isinstance(bank, str) else self.pb[bank]
            view = t[:]
            idn = self.identf
        wts = list(waits) + [self.bank_r[bank]]
        tw = pg.op("tensor",
                   lambda E: E.transpose(view[0:m, 0:pp], src, idn[0:pp, 0:pp]),
                   waits=wts)
        self.bank_w[bank] = tw
        tr = pg.op(evac, lambda E: E.tensor_copy(out=dst, in_=view[0:m, 0:pp]),
                   waits=[tw] + list(evac_waits))
        self.bank_r[bank] = tr
        return tr

    def finish(self, final_waits):
        pg = self.pg
        pg.op("gpsimd", lambda E: E.nop(), waits=list(final_waits), inc=None)
        pg.emit()
        self.stack.close()
        return self.nc


def emit_layernorm(ctx, sumbuf, D, stat, out_ap, out_dtype_note=None, waits=(),
                   sq_scratch=None):
    pg = ctx.pg
    t_mean = pg.op("vector",
                   lambda E: E.reduce_sum(stat[:, 0:1], sumbuf[:, 0:D], axis=AX.X),
                   waits=list(waits))
    t_sq = pg.op("scalar",
                 lambda E: E.activation(sq_scratch[:, 0:D], sumbuf[:, 0:D], AF.Square,
                                        accum_out=stat[:, 1:2]),
                 waits=list(waits) + [pg.tok("s_act")])
    t = pg.op("vector",
              lambda E: E.tensor_scalar_mul(stat[:, 2:3], stat[:, 0:1], 1.0 / D),
              waits=[t_mean, t_sq])
    t = pg.op("vector", lambda E: E.tensor_mul(stat[:, 3:4], stat[:, 2:3], stat[:, 2:3]),
              waits=[t])
    t = pg.op("vector",
              lambda E: E.tensor_scalar(stat[:, 4:5], stat[:, 1:2], 1.0 / D, None, ALU.mult),
              waits=[t])
    t = pg.op("vector", lambda E: E.tensor_sub(stat[:, 5:6], stat[:, 4:5], stat[:, 3:4]),
              waits=[t])
    t = pg.op("vector", lambda E: E.tensor_scalar_add(stat[:, 6:7], stat[:, 5:6], 1e-5),
              waits=[t])
    t_rt = pg.op("scalar", lambda E: E.sqrt(stat[:, 7:8], stat[:, 6:7]),
                 waits=[t, t_sq])
    t_rc = pg.op("vector", lambda E: E.reciprocal(stat[:, 7:8], stat[:, 7:8]),
                 waits=[t_rt, t])
    t_o = pg.op("vector",
                lambda E: E.tensor_scalar(out_ap, sumbuf[:, 0:D],
                                          stat[:, 2:3], stat[:, 7:8],
                                          ALU.subtract, ALU.mult),
                waits=[t_rc])
    return t_o



def build_l0_ffn(D=2048, HID=5120, T=2048, ctx=None, ext=None, x2bf=None):
    standalone = ctx is None
    if standalone:
        ctx = Ctx()
    pg = ctx.pg
    EX = (lambda n, sh: ctx.inp(n, sh)) if ext is None else (lambda n, sh: ext[n])
    DCH, HG, TT = D // P, HID // P, T // P
    TOKH = min(T, 1024)
    NH = T // TOKH
    TH_T = TOKH // P

    xs = EX("xslab", (T, D))
    wfc = EX("w_fc", (HID, D))
    wpr = EX("w_proj", (D, HID))
    out = ctx.outp("out", (T, D)) if ext is None else ext["out"]
    wfcT = ctx.scratch("wfcT", [HG, P, D], BF16)
    wprT = ctx.scratch("wprT", [HG, P, D], BF16)

    if standalone:
        ctx.setup_consts()

    wbuf = [ctx.sb(f"wbuf_{i}", [P, HID], F32) for i in range(2)]
    red = ctx.sb("red", [P, HG + 2], F32)
    scal = ctx.sb("scal", [P, 8], F32)
    tiny = ctx.sb("tiny", [1, 8], F32)
    x_fm = ctx.sb("x_fm", [P, DCH * TOKH], BF16)
    act = ctx.sb("act", [P, HG * TOKH], BF16)
    tern = act[:, 0:D]
    stage = act[:, D:2 * D]
    wrow = [ctx.sb(f"wrow{i}", [P, D], BF16) for i in range(2)]
    xbrow = [ctx.sb(f"xbrow{i}", [P, D], BF16) for i in range(2)]
    xf32 = [ctx.sb("xf32_0", [P, D], F32)] * 2
    sumbuf = [ctx.sb(f"sumbuf{i}", [P, D], F32) for i in range(2)]
    sqbuf = ctx.sb("sqbuf", [P, D], F32)
    q1 = sumbuf[0]
    q2 = sqbuf
    stat = ctx.sb("stat_b", [P, 8], F32)
    outbuf = [wbuf[0][:, 0:D]] * 2
    relu_t = ctx.sb("relu_t", [P, 512], F32)

    def absmean(w, ncols, which):
        ng = (w.shape[0]) // P
        toks = []
        for g in range(ng):
            b = wbuf[g % 2]
            w_prev = toks[g - 2] if g >= 2 else pg.tok("s_dve")
            t_l = pg.dma("sync", b[:, 0:ncols], w[g * P:(g + 1) * P, :], waits=[w_prev],
                         stream=f"w{g % 2}")
            t_r = pg.op("vector",
                        lambda E, b=b, g=g: E.reduce_sum(
                            red[:, g:g + 1], b[:, 0:ncols],
                            axis=AX.X, apply_absolute_value=True),
                        waits=[t_l, pg.tok("s_dve")])
            toks.append(t_r)
        t_s = pg.op("vector",
                    lambda E, ng=ng: E.reduce_sum(red[:, ng:ng + 1], red[:, 0:ng], axis=AX.X),
                    waits=[toks[-1]])
        t_m = pg.op("tensor",
                    lambda E, which=which, ng=ng: E.matmul(
                        ctx.pb[7][0:1, which:which + 1],
                        ctx.ones_col_f[:],
                        red[:, ng:ng + 1],
                        start=True, stop=True),
                    waits=[t_s, ctx.bank_r[7]])
        ctx.bank_w[7] = t_m
        return t_m

    tm0 = absmean(wfc, D, 0)
    tm1 = absmean(wpr, HID, 2)
    t1 = pg.op("vector",
               lambda E: E.tensor_scalar(tiny[0:1, 0:1], ctx.pb[7][0:1, 0:1],
                                         1.0 / (HID * D), 1e-5, ALU.mult, ALU.max),
               waits=[tm0, tm1])
    t2 = pg.op("vector",
               lambda E: E.tensor_scalar(tiny[0:1, 1:2], ctx.pb[7][0:1, 2:3],
                                         1.0 / (HID * D), 1e-5, ALU.mult, ALU.max),
               waits=[tm0, tm1, t1])
    ctx.bank_r[7] = t2
    t_bc = pg.op("tensor",
                 lambda E: E.matmul(ctx.pb[6][:, 0:2], ctx.ones_row_f[:],
                                    tiny[0:1, 0:2],
                                    start=True, stop=True),
                 waits=[t2, ctx.bank_r[6]])
    ctx.bank_w[6] = t_bc
    t = pg.op("vector", lambda E: E.tensor_copy(scal[:, 0:1], ctx.pb[6][:, 0:1]), waits=[t_bc])
    t = pg.op("vector", lambda E: E.tensor_scalar_mul(scal[:, 1:2], ctx.pb[6][:, 0:1], 0.7),
              waits=[t_bc, t])
    t = pg.op("vector", lambda E: E.tensor_scalar_mul(scal[:, 2:3], ctx.pb[6][:, 0:1], -0.7),
              waits=[t_bc, t])
    t = pg.op("vector", lambda E: E.tensor_copy(scal[:, 3:4], ctx.pb[6][:, 1:2]),
              waits=[t_bc, t])
    t = pg.op("vector", lambda E: E.tensor_scalar_mul(scal[:, 4:5], ctx.pb[6][:, 1:2], 0.7),
              waits=[t_bc, t])
    t = pg.op("vector", lambda E: E.tensor_scalar_mul(scal[:, 5:6], ctx.pb[6][:, 1:2], -0.7),
              waits=[t_bc, t])
    ctx.bank_r[6] = t
    pg.barrier()

    def tern_transpose(w, wT, rows_are_h, scol):
        am = scal[:, scol:scol + 1]
        thr = scal[:, scol + 1:scol + 2]
        nthr = scal[:, scol + 2:scol + 3]
        prev_out = pg.tok(pg.sem("q_gpsimd_wT"))
        for hg in range(HG):
            b = wbuf[hg % 2]
            if rows_are_h:
                src = w[hg * P:(hg + 1) * P, :]
                dst = b[:, 0:D]
            else:
                src = w[:, hg * P:(hg + 1) * P].rearrange("(c p) h -> p c h", p=P)
                dst = b[:, 0:D].rearrange("p (c h) -> p c h", h=P)
            t_l = pg.dma("sync", dst, src,
                         waits=[prev_out, pg.tok("s_dve")], stream=f"w{hg % 2}")
            t_q = pg.op("vector",
                        lambda E, b=b: E.tensor_single_scalar(q1[:, 0:D], b[:, 0:D], thr, ALU.is_gt),
                        waits=[t_l, pg.tok("s_dve")])
            t_q = pg.op("vector",
                        lambda E, b=b: E.tensor_single_scalar(q2[:, 0:D], b[:, 0:D], nthr, ALU.is_lt),
                        waits=[t_l, t_q])
            t_q = pg.op("vector", lambda E: E.tensor_sub(q1[:, 0:D], q1[:, 0:D], q2[:, 0:D]),
                        waits=[t_q])
            t_q = pg.op("vector", lambda E: E.tensor_scalar_mul(tern[:, 0:D], q1[:, 0:D], am),
                        waits=[t_q])
            ev_toks = []
            for dc in range(DCH):
                ev_toks.append(ctx.transpose(tern[:, dc * P:(dc + 1) * P],
                                             stage[:, dc * P:(dc + 1) * P],
                                             bank=dc % 2, waits=[t_q],
                                             evac_waits=[prev_out]))
            prev_out = pg.dma("gpsimd", wT[hg], stage[:, 0:D], waits=ev_toks[-2:],
                              stream="wT")
        return prev_out

    tern_transpose(wfc, wfcT, True, 0)
    tern_transpose(wpr, wprT, False, 3)
    pg.barrier()

    out_toks = []
    for h in range(NH):
        for tt in range(TH_T):
            g_tt = h * TH_T + tt
            b = xbrow[tt % 2]
            t_l = pg.dma("gpsimd", b[:, 0:D], xs[g_tt * P:(g_tt + 1) * P, :],
                         waits=[pg.tok("s_dve") if tt >= 2 else None],
                         stream=f"x{tt % 2}")
            for dc in range(DCH):
                ctx.transpose(b[:, dc * P:(dc + 1) * P],
                              x_fm[:, dc * TOKH + tt * P: dc * TOKH + (tt + 1) * P],
                              bank=dc % 2, waits=[t_l])
        pg.barrier()

        for hg in range(HG):
            b = wrow[hg % 2]
            t_l = pg.dma("sync", b[:, 0:D], wfcT[hg],
                         waits=[pg.tok("s_pe") if hg >= 2 else None],
                         stream=f"w{hg % 2}")
            TB = min(512, TOKH)
            for tb in range(TOKH // TB):
                bank = tb % 2
                pt = ctx.pb[bank]
                mm = None
                for dc in range(DCH):
                    st, sp = dc == 0, dc == DCH - 1
                    wts = [t_l]
                    if st:
                        wts.append(ctx.bank_r[bank])
                    mm = pg.op("tensor",
                               lambda E, pt=pt, b=b, dc=dc, tb=tb, st=st, sp=sp:
                               E.matmul(pt[:, 0:TB], b[:, dc * P:(dc + 1) * P],
                                        x_fm[:, dc * TOKH + tb * TB: dc * TOKH + (tb + 1) * TB],
                                        start=st, stop=sp),
                               waits=wts)
                ctx.bank_w[bank] = mm
                t_r = pg.op("scalar",
                            lambda E, pt=pt: E.activation(relu_t[:, 0:TB], pt[:, 0:TB], AF.Relu),
                            waits=[mm, pg.tok("s_dve")])
                t_a = pg.op("vector",
                            lambda E, hg=hg, tb=tb: E.tensor_mul(
                                act[:, hg * TOKH + tb * TB: hg * TOKH + (tb + 1) * TB],
                                relu_t[:, 0:TB], relu_t[:, 0:TB]),
                            waits=[t_r])
                ctx.bank_r[bank] = t_a
        pg.barrier()

        NDB = ceil_div(D, 512)
        DBW = min(512, D)
        BLKT = min(256, TOKH)
        NSUB = BLKT // P
        assert NSUB * NDB <= 8
        for blk in range(TOKH // BLKT):
            mm_last = [[None] * NDB for _ in range(NSUB)]
            for hg in range(HG):
                b = wrow[hg % 2]
                t_l = pg.dma("sync", b[:, 0:D], wprT[hg],
                             waits=[pg.tok("s_pe") if hg >= 2 else None],
                             stream=f"w{hg % 2}")
                for sub in range(NSUB):
                    tt = h * TH_T + blk * NSUB + sub
                    lt = blk * BLKT + sub * P
                    for db in range(NDB):
                        bank = sub * NDB + db
                        st = hg == 0
                        wts = [t_l]
                        if st:
                            wts.append(ctx.bank_r[bank])
                        mm_last[sub][db] = pg.op(
                            "tensor",
                            lambda E, bank=bank, b=b, db=db, hg=hg, lt=lt, st=st:
                            E.matmul(ctx.pb[bank][:, 0:DBW],
                                     act[:, hg * TOKH + lt: hg * TOKH + lt + P],
                                     b[:, db * DBW:(db + 1) * DBW],
                                     start=st, stop=(hg == HG - 1)),
                            waits=wts)
            for sub in range(NSUB):
                tt = h * TH_T + blk * NSUB + sub
                g_tt = tt
                bx = xf32[sub]
                t_rx = pg.dma("sync", bx[:, 0:D], xs[g_tt * P:(g_tt + 1) * P, :],
                              waits=[pg.tok("s_dve")], stream=f"r{sub}")
                sb_ = sumbuf[sub]
                t_s = None
                for db in range(NDB):
                    bank = sub * NDB + db
                    ctx.bank_w[bank] = mm_last[sub][db]
                    t_s = pg.op("vector",
                                lambda E, bank=bank, db=db, bx=bx, sb_=sb_: E.tensor_add(
                                    sb_[:, db * DBW:(db + 1) * DBW],
                                    ctx.pb[bank][:, 0:DBW],
                                    bx[:, db * DBW:(db + 1) * DBW]),
                                waits=[mm_last[sub][db], t_rx, pg.tok("s_dve")])
                    ctx.bank_r[bank] = t_s
                ob = outbuf[sub]
                t_o = emit_layernorm(ctx, sb_, D, stat, ob[:, 0:D],
                                     waits=[t_s,
                                            out_toks[-1] if out_toks else None],
                                     sq_scratch=sqbuf)
                t_out = pg.dma("sync", out[g_tt * P:(g_tt + 1) * P, :], ob[:, 0:D],
                               waits=[t_o], stream=f"o{sub % 2}")
                out_toks.append(t_out)
                if x2bf is not None:
                    out_toks.append(pg.dma(
                        "gpsimd", x2bf[g_tt * P:(g_tt + 1) * P, :], ob[:, 0:D],
                        waits=[t_o], stream=f"ob{sub % 2}"))
        pg.barrier()

    if standalone:
        return ctx.finish(pg.all_dma_toks())
    return None





def emit_rope(pg, kn3, kr3, e3, cs, sn, t_n, HALF, ROPE, HD):
    t_r1 = pg.op("vector", lambda E: E.tensor_tensor(
        kr3[:, :, 0:HALF], kn3[:, :, 0:HALF], cs, ALU.mult), waits=[t_n, pg.tok("s_dve")])
    t_r2 = pg.op("vector", lambda E: E.tensor_tensor(
        e3, kn3[:, :, HALF:ROPE], sn, ALU.mult), waits=[t_r1])
    t_r3 = pg.op("vector", lambda E: E.tensor_tensor(
        kr3[:, :, 0:HALF], kr3[:, :, 0:HALF], e3, ALU.add), waits=[t_r2])
    t_r4 = pg.op("vector", lambda E: E.tensor_tensor(
        kr3[:, :, HALF:ROPE], kn3[:, :, HALF:ROPE], cs, ALU.mult), waits=[t_r3])
    t_r5 = pg.op("vector", lambda E: E.tensor_tensor(
        e3, kn3[:, :, 0:HALF], sn, ALU.mult), waits=[t_r4])
    t_r6 = pg.op("vector", lambda E: E.tensor_tensor(
        kr3[:, :, HALF:ROPE], kr3[:, :, HALF:ROPE], e3, ALU.subtract), waits=[t_r5])
    t_r7 = pg.op("vector", lambda E: E.tensor_copy(
        kr3[:, :, ROPE:HD], kn3[:, :, ROPE:HD]), waits=[t_r6])
    return t_r7



def build_l0_attn(D=2048, NH=16, NKV=8, HD=128, S=8192, T=2048, R=32,
                  ROPE=64, EPS=1.1920929e-07, ctx=None, ext=None):
    KVD = NKV * HD
    DCH = D // P
    KT = S // P
    TT = T // P
    QG = min(1024, T)
    NQG = T // QG
    QGT = QG // P
    HPH = NH // 2
    HPK = NH // NKV
    HALF = ROPE // 2
    SC = 1.0 / np.sqrt(HD)
    NB = float(np.sqrt(HD))

    standalone = ctx is None
    if standalone:
        ctx = Ctx(psum_spec=[("ps0", [P, 1024]), ("ps1", [P, 1024]),
                             ("av", [P, 1536]), ("sp", [P, 512])])
    pg = ctx.pg
    EX = (lambda n, sh: ctx.inp(n, sh)) if ext is None else (lambda n, sh: ext[n])

    xb = EX("xb", (S, D))
    Wq = EX("W_q", (D, D)); Wk = EX("W_k", (KVD, D))
    Wv = EX("W_v", (KVD, D)); Wo = EX("W_o", (D, D))
    Aq = EX("A_q", (R, D)); Bq = EX("B_q", (D, R))
    Ak = EX("A_k", (R, D)); Bk = EX("B_k", (KVD, R))
    Av = EX("A_v", (R, D)); Bv = EX("B_v", (KVD, R))
    Ao = EX("A_o", (R, D)); Bo = EX("B_o", (D, R))
    sq = EX("s_q", (1,)); sk = EX("s_k", (1,))
    sv = EX("s_v", (1,)); so = EX("s_o", (1,))
    qg_in = EX("q_gain", (NH,))
    cosk = EX("cosk", (S, HALF)); sink = EX("sink", (S, HALF))
    kbias = EX("kbias", (P, KT))
    bandm = EX("bandm", (P, 2 * QG - P))
    out = ctx.outp("out", (T, D)) if ext is None else ext["out"]
    kscr = ctx.scratch("kscr", [NKV, KT, P, P], BF16)
    vscr = ctx.scratch("vscr", [KT, P, NKV * 129], BF16)
    qscr = ctx.scratch("qscr", [NH, TT, P, P], BF16)
    yscr = ctx.scratch("yscr", [NH, TT, P, P], BF16)

    if standalone:
        ctx.setup_consts()

    QW = (NH // 2) * HD
    AR1B = 2 * DCH * max(T, D)
    AR2B = max(2 * DCH * 2 * KVD, 2 * DCH * QW, 2 * (KT * P + KT * 129 + T),
               8192 + 16 * D + 2 * NH * P)
    AR1 = ctx.arena("AR1", AR1B)
    AR2 = ctx.arena("AR2", AR2B)
    x_own = ctx.view(AR1, 0, DCH * T, BF16)
    wo_big = ctx.view(AR1, 0, DCH * D, BF16)
    wbig = ctx.view(AR2, 0, DCH * 2 * KVD, BF16)
    KH0 = ctx.view(AR2, 0, KT * P, BF16)
    VH0 = ctx.view(AR2, 2 * KT * P, KT * 129, BF16)
    QH0 = ctx.view(AR2, 2 * (KT * P + KT * 129), T, BF16)
    cos_sb = ctx.sb("cos_sb", [P, KT * HALF], F32)
    sin_sb = ctx.sb("sin_sb", [P, KT * HALF], F32)
    kb_sb = ctx.sb("kb_sb", [P, KT], F32)
    gbc = ctx.sb("gbc", [P, NH], F32)
    mneg = ctx.sb("mneg", [P, NH], F32)
    lA = {}
    for nm in ("q", "k", "v", "o"):
        lA[nm] = ctx.sb(f"lA_{nm}", [P, DCH * R], BF16)
    lB = {}
    for nm, wdt in (("q", D), ("k", KVD), ("v", KVD), ("o", D)):
        lB[nm] = ctx.sb(f"lB_{nm}", [R, wdt], BF16)
    wld = [ctx.sb("wld0", [P, D], F32)] * 2
    wldb = ctx.sb("wldb", [P, D], BF16)
    xstg = [ctx.sb(f"xstg{i}", [P, DCH * P], BF16) for i in range(2)]
    xrow = [ctx.sb(f"xrow{i}", [P, D], BF16) for i in range(2)]
    epi = ctx.sb("epi", [P, max(KVD, 1024)], F32)
    epi2 = ctx.sb("epi2", [P, 40], F32)
    KNB = 4 * max(KVD, 1024)
    _a4t = ((4 * QG + 2 * P + 2 * (min(1024, T) // P) * P + 63) // 64) * 64
    AR3 = ctx.arena("AR3", max(2 * KNB + 2 * NKV * 129 + 2 * NKV * P,
                               2 * KNB + (NH // 2) * P * 2,
                               _a4t + 2 * (2 * QG - P) + 2 * NH * KT))
    bandm_sb = ctx.view(AR3, _a4t, 2 * QG - P, BF16)
    bias_hk = ctx.view(AR3, _a4t + 2 * (2 * QG - P), NH * KT, BF16)
    knorm = ctx.view(AR3, 0, max(KVD, 1024), F32)
    krot = ctx.view(AR3, KNB, max(KVD, 1024), F32)
    vstg = ctx.view(AR3, 2 * KNB, NKV * 129, BF16)
    ktr = ctx.view(AR3, 2 * KNB + 2 * NKV * 129, NKV * P, BF16)
    tlora = ctx.sb("tlora", [R, 2 * P], BF16)
    qstg = ctx.view(AR3, 2 * KNB, (NH // 2) * P, BF16)
    PTb = [ctx.view(AR3, i * 2 * QG, QG, BF16) for i in range(2)]
    ystg = ctx.view(AR3, 4 * QG, P, BF16)
    ystag = ctx.view(AR3, 4 * QG + 2 * P, min(1024, T) // P * P, BF16)
    rcp = epi2[:, 38:39]
    tiny = epi2[0:1, 0:max(NH, 8)]
    xres = [ctx.view(AR2, 8192, D, F32)] * 2
    sumb = ctx.view(AR2, 8192 + 4 * D, D, F32)
    lnsq = ctx.view(AR2, 8192 + 8 * D, D, F32)
    stat = epi2[:, 32:40]
    obuf = [ctx.view(AR2, 8192 + 12 * D, D, F32)] * 2

    pg.dma("sync", cos_sb[:].rearrange("p (t c) -> p t c", c=HALF),
           cosk.ap().rearrange("(t p) c -> p t c", p=P), stream="c")
    pg.dma("sync", sin_sb[:].rearrange("p (t c) -> p t c", c=HALF),
           sink.ap().rearrange("(t p) c -> p t c", p=P), stream="c")
    pg.dma("sync", kb_sb[:], kbias.ap(), stream="c")
    t_g = pg.dma("sync", tiny[0:1, 0:NH], qg_in.ap().unsqueeze(0), stream="c")
    sc_t = epi2[0:1, 16:20]
    pg.dma("sync", sc_t[0:1, 0:1], sq.ap().unsqueeze(0), stream="c")
    pg.dma("sync", sc_t[0:1, 1:2], sk.ap().unsqueeze(0), stream="c")
    pg.dma("sync", sc_t[0:1, 2:3], sv.ap().unsqueeze(0), stream="c")
    t_sc = pg.dma("sync", sc_t[0:1, 3:4], so.ap().unsqueeze(0), stream="c")
    pg.barrier()
    t_mm = pg.op("tensor", lambda E: E.matmul(ctx.pt["sp"][:, 0:NH], ctx.ones_row_f[:],
                                              tiny[0:1, 0:NH], start=True, stop=True),
                 waits=[pg.tok("q_sync_c")])
    t2 = pg.op("vector", lambda E: E.tensor_copy(gbc[:, 0:NH], ctx.pt["sp"][:, 0:NH]),
               waits=[t_mm])
    t3 = pg.op("scalar", lambda E: E.activation(mneg[:, 0:NH], gbc[:, 0:NH], AF.Abs),
               waits=[t2])
    t4 = pg.op("vector", lambda E: E.tensor_scalar_mul(mneg[:, 0:NH], mneg[:, 0:NH], -NB),
               waits=[t3])
    t_mm2 = pg.op("tensor", lambda E: E.matmul(ctx.pt["sp"][:, 64:68], ctx.ones_row_f[:],
                                               sc_t[0:1, 0:4], start=True, stop=True),
                  waits=[t_sc, t2, t4])
    scbc = ctx.sb("scbc", [P, 4], F32)
    t6 = pg.op("vector", lambda E: E.tensor_copy(scbc[:, 0:4], ctx.pt["sp"][:, 64:68]),
               waits=[t_mm2])
    pg.barrier()

    def prep_lora(Amat, Bmat, nm, outw, scol=None):
        t_l = pg.dma("sync", wld[0][0:R, 0:D], Amat.ap(), waits=[pg.tok("s_dve")],
                     stream="w0")
        t_c = pg.op("vector", lambda E: E.tensor_copy(wldb[0:R, 0:D], wld[0][0:R, 0:D]),
                    waits=[t_l])
        for dc in range(DCH):
            ctx.transpose(wldb[0:R, dc * P:(dc + 1) * P],
                          lA[nm][:, dc * R:(dc + 1) * R], "sp", waits=[t_c])
        t_l2 = pg.dma("sync", wld[1][:, 0:(outw // P) * R].rearrange(
            "p (c r) -> p c r", r=R),
            Bmat.ap().rearrange("(c p) r -> p c r", p=P),
            waits=[pg.tok("s_dve")], stream="w1")
        t_c2 = pg.op("vector", lambda E: E.tensor_copy(
            wldb[:, 0:(outw // P) * R], wld[1][:, 0:(outw // P) * R]),
            waits=[t_l2, pg.tok("s_pe")])
        for c in range(outw // P):
            ctx.transpose(wldb[:, c * R:(c + 1) * R],
                          lB[nm][0:R, c * P:(c + 1) * P], "sp", waits=[t_c2])

    prep_lora(Aq, Bq, "q", D)
    prep_lora(Ak, Bk, "k", KVD)
    prep_lora(Av, Bv, "v", KVD)
    prep_lora(Ao, Bo, "o", D)
    pg.barrier()

    KV2 = 2 * KVD
    def prep_w(Wmat, outw, col_off, stride, scale_col, dest, dest_onecol=None):
        for oc in range(outw // P):
            b = wld[oc % 2]
            t_l = pg.dma("sync", b[:, 0:D], Wmat[oc * P:(oc + 1) * P, :],
                         waits=[pg.tok("s_dve")], stream=f"w{oc % 2}")
            t_c = pg.op("vector",
                        lambda E, b=b: E.tensor_scalar_mul(wldb[:, 0:D], b[:, 0:D],
                                                           scbc[:, scale_col:scale_col + 1]),
                        waits=[t_l, pg.tok("s_dve")])
            for dc in range(DCH):
                ctx.transpose(wldb[:, dc * P:(dc + 1) * P],
                              dest[:, dc * stride + col_off + oc * P:
                                   dc * stride + col_off + (oc + 1) * P],
                              "sp", waits=[t_c])

    prep_w(Wk, KVD, 0, KV2, 1, wbig)
    prep_w(Wv, KVD, KVD, KV2, 2, wbig)
    pg.barrier()

    vst_init = pg.op("gpsimd", lambda E: E.memset(vstg[:], 1.0))
    rms_c = 1.0 / HD
    for kt in range(KT):
        own = kt < TT
        xr = xrow[kt % 2]
        t_lx = pg.dma("gpsimd", xr[:, 0:D], xb[kt * P:(kt + 1) * P, :],
                      waits=[pg.tok("s_pe")], stream=f"x{kt % 2}")
        if own:
            xfm = lambda dc, kt=kt: x_own[:, dc * T + kt * P: dc * T + (kt + 1) * P]
        else:
            stg = xstg[kt % 2]
            xfm = lambda dc, stg=stg: stg[:, dc * P:(dc + 1) * P]
        for dc in range(DCH):
            ctx.transpose(xr[:, dc * P:(dc + 1) * P], xfm(dc), "sp",
                          waits=[t_lx], evac_waits=[pg.tok("s_pe")])
        t_x = pg.tok("s_dve")
        for j, nm in enumerate(("k", "v")):
            mm = None
            for dc in range(DCH):
                mm = pg.op("tensor",
                           lambda E, dc=dc, j=j, nm=nm, xfm=xfm, st=(dc == 0), sp_=(dc == DCH - 1):
                           E.matmul(ctx.pt["av"][0:R, j * P:j * P + P],
                                    lA[nm][:, dc * R:(dc + 1) * R],
                                    xfm(dc), start=st, stop=sp_),
                           waits=[t_x, ctx.bank_r["av"] if dc == 0 else None])
            ctx.bank_w["av"] = mm
            tev = pg.op("vector",
                        lambda E, j=j: E.tensor_copy(tlora[0:R, j * P:(j + 1) * P],
                                                     ctx.pt["av"][0:R, j * P:j * P + P]),
                        waits=[mm, pg.tok("s_act")])
            ctx.bank_r["av"] = tev
        t_tl = pg.tok("s_dve")
        kps = ["ps0", "ps1"][kt % 2]
        for j, (nm, ps) in enumerate((("k", kps), ("v", "av"))):
            co = j * KVD
            mm = None
            for dc in range(DCH):
                for nb in range(ceil_div(KVD, 512)):
                    st = dc == 0
                    w_ = [t_x]
                    if st and nb == 0:
                        w_.append(ctx.bank_r[ps])
                    po = 512 if j == 1 else 0
                    mm = pg.op("tensor",
                               lambda E, dc=dc, nb=nb, co=co, ps=ps, st=st, xfm=xfm, po=po:
                               E.matmul(ctx.pt[ps][:, po + nb * 512:po + min(KVD, (nb + 1) * 512)],
                                        xfm(dc),
                                        wbig[:, dc * KV2 + co + nb * 512:
                                             dc * KV2 + co + min(KVD, (nb + 1) * 512)],
                                        start=st, stop=False),
                               waits=w_)
            for nb in range(ceil_div(KVD, 512)):
                po = 512 if j == 1 else 0
                mm = pg.op("tensor",
                           lambda E, nb=nb, j=j, nm=nm, ps=ps, po=po:
                           E.matmul(ctx.pt[ps][:, po + nb * 512:po + min(KVD, (nb + 1) * 512)],
                                    tlora[0:R, j * P:(j + 1) * P],
                                    lB[nm][0:R, nb * 512:min(KVD, (nb + 1) * 512)],
                                    start=False, stop=True),
                           waits=[t_tl])
            ctx.bank_w[ps] = mm
        t_sq = pg.op("scalar",
                     lambda E, kps=kps: E.activation(epi[:, 0:KVD], ctx.pt[kps][:, 0:KVD],
                                                     AF.Square),
                     waits=[ctx.bank_w[kps], pg.tok("s_dve")])
        t_red = pg.op("vector",
                      lambda E: E.reduce_sum(
                          epi2[:, 0:NKV],
                          epi[:, 0:KVD].rearrange("p (h d) -> p h d", d=HD), axis=AX.X),
                      waits=[t_sq])
        t_st = pg.op("vector",
                     lambda E: E.tensor_scalar(epi2[:, 8:8 + NKV], epi2[:, 0:NKV],
                                               rms_c, EPS, ALU.mult, ALU.add),
                     waits=[t_red])
        t_rt = pg.op("scalar", lambda E: E.sqrt(epi2[:, 16:16 + NKV], epi2[:, 8:8 + NKV]),
                     waits=[t_st, t_sq])
        t_rc = pg.op("vector", lambda E: E.reciprocal(epi2[:, 16:16 + NKV],
                                                      epi2[:, 16:16 + NKV]),
                     waits=[t_rt])
        t_n = pg.op("vector",
                    lambda E, kps=kps: E.tensor_tensor(
                        knorm[:, 0:KVD].rearrange("p (h d) -> p h d", d=HD),
                        ctx.pt[kps][:, 0:KVD].rearrange("p (h d) -> p h d", d=HD),
                        epi2[:, 16:16 + NKV].unsqueeze(2).broadcast_to(
                            [P, NKV, HD]),
                        ALU.mult),
                    waits=[t_rc])
        ctx.bank_r[kps] = t_n
        kn3 = knorm[:, 0:KVD].rearrange("p (h d) -> p h d", d=HD)
        kr3 = krot[:, 0:KVD].rearrange("p (h d) -> p h d", d=HD)
        cs = cos_sb[:, kt * HALF:(kt + 1) * HALF].unsqueeze(1).broadcast_to([P, NKV, HALF])
        sn = sin_sb[:, kt * HALF:(kt + 1) * HALF].unsqueeze(1).broadcast_to([P, NKV, HALF])
        e3 = epi[:, 0:NKV * HALF].rearrange("p (h c) -> p h c", c=HALF)
        t_r7 = emit_rope(pg, kn3, kr3, e3, cs, sn, t_n, HALF, ROPE, HD)
        tr_toks = []
        for hh in range(NKV):
            tr_toks.append(ctx.transpose(krot[:, hh * HD:hh * HD + P],
                                         ktr[:, hh * P:(hh + 1) * P], "sp",
                                         waits=[t_r7],
                                         evac_waits=[pg.tok("q_gpsimd_ks")
                                                     if kt > 0 else None]))
        pg.dma("gpsimd", kscr[:, kt].rearrange("h p n -> p h n"),
               ktr[:].rearrange("p (h n) -> p h n", n=P),
               waits=[tr_toks[-1]], stream="ks")
        t_v = pg.op("vector",
                    lambda E: E.tensor_copy(
                        vstg[:].rearrange("p (h n) -> p h n", n=129)[:, :, 0:P],
                        ctx.pt["av"][:, 512:512 + KVD].rearrange("p (h d) -> p h d", d=HD)),
                    waits=[ctx.bank_w["av"], vst_init,
                           pg.tok("q_gpsimd_vs") if kt > 0 else None])
        ctx.bank_r["av"] = t_v
        pg.dma("gpsimd", vscr[kt], vstg[:], waits=[t_v], stream="vs")
    pg.barrier()

    for half in range(2):
        QW = HPH * HD
        for oc in range(QW // P):
            b = wld[oc % 2]
            goc = half * (QW // P) + oc
            t_l = pg.dma("sync", b[:, 0:D], Wq[goc * P:(goc + 1) * P, :],
                         waits=[pg.tok("s_dve")], stream=f"w{oc % 2}")
            t_c = pg.op("vector",
                        lambda E, b=b: E.tensor_scalar_mul(wldb[:, 0:D], b[:, 0:D],
                                                           scbc[:, 0:1]),
                        waits=[t_l, pg.tok("s_dve")])
            for dc in range(DCH):
                ctx.transpose(wldb[:, dc * P:(dc + 1) * P],
                              wbig[:, dc * QW + oc * P: dc * QW + (oc + 1) * P],
                              "sp", waits=[t_c], evac_waits=[pg.tok("s_pe")])
        pg.barrier()
        for tt in range(TT):
            xfm = lambda dc, tt=tt: x_own[:, dc * T + tt * P: dc * T + (tt + 1) * P]
            mm = None
            for dc in range(DCH):
                mm = pg.op("tensor",
                           lambda E, dc=dc, xfm=xfm, st=(dc == 0), sp_=(dc == DCH - 1):
                           E.matmul(ctx.pt["av"][0:R, 0:P],
                                    lA["q"][:, dc * R:(dc + 1) * R],
                                    xfm(dc), start=st, stop=sp_),
                           waits=[ctx.bank_r["av"] if dc == 0 else None])
            ctx.bank_w["av"] = mm
            tev = pg.op("vector",
                        lambda E: E.tensor_copy(tlora[0:R, 0:P], ctx.pt["av"][0:R, 0:P]),
                        waits=[mm, pg.tok("s_act")])
            ctx.bank_r["av"] = tev
            NBQ = ceil_div(QW, 512)
            QBW = min(QW, 512)
            mm = None
            for dc in range(DCH):
                for nb in range(NBQ):
                    st = dc == 0
                    w_ = []
                    if st and nb == 0:
                        w_.append(ctx.bank_r["ps0"])
                    mm = pg.op("tensor",
                               lambda E, dc=dc, nb=nb, st=st, xfm=xfm:
                               E.matmul(ctx.pt["ps0"][:, nb * QBW:(nb + 1) * QBW],
                                        xfm(dc),
                                        wbig[:, dc * QW + nb * QBW:
                                             dc * QW + (nb + 1) * QBW],
                                        start=st, stop=False),
                               waits=w_)
            for nb in range(NBQ):
                mm = pg.op("tensor",
                           lambda E, nb=nb, half=half:
                           E.matmul(ctx.pt["ps0"][:, nb * QBW:(nb + 1) * QBW],
                                    tlora[0:R, 0:P],
                                    lB["q"][0:R, half * QW + nb * QBW:
                                            half * QW + (nb + 1) * QBW],
                                    start=False, stop=True),
                           waits=[tev])
            ctx.bank_w["ps0"] = mm
            t_sq = pg.op("scalar",
                         lambda E: E.activation(epi[:, 0:QW], ctx.pt["ps0"][:, 0:QW],
                                                AF.Square),
                         waits=[mm, pg.tok("s_dve")])
            t_red = pg.op("vector",
                          lambda E: E.reduce_sum(
                              epi2[:, 0:HPH],
                              epi[:, 0:QW].rearrange("p (h d) -> p h d", d=HD),
                              axis=AX.X),
                          waits=[t_sq])
            t_st = pg.op("vector",
                         lambda E: E.tensor_scalar(epi2[:, 8:8 + HPH], epi2[:, 0:HPH],
                                                   rms_c, EPS, ALU.mult, ALU.add),
                         waits=[t_red])
            t_rt = pg.op("scalar", lambda E: E.sqrt(epi2[:, 16:16 + HPH],
                                                    epi2[:, 8:8 + HPH]),
                         waits=[t_st, t_sq])
            t_rc = pg.op("vector", lambda E: E.reciprocal(epi2[:, 16:16 + HPH],
                                                          epi2[:, 16:16 + HPH]),
                         waits=[t_rt])
            t_gm = pg.op("vector",
                         lambda E, half=half: E.tensor_mul(
                             epi2[:, 24:24 + HPH], epi2[:, 16:16 + HPH],
                             gbc[:, half * HPH:(half + 1) * HPH]),
                         waits=[t_rc])
            t_n = pg.op("vector",
                        lambda E: E.tensor_tensor(
                            knorm[:, 0:QW].rearrange("p (h d) -> p h d", d=HD),
                            ctx.pt["ps0"][:, 0:QW].rearrange("p (h d) -> p h d", d=HD),
                            epi2[:, 24:24 + HPH].unsqueeze(2).broadcast_to(
                                [P, HPH, HD]),
                            ALU.mult),
                        waits=[t_gm])
            ctx.bank_r["ps0"] = t_n
            kn3 = knorm[:, 0:QW].rearrange("p (h d) -> p h d", d=HD)
            kr3 = krot[:, 0:QW].rearrange("p (h d) -> p h d", d=HD)
            cs = cos_sb[:, tt * HALF:(tt + 1) * HALF].unsqueeze(1).broadcast_to([P, HPH, HALF])
            sn = sin_sb[:, tt * HALF:(tt + 1) * HALF].unsqueeze(1).broadcast_to([P, HPH, HALF])
            e3 = epi[:, 0:HPH * HALF].rearrange("p (h c) -> p h c", c=HALF)
            t_r7 = emit_rope(pg, kn3, kr3, e3, cs, sn, t_n, HALF, ROPE, HD)
            for hh in range(HPH):
                ctx.transpose(krot[:, hh * HD:hh * HD + P],
                              qstg[:, hh * P:(hh + 1) * P], "sp",
                              waits=[t_r7],
                              evac_waits=[pg.tok("q_gpsimd_qs")])
            pg.dma("gpsimd", qscr[half * HPH:(half + 1) * HPH, tt].rearrange(
                "h p n -> p h n"),
                qstg[:].rearrange("p (h n) -> p h n", n=P),
                waits=[pg.tok("s_dve")], stream="qs")
        pg.barrier()

    def kt_list(qgi):
        own_band = list(range(qgi * QG // P, (qgi + 1) * QG // P))
        own_full = list(range(0, qgi * QG // P))
        tail = list(range(TT, KT))
        return own_full, own_band, tail

    avbase = lambda i: (i // 3) * 512 + (i % 3) * 129
    av_tok = [None, None]
    t_bm = pg.dma("gpsimd", bandm_sb[:], bandm.ap(), stream="c",
                  waits=[pg.tok("s_dve"), pg.tok("s_pe")])
    tb = pg.tok("s_dve")
    for h in range(NH):
        tb = pg.op("vector",
                   lambda E, h=h: E.tensor_scalar_add(bias_hk[:, h * KT:(h + 1) * KT],
                                                      kb_sb[:, 0:KT], mneg[:, h:h + 1]),
                   waits=[tb, pg.tok("s_pe"), t_bm])
    t_bias = tb

    for kv in range(NKV):
        t_kh = pg.dma("sync", KH0.rearrange("p (t n) -> p t n", n=P),
                      kscr[kv].rearrange("t p n -> p t n"),
                      waits=[pg.tok("s_pe")], stream="kh")
        t_vh = pg.dma("sync", VH0.rearrange("p (t n) -> p t n", n=129),
                      vscr[:, :, kv * 129:(kv + 1) * 129].rearrange("t p n -> p t n"),
                      waits=[pg.tok("s_pe")], stream="vh")
        for hh in range(HPK):
            h = kv * HPK + hh
            t_qh = pg.dma("sync", QH0.rearrange("p (t n) -> p t n", n=P),
                          qscr[h].rearrange("t p n -> p t n"),
                          waits=[pg.tok("s_pe")], stream="qh")
            for qgi in range(NQG):
                own_full, own_band, tail = kt_list(qgi)
                kts = own_full + own_band + tail
                first_kt = kts[0]
                last_kt = kts[-1]
                psn = ["ps0", "ps1"]
                nav_banks = ceil_div(avbase(QGT - 1) + 129, 512)
                mmz = None
                for bkk in range(nav_banks):
                    mmz = pg.op("tensor",
                                lambda E, bkk=bkk: E.matmul(
                                    ctx.pt["av"][:, bkk * 512:(bkk + 1) * 512],
                                    ctx.ident[:], ctx.zer_b[:, 0:512],
                                    start=True, stop=True),
                                waits=[ctx.bank_r["av"] if bkk == 0 else None])
                ctx.bank_w["av"] = mmz
                for ki, kt in enumerate(kts):
                    ps = psn[ki % 2]
                    PT = PTb[ki % 2]
                    mm = None
                    for nb in range(QG // 512 if QG >= 512 else 1):
                        NBW = min(512, QG)
                        w_ = [t_kh, t_qh, ctx.bank_r[ps] if nb == 0 else None]
                        mm = pg.op("tensor",
                                   lambda E, ps=ps, kt=kt, nb=nb, NBW=NBW, h=h, qgi=qgi:
                                   E.matmul(ctx.pt[ps][:, nb * NBW:(nb + 1) * NBW],
                                            KH0[:, kt * P:(kt + 1) * P],
                                            QH0[:, qgi * QG + nb * NBW:
                                                qgi * QG + (nb + 1) * NBW],
                                            start=True, stop=True),
                                   waits=w_)
                    ctx.bank_w[ps] = mm
                    t_pre = mm
                    if kt in own_band:
                        s = kt * P - qgi * QG
                        off = (QG - P) - s
                        t_pre = pg.op("vector",
                                      lambda E, ps=ps, off=off: E.tensor_add(
                                          ctx.pt[ps][:, 0:QG],
                                          ctx.pt[ps][:, 0:QG],
                                          bandm_sb[:, off:off + QG]),
                                      waits=[mm, t_bm, pg.tok("s_dve")])
                    t_e = pg.op("scalar",
                                lambda E, ps=ps, h=h, kt=kt, PT=PT: E.activation(
                                    PT[:, 0:QG], ctx.pt[ps][:, 0:QG], AF.Exp,
                                    bias=bias_hk[:, h * KT + kt:h * KT + kt + 1],
                                    scale=SC),
                                waits=[t_pre, t_bias, av_tok[ki % 2], pg.tok("s_act")])
                    ctx.bank_r[ps] = t_e
                    for i in range(QGT):
                        w_ = [t_e, t_vh, t_qh]
                        mm2 = pg.op("tensor",
                                    lambda E, i=i, kt=kt, PT=PT: E.matmul(
                                        ctx.pt["av"][:, avbase(i):avbase(i) + 129],
                                        PT[:, i * P:(i + 1) * P],
                                        VH0[:, kt * 129:(kt + 1) * 129],
                                        start=False, stop=(kt == last_kt),
                                        skip_group_check=True),
                                    waits=w_)
                    av_tok[ki % 2] = mm2
                    ctx.bank_w["av"] = mm2
                last_r = None
                for i in range(QGT):
                    t_rc2 = pg.op("vector",
                                  lambda E, i=i: E.reciprocal(
                                      rcp[:, 0:1],
                                      ctx.pt["av"][:, avbase(i) + P:avbase(i) + 129]),
                                  waits=[ctx.bank_w["av"], pg.tok("s_dve")])
                    t_y = pg.op("vector",
                                lambda E, i=i: E.tensor_scalar_mul(
                                    ystg[:, 0:P],
                                    ctx.pt["av"][:, avbase(i):avbase(i) + P],
                                    rcp[:, 0:1]),
                                waits=[t_rc2, pg.tok("s_pe")])
                    last_r = ctx.transpose(
                        ystg[:, 0:P],
                        ystag[:, i * P:(i + 1) * P],
                        "sp", waits=[t_y],
                        evac_waits=[pg.tok("q_gpsimd_ys")])
                ctx.bank_r["av"] = last_r
                pg.dma("gpsimd",
                       yscr[h, qgi * QGT:(qgi + 1) * QGT].rearrange("t p n -> p t n"),
                       ystag[:, 0:QGT * P].rearrange("p (t n) -> p t n", n=P),
                       waits=[last_r], stream="ys")
    pg.barrier()

    for oc in range(D // P):
        b = wld[oc % 2]
        t_l = pg.dma("sync", b[:, 0:D], Wo[oc * P:(oc + 1) * P, :],
                     waits=[pg.tok("s_dve")], stream=f"w{oc % 2}")
        t_c = pg.op("vector",
                    lambda E, b=b: E.tensor_scalar_mul(wldb[:, 0:D], b[:, 0:D],
                                                       scbc[:, 3:4]),
                    waits=[t_l, pg.tok("s_dve")])
        for dc in range(DCH):
            ctx.transpose(wldb[:, dc * P:(dc + 1) * P],
                          wo_big[:, dc * D + oc * P: dc * D + (oc + 1) * P],
                          "sp", waits=[t_c], evac_waits=[pg.tok("s_pe")])
    pg.barrier()
    out_toks = []
    OH = D // 2
    ybufs = [ctx.view(AR2, 8192 + 16 * D, NH * P, BF16)] * 2
    for tt in range(TT):
        yb = ybufs[tt % 2]
        t_yl = pg.dma("sync", yb[:].rearrange("p (h n) -> p h n", n=P),
                      yscr[:, tt].rearrange("h p n -> p h n"),
                      waits=[pg.tok("s_pe")], stream=f"yl{tt % 2}")
        yfm = lambda dc, yb=yb: yb[:, dc * P:(dc + 1) * P]
        mm = None
        for dc in range(DCH):
            mm = pg.op("tensor",
                       lambda E, dc=dc, yfm=yfm, st=(dc == 0), sp_=(dc == DCH - 1):
                       E.matmul(ctx.pt["av"][0:R, 0:P],
                                lA["o"][:, dc * R:(dc + 1) * R],
                                yfm(dc), start=st, stop=sp_),
                       waits=[t_yl, ctx.bank_r["av"] if dc == 0 else None])
        ctx.bank_w["av"] = mm
        tev = pg.op("vector",
                    lambda E: E.tensor_copy(tlora[0:R, 0:P], ctx.pt["av"][0:R, 0:P]),
                    waits=[mm, pg.tok("s_act")])
        ctx.bank_r["av"] = tev
        NBO = ceil_div(OH, 512)
        OBW = min(OH, 512)
        for halfo, ps in enumerate(("ps0", "ps1")):
            mm = None
            for dc in range(DCH):
                for nb in range(NBO):
                    st = dc == 0
                    w_ = [t_yl]
                    if st and nb == 0:
                        w_.append(ctx.bank_r[ps])
                    mm = pg.op("tensor",
                               lambda E, dc=dc, nb=nb, halfo=halfo, ps=ps, st=st, yfm=yfm:
                               E.matmul(ctx.pt[ps][:, nb * OBW:(nb + 1) * OBW],
                                        yfm(dc),
                                        wo_big[:, dc * D + halfo * OH + nb * OBW:
                                               dc * D + halfo * OH + (nb + 1) * OBW],
                                        start=st, stop=False),
                               waits=w_)
            for nb in range(NBO):
                mm = pg.op("tensor",
                           lambda E, nb=nb, halfo=halfo, ps=ps:
                           E.matmul(ctx.pt[ps][:, nb * OBW:(nb + 1) * OBW],
                                    tlora[0:R, 0:P],
                                    lB["o"][0:R, halfo * OH + nb * OBW:
                                            halfo * OH + (nb + 1) * OBW],
                                    start=False, stop=True),
                           waits=[tev])
            ctx.bank_w[ps] = mm
        bx = xres[tt % 2]
        t_rx = pg.dma("sync", bx[:, 0:D], xb[tt * P:(tt + 1) * P, :],
                      waits=[pg.tok("s_dve")], stream=f"r{tt % 2}")
        t_s = None
        for halfo, ps in enumerate(("ps0", "ps1")):
            t_s = pg.op("vector",
                        lambda E, ps=ps, halfo=halfo, bx=bx: E.tensor_add(
                            sumb[:, halfo * OH:(halfo + 1) * OH],
                            ctx.pt[ps][:, 0:OH],
                            bx[:, halfo * OH:(halfo + 1) * OH]),
                        waits=[ctx.bank_w[ps], t_rx, pg.tok("s_dve")])
            ctx.bank_r[ps] = t_s
        ob = obuf[tt % 2]
        t_o = emit_layernorm(ctx, sumb, D, stat, ob[:, 0:D],
                             waits=[t_s, out_toks[-1] if out_toks else None],
                             sq_scratch=lnsq)
        t_out = pg.dma("sync", out[tt * P:(tt + 1) * P, :], ob[:, 0:D],
                       waits=[t_o], stream=f"o{tt % 2}")
        out_toks.append(t_out)

    if standalone:
        return ctx.finish(pg.all_dma_toks())
    return None





def build_l1_attn(D=2048, NH=16, HD=128, KVC=512, QC=1024, S=8192, T=2048,
                  ctx=None, ext=None, split=False):
    DCH = D // P
    KT = S // P
    TT = T // P
    LTC = KVC // P
    QCC = QC // P
    QG = min(1024, T)
    NQG = T // QG
    QGT = QG // P
    HALF1 = HD // 2
    SC = 1.0 / float(np.sqrt(2 * HD))
    NOB = ceil_div(D, 512)

    standalone = ctx is None
    if standalone:
        ctx = Ctx(psum_spec=[("ps0", [P, 1024]), ("ps1", [P, 1024]),
                             ("av", [P, 1536]), ("sp", [P, 512])])
    pg = ctx.pg
    EX = (lambda n, sh: ctx.inp(n, sh)) if ext is None else (lambda n, sh: ext[n])

    if split:
        xloc = ext["xloc"]
        xg = ext["xg"]
        xg_bf = True
        cos1q = ext["cos1q"]
        sin1q = ext["sin1q"]
        cc_tok = ext.get("cc_tok")
    else:
        xb = EX("xb", (S, D))
        xloc = xb
        xg = xb
        xg_bf = False
        cos1q = None
    Wkvd = EX("w_kv_down", (D, KVC))
    Wupk = EX("w_kv_up_k", (KVC, D))
    Wupv = EX("w_kv_up_v", (KVC, D))
    Wqd = EX("w_q_down", (D, QC))
    Wqu = EX("w_q_up", (QC, D))
    Wkr = EX("w_k_rope", (D, HD))
    Wqr = EX("w_q_rope", (D, HD))
    Wout = EX("w_out", (D, D))
    cos1 = EX("cos1", (S, HALF1))
    sin1 = EX("sin1", (S, HALF1))
    out = ctx.outp("out", (T, D)) if ext is None else ext["out"]
    yscr = ctx.scratch("yscr1", [NH, TT, P, P], BF16)

    if standalone:
        ctx.setup_consts()

    ARXB = max(2 * DCH * QC + 2 * DCH * P + 2 * QC + 128,
               2 * DCH * KVC + 2 * DCH * HD + 2 * DCH * 512 + 2048 + 4 * HALF1 + 64,
               2 * KT * P + 2 * KT * 129 + 2 * (2 * LTC + QCC) * HD
               + 4 * QG + 2 * P + 2 * QGT * P + 256,
               16 * D + 2 * NH * P + 256)
    ARX = ctx.arena("ARX1", ARXB)
    ARY = ctx.arena("ARY1", 2 * max(LTC * S, DCH * D))
    lat_fm = ctx.view(ARY, 0, LTC * S, BF16)
    wo_sb = ctx.view(ARY, 0, DCH * D, BF16)
    wqd_sb = ctx.view(ARX, 0, DCH * QC, BF16)
    xstg_c1 = ctx.view(ARX, 2 * DCH * QC, DCH * P, BF16)
    wkvd_sb = ctx.view(ARX, 0, DCH * KVC, BF16)
    wkr_sb = ctx.view(ARX, 2 * DCH * KVC, DCH * HD, BF16)
    xstg_c2 = ctx.view(ARX, 2 * DCH * (KVC + HD), DCH * 512, BF16)
    KH0 = ctx.view(ARX, 0, KT * P, BF16)
    VH0 = ctx.view(ARX, 2 * KT * P, KT * 129, BF16)
    upk_h = ctx.view(ARX, 2 * (KT * P + KT * 129), LTC * HD, BF16)
    upv_h = ctx.view(ARX, 2 * (KT * P + KT * 129) + 2 * LTC * HD, LTC * HD, BF16)
    wqu_h = ctx.view(ARX, 2 * (KT * P + KT * 129) + 4 * LTC * HD, QCC * HD, BF16)
    _c3t = 2 * (KT * P + KT * 129) + 4 * LTC * HD + 2 * QCC * HD
    xres = ctx.view(ARX, 0, D, F32)
    sumb = ctx.view(ARX, 4 * D, D, F32)
    lnsq = ctx.view(ARX, 8 * D, D, F32)
    obuf = ctx.view(ARX, 12 * D, D, F32)
    Y_t = ctx.view(ARX, 16 * D, NH * P, BF16)

    t1fm = ctx.sb("t1fm1", [P, QCC * T], BF16)
    kr_fm = ctx.sb("kr_fm1", [P, S], BF16)
    qr_fm = ctx.sb("qr_fm1", [P, T], BF16)
    qch = ctx.sb("qch1", [P, T], BF16)
    cos_sb = ctx.sb("cos1_sb", [P, KT * HALF1], BF16)
    sin_sb = ctx.sb("sin1_sb", [P, KT * HALF1], BF16)
    cosq_sb, sinq_sb = cos_sb, sin_sb
    wqr_sb = ctx.sb("wqr_sb1", [P, DCH * HD], BF16)
    xrow = [ctx.sb(f"xrow1_{i}", [P, D], BF16) for i in range(2)]
    t1tok = ctx.view(ARX, ((2 * DCH * QC + 2 * DCH * P + 63) // 64) * 64, QC, BF16)
    qrtok = ctx.sb("qrtok1", [P, HD], F32)
    qrrot = ctx.sb("qrrot1", [P, HD], F32)
    krtok = qrtok
    krrot = qrrot
    krstg = ctx.view(ARX, 2 * DCH * (KVC + HD) + 2 * DCH * 512, 512, F32)
    _rop_off = ((max(2 * DCH * QC + 2 * DCH * P,
                     2 * DCH * (KVC + HD) + 2 * DCH * 512 + 2048) + 63) // 64) * 64
    ropes = ctx.view(ARX, _rop_off, HALF1, F32)
    sqrow = ctx.sb("sqrow1", [P, 512], BF16)
    sca = ctx.sb("sca1", [1, 16], F32)
    mb_h = ctx.sb("mb_h1", [P, 1], F32)
    PTb1 = [ctx.view(ARX, _c3t + i * 2 * QG, QG, BF16) for i in range(2)]
    ystg = ctx.view(ARX, _c3t + 4 * QG, P, BF16)
    ystag = ctx.view(ARX, _c3t + 4 * QG + 2 * P, QGT * P, BF16)
    rcp = ctx.sb("rcp1", [P, 1], F32)
    stat = ctx.sb("stat1", [P, 8], F32)

    pg.dma("gpsimd", wqd_sb.rearrange("p (c n) -> p c n", n=QC),
           Wqd.ap().rearrange("(c p) n -> p c n", p=P), stream="w0")
    pg.dma("gpsimd", wqr_sb.rearrange("p (c n) -> p c n", n=HD),
           Wqr.ap().rearrange("(c p) n -> p c n", p=P), stream="w1")
    if split:
        pg.dma("gpsimd", cos_sb[:, 0:TT * HALF1].rearrange("p (t c) -> p t c", c=HALF1),
               cos1q.ap().rearrange("(t p) c -> p t c", p=P), stream="w2")
        pg.dma("gpsimd", sin_sb[:, 0:TT * HALF1].rearrange("p (t c) -> p t c", c=HALF1),
               sin1q.ap().rearrange("(t p) c -> p t c", p=P), stream="w3")
    else:
        pg.dma("gpsimd", cos_sb[:].rearrange("p (t c) -> p t c", c=HALF1),
               cos1.ap().rearrange("(t p) c -> p t c", p=P), stream="w2")
        pg.dma("gpsimd", sin_sb[:].rearrange("p (t c) -> p t c", c=HALF1),
               sin1.ap().rearrange("(t p) c -> p t c", p=P), stream="w3")
    pg.barrier()

    def rope1(src, dst, tl, t_in, tabs=None):
        cs_t, sn_t = tabs if tabs is not None else (cos_sb, sin_sb)
        s2 = src.rearrange("p (c two) -> p c two", two=2)
        d2 = dst.rearrange("p (c two) -> p c two", two=2)
        cs = cs_t[:, tl * HALF1:(tl + 1) * HALF1]
        sn = sn_t[:, tl * HALF1:(tl + 1) * HALF1]
        e2 = ropes[:, 0:HALF1]
        t = pg.op("vector", lambda E: E.tensor_tensor(
            d2[:, :, 0], s2[:, :, 0], cs, ALU.mult), waits=[t_in, pg.tok("s_dve")])
        t = pg.op("vector", lambda E: E.tensor_tensor(
            e2, s2[:, :, 1], sn, ALU.mult), waits=[t])
        t = pg.op("vector", lambda E: E.tensor_tensor(
            d2[:, :, 0], d2[:, :, 0], e2, ALU.subtract), waits=[t])
        t = pg.op("vector", lambda E: E.tensor_tensor(
            d2[:, :, 1], s2[:, :, 1], cs, ALU.mult), waits=[t])
        t = pg.op("vector", lambda E: E.tensor_tensor(
            e2, s2[:, :, 0], sn, ALU.mult), waits=[t])
        t = pg.op("vector", lambda E: E.tensor_tensor(
            d2[:, :, 1], d2[:, :, 1], e2, ALU.add), waits=[t])
        return t

    def max_norm_pass(src_bf, width, out_col, t_in, init):
        t = t_in
        if init:
            t = pg.op("vector", lambda E: E.memset(sca[0:1, out_col:out_col + 1], 0.0),
                      waits=[t_in, pg.tok("s_dve")])
        for off in range(0, width, 512):
            w = min(512, width - off)
            t_sq = pg.op("scalar",
                         lambda E, off=off, w=w: E.activation(
                             sqrow[:, 0:w], src_bf[:, off:off + w], AF.Square),
                         waits=[t, pg.tok("s_act"), pg.tok("s_dve")])
            t_mm = pg.op("tensor",
                         lambda E, w=w: E.matmul(ctx.pt["sp"][0:1, 0:w],
                                                 ctx.ones_col_b[:], sqrow[:, 0:w],
                                                 start=True, stop=True),
                         waits=[t_sq, ctx.bank_r["sp"]])
            ctx.bank_w["sp"] = t_mm
            t_mx = pg.op("vector",
                         lambda E, w=w: E.reduce_max(sca[0:1, 15:16],
                                                     ctx.pt["sp"][0:1, 0:w], axis=AX.X),
                         waits=[t_mm])
            ctx.bank_r["sp"] = t_mx
            t = pg.op("vector",
                      lambda E: E.tensor_max(sca[0:1, out_col:out_col + 1],
                                             sca[0:1, out_col:out_col + 1],
                                             sca[0:1, 15:16]),
                      waits=[t_mx])
        return t

    for tt in range(TT):
        xr = xrow[tt % 2]
        t_lx = pg.dma("gpsimd", xr[:, 0:D], xloc[tt * P:(tt + 1) * P, :],
                      waits=[pg.tok("s_pe")], stream=f"x{tt % 2}")
        for dc in range(DCH):
            ctx.transpose(xr[:, dc * P:(dc + 1) * P], xstg_c1[:, dc * P:(dc + 1) * P],
                          "sp", waits=[t_lx], evac_waits=[pg.tok("s_pe")])
        t_x = pg.tok("s_dve")
        NBQ = ceil_div(QC, 512)
        QBW = min(QC, 512)
        mm = None
        for dc in range(DCH):
            for nb in range(NBQ):
                st = dc == 0
                w_ = [t_x]
                if st and nb == 0:
                    w_.append(ctx.bank_r["ps0"])
                mm = pg.op("tensor",
                           lambda E, dc=dc, nb=nb, st=st:
                           E.matmul(ctx.pt["ps0"][:, nb * QBW:(nb + 1) * QBW],
                                    xstg_c1[:, dc * P:(dc + 1) * P],
                                    wqd_sb[:, dc * QC + nb * QBW: dc * QC + (nb + 1) * QBW],
                                    start=st, stop=(dc == DCH - 1)),
                           waits=w_)
        ctx.bank_w["ps0"] = mm
        t_ev = pg.op("vector", lambda E: E.tensor_copy(t1tok[:, 0:QC],
                                                       ctx.pt["ps0"][:, 0:QC]),
                     waits=[mm, pg.tok("s_dve")])
        ctx.bank_r["ps0"] = t_ev
        for c in range(QCC):
            ctx.transpose(t1tok[:, c * P:(c + 1) * P],
                          t1fm[:, c * T + tt * P: c * T + (tt + 1) * P],
                          "sp", waits=[t_ev], evac_waits=[pg.tok("s_pe")])
        mm = None
        for dc in range(DCH):
            st = dc == 0
            w_ = [t_x]
            if st:
                w_.append(ctx.bank_r["av"])
            mm = pg.op("tensor",
                       lambda E, dc=dc, st=st:
                       E.matmul(ctx.pt["av"][:, 0:HD], xstg_c1[:, dc * P:(dc + 1) * P],
                                wqr_sb[:, dc * HD:(dc + 1) * HD],
                                start=st, stop=(dc == DCH - 1)),
                       waits=w_)
        ctx.bank_w["av"] = mm
        t_qr = pg.op("vector", lambda E: E.tensor_copy(qrtok[:, 0:HD],
                                                       ctx.pt["av"][:, 0:HD]),
                     waits=[mm, pg.tok("s_dve")])
        ctx.bank_r["av"] = t_qr
        t_ro = rope1(qrtok[:, 0:HD], qrrot[:, 0:HD], tt, t_qr,
                     tabs=(cosq_sb, sinq_sb))
        ctx.transpose(qrrot[:, 0:HD], qr_fm[:, tt * P:(tt + 1) * P], "sp",
                      waits=[t_ro])
    t_q2m = max_norm_pass(qr_fm[:], T, 0, pg.tok("s_dve"), True)
    pg.barrier()

    pg.dma("gpsimd", wkvd_sb.rearrange("p (c n) -> p c n", n=KVC),
           Wkvd.ap().rearrange("(c p) n -> p c n", p=P), stream="w0")
    pg.dma("gpsimd", wkr_sb.rearrange("p (c n) -> p c n", n=HD),
           Wkr.ap().rearrange("(c p) n -> p c n", p=P), stream="w1")
    if split:
        pg.dma("gpsimd", cos_sb[:].rearrange("p (t c) -> p t c", c=HALF1),
               cos1.ap().rearrange("(t p) c -> p t c", p=P), stream="w2")
        pg.dma("gpsimd", sin_sb[:].rearrange("p (t c) -> p t c", c=HALF1),
               sin1.ap().rearrange("(t p) c -> p t c", p=P), stream="w3")
    pg.barrier()
    for blk in range(S // 512):
        for j in range(4):
            kt = blk * 4 + j
            xr = xrow[j % 2]
            t_lx = pg.dma("sync" if xg_bf else "gpsimd",
                          xr[:, 0:D], xg[kt * P:(kt + 1) * P, :],
                          waits=[pg.tok("s_pe"),
                                 cc_tok if split else None],
                          stream=f"x{j % 2}")
            for dc in range(DCH):
                ctx.transpose(xr[:, dc * P:(dc + 1) * P],
                              xstg_c2[:, dc * 512 + j * P: dc * 512 + (j + 1) * P],
                              "sp", waits=[t_lx], evac_waits=[pg.tok("s_pe")])
        t_x = pg.tok("s_dve")
        for lt in range(LTC):
            psn, off = [("ps0", 0), ("ps0", 512), ("ps1", 0), ("ps1", 512)][lt % 4]
            mm = None
            for dc in range(DCH):
                st = dc == 0
                w_ = [t_x]
                if st:
                    w_.append(ctx.bank_r[psn])
                mm = pg.op("tensor",
                           lambda E, dc=dc, lt=lt, psn=psn, off=off, st=st:
                           E.matmul(ctx.pt[psn][:, off:off + 512],
                                    wkvd_sb[:, dc * KVC + lt * P: dc * KVC + (lt + 1) * P],
                                    xstg_c2[:, dc * 512:(dc + 1) * 512],
                                    start=st, stop=(dc == DCH - 1)),
                           waits=w_)
            ctx.bank_w[psn] = mm
            t_ev = pg.op("vector",
                         lambda E, lt=lt, blk=blk, psn=psn, off=off: E.tensor_copy(
                             lat_fm[:, lt * S + blk * 512:lt * S + (blk + 1) * 512],
                             ctx.pt[psn][:, off:off + 512]),
                         waits=[mm, pg.tok("s_dve")])
            ctx.bank_r[psn] = t_ev
        mm = None
        for dc in range(DCH):
            st = dc == 0
            w_ = [t_x]
            if st:
                w_.append(ctx.bank_r["av"])
            mm = pg.op("tensor",
                       lambda E, dc=dc, st=st:
                       E.matmul(ctx.pt["av"][:, 0:512],
                                wkr_sb[:, dc * HD:(dc + 1) * HD],
                                xstg_c2[:, dc * 512:(dc + 1) * 512],
                                start=st, stop=(dc == DCH - 1)),
                       waits=w_)
        ctx.bank_w["av"] = mm
        t_evk = pg.op("vector",
                      lambda E: E.tensor_copy(krstg[:, 0:512], ctx.pt["av"][:, 0:512]),
                      waits=[mm, pg.tok("s_dve"), pg.tok("s_pe")])
        ctx.bank_r["av"] = t_evk
        for j in range(4):
            kt = blk * 4 + j
            t_t = ctx.transpose(krstg[:, j * P:(j + 1) * P], krtok[:, 0:P], "sp",
                                waits=[t_evk], evac_waits=[pg.tok("s_pe")])
            t_ro = rope1(krtok[:, 0:P], krrot[:, 0:P], kt, t_t)
            ctx.transpose(krrot[:, 0:P], kr_fm[:, kt * P:(kt + 1) * P], "sp",
                          waits=[t_ro], evac_waits=[pg.tok("s_pe")])
    t_k2m = max_norm_pass(kr_fm[:], S, 1, pg.tok("s_dve"), True)
    pg.barrier()

    avbase = lambda i: (i // 3) * 512 + (i % 3) * 129
    av_tok1 = [None, None]
    for h in range(NH):
        t_wq = pg.dma("gpsimd", wqu_h.rearrange("p (c n) -> p c n", n=HD),
                      Wqu[:, h * HD:(h + 1) * HD].rearrange("(c p) n -> p c n", p=P),
                      waits=[pg.tok("s_pe")], stream="wh0")
        t_uk = pg.dma("gpsimd", upk_h.rearrange("p (c n) -> p c n", n=HD),
                      Wupk[:, h * HD:(h + 1) * HD].rearrange("(c p) n -> p c n", p=P),
                      waits=[pg.tok("s_pe")], stream="wh1")
        t_uv = pg.dma("gpsimd", upv_h.rearrange("p (c n) -> p c n", n=HD),
                      Wupv[:, h * HD:(h + 1) * HD].rearrange("(c p) n -> p c n", p=P),
                      waits=[pg.tok("s_pe")], stream="wh2")
        QB2 = min(512, T)
        t_qn = pg.op("vector", lambda E: E.memset(sca[0:1, 2:3], 0.0),
                     waits=[pg.tok("s_dve")])
        for qb in range(ceil_div(T, 512)):
            psn = ["ps0", "ps1"][qb % 2]
            mm = None
            for c in range(QCC):
                st = c == 0
                w_ = [t_wq]
                if st:
                    w_.append(ctx.bank_r[psn])
                mm = pg.op("tensor",
                           lambda E, c=c, qb=qb, psn=psn, st=st:
                           E.matmul(ctx.pt[psn][:, 0:QB2],
                                    wqu_h[:, c * HD:(c + 1) * HD],
                                    t1fm[:, c * T + qb * QB2: c * T + (qb + 1) * QB2],
                                    start=st, stop=(c == QCC - 1)),
                           waits=w_)
            ctx.bank_w[psn] = mm
            t_ev = pg.op("vector",
                         lambda E, qb=qb, psn=psn: E.tensor_copy(
                             qch[:, qb * QB2:(qb + 1) * QB2], ctx.pt[psn][:, 0:QB2]),
                         waits=[mm, pg.tok("s_dve")])
            ctx.bank_r[psn] = t_ev
            t_qn = max_norm_pass(qch[:, qb * QB2:(qb + 1) * QB2], QB2, 2, t_ev, False)
        t_kn = pg.op("vector", lambda E: E.memset(sca[0:1, 3:4], 0.0),
                     waits=[pg.tok("s_dve")])
        for sb_ in range(S // 512):
            psn = ["ps0", "ps1"][sb_ % 2]
            mm = None
            for lt in range(LTC):
                st = lt == 0
                w_ = [t_uk]
                if st:
                    w_.append(ctx.bank_r[psn])
                mm = pg.op("tensor",
                           lambda E, lt=lt, sb_=sb_, psn=psn, st=st:
                           E.matmul(ctx.pt[psn][:, 0:512],
                                    upk_h[:, lt * HD:(lt + 1) * HD],
                                    lat_fm[:, lt * S + sb_ * 512: lt * S + (sb_ + 1) * 512],
                                    start=st, stop=(lt == LTC - 1)),
                           waits=w_)
            ctx.bank_w[psn] = mm
            t_ev = pg.op("vector",
                         lambda E, sb_=sb_, psn=psn: E.tensor_copy(
                             KH0[:, sb_ * 512:(sb_ + 1) * 512], ctx.pt[psn][:, 0:512]),
                         waits=[mm, pg.tok("s_dve")])
            ctx.bank_r[psn] = t_ev
            t_kn = max_norm_pass(KH0[:, sb_ * 512:(sb_ + 1) * 512], 512, 3, t_ev, False)
        t_m = pg.op("vector", lambda E: E.tensor_add(sca[0:1, 4:5], sca[0:1, 2:3],
                                                     sca[0:1, 0:1]),
                    waits=[t_qn, t_q2m])
        t_m = pg.op("vector", lambda E: E.tensor_add(sca[0:1, 5:6], sca[0:1, 3:4],
                                                     sca[0:1, 1:2]),
                    waits=[t_kn, t_k2m, t_m])
        t_m = pg.op("scalar", lambda E: E.sqrt(sca[0:1, 6:7], sca[0:1, 4:5]),
                    waits=[t_m, pg.tok("s_act")])
        t_m = pg.op("scalar", lambda E: E.sqrt(sca[0:1, 7:8], sca[0:1, 5:6]),
                    waits=[t_m])
        t_m = pg.op("vector", lambda E: E.tensor_mul(sca[0:1, 8:9], sca[0:1, 6:7],
                                                     sca[0:1, 7:8]),
                    waits=[t_m, pg.tok("s_dve")])
        t_m = pg.op("vector", lambda E: E.tensor_scalar_mul(sca[0:1, 9:10],
                                                            sca[0:1, 8:9], -SC),
                    waits=[t_m])
        t_mb = pg.op("tensor", lambda E: E.matmul(ctx.pt["sp"][:, 0:1],
                                                  ctx.ones_row_f[:], sca[0:1, 9:10],
                                                  start=True, stop=True),
                     waits=[t_m, ctx.bank_r["sp"]])
        ctx.bank_w["sp"] = t_mb
        t_mb2 = pg.op("vector", lambda E: E.tensor_copy(mb_h[:, 0:1],
                                                        ctx.pt["sp"][:, 0:1]),
                      waits=[t_mb])
        ctx.bank_r["sp"] = t_mb2
        t_vi = pg.op("gpsimd", lambda E: E.memset(VH0, 1.0),
                     waits=[pg.tok("s_pe"), pg.tok("q_gpsimd_ys")])
        vb_tok = [None, None]
        for kt in range(KT):
            po = [0, 512][kt % 2]
            mm = None
            for lt in range(LTC):
                st = lt == 0
                w_ = [t_uv, t_vi]
                if st:
                    w_.append(vb_tok[kt % 2])
                    if kt < 2:
                        w_.append(ctx.bank_r["av"])
                mm = pg.op("tensor",
                           lambda E, lt=lt, kt=kt, st=st, po=po:
                           E.matmul(ctx.pt["av"][:, po:po + HD],
                                    lat_fm[:, lt * S + kt * P: lt * S + (kt + 1) * P],
                                    upv_h[:, lt * HD:(lt + 1) * HD],
                                    start=st, stop=(lt == LTC - 1)),
                           waits=w_)
            ctx.bank_w["av"] = mm
            t_ev = pg.op("vector",
                         lambda E, kt=kt, po=po: E.tensor_copy(
                             VH0[:, kt * 129:kt * 129 + P], ctx.pt["av"][:, po:po + HD]),
                         waits=[mm, pg.tok("s_dve")])
            vb_tok[kt % 2] = t_ev
            ctx.bank_r["av"] = t_ev
        for qgi in range(NQG):
            nav_banks = ceil_div(avbase(QGT - 1) + 129, 512)
            mmz = None
            for bkk in range(nav_banks):
                mmz = pg.op("tensor",
                            lambda E, bkk=bkk: E.matmul(
                                ctx.pt["av"][:, bkk * 512:(bkk + 1) * 512],
                                ctx.ident[:], ctx.zer_b[:, 0:512],
                                start=True, stop=True),
                            waits=[ctx.bank_r["av"] if bkk == 0 else None])
            ctx.bank_w["av"] = mmz
            psn = ["ps0", "ps1"]
            for kt in range(KT):
                ps = psn[kt % 2]
                PT = PTb1[kt % 2]
                mm = None
                for nb in range(ceil_div(QG, 512)):
                    NBW = min(512, QG)
                    w_ = [pg.tok("s_dve")]
                    if nb == 0:
                        w_.append(ctx.bank_r[ps])
                    mm = pg.op("tensor",
                               lambda E, ps=ps, kt=kt, nb=nb, NBW=NBW, qgi=qgi:
                               E.matmul(ctx.pt[ps][:, nb * NBW:(nb + 1) * NBW],
                                        KH0[:, kt * P:(kt + 1) * P],
                                        qch[:, qgi * QG + nb * NBW:
                                            qgi * QG + (nb + 1) * NBW],
                                        start=True, stop=False),
                               waits=w_)
                    mm = pg.op("tensor",
                               lambda E, ps=ps, kt=kt, nb=nb, NBW=NBW, qgi=qgi:
                               E.matmul(ctx.pt[ps][:, nb * NBW:(nb + 1) * NBW],
                                        kr_fm[:, kt * P:(kt + 1) * P],
                                        qr_fm[:, qgi * QG + nb * NBW:
                                              qgi * QG + (nb + 1) * NBW],
                                        start=False, stop=True),
                               waits=[])
                ctx.bank_w[ps] = mm
                t_e = pg.op("scalar",
                            lambda E, ps=ps, PT=PT: E.activation(
                                PT[:, 0:QG], ctx.pt[ps][:, 0:QG], AF.Exp,
                                bias=mb_h[:, 0:1], scale=SC),
                            waits=[mm, t_mb2, av_tok1[kt % 2], pg.tok("s_act")])
                ctx.bank_r[ps] = t_e
                for i in range(QGT):
                    mm2 = pg.op("tensor",
                                lambda E, i=i, kt=kt, PT=PT: E.matmul(
                                    ctx.pt["av"][:, avbase(i):avbase(i) + 129],
                                    PT[:, i * P:(i + 1) * P],
                                    VH0[:, kt * 129:(kt + 1) * 129],
                                    start=False, stop=(kt == KT - 1),
                                    skip_group_check=True),
                                waits=[t_e])
                av_tok1[kt % 2] = mm2
                ctx.bank_w["av"] = mm2
            last_r = None
            for i in range(QGT):
                t_rc2 = pg.op("vector",
                              lambda E, i=i: E.reciprocal(
                                  rcp[:, 0:1],
                                  ctx.pt["av"][:, avbase(i) + P:avbase(i) + 129]),
                              waits=[ctx.bank_w["av"], pg.tok("s_dve")])
                t_y = pg.op("vector",
                            lambda E, i=i: E.tensor_scalar_mul(
                                ystg[:, 0:P],
                                ctx.pt["av"][:, avbase(i):avbase(i) + P],
                                rcp[:, 0:1]),
                            waits=[t_rc2, pg.tok("s_pe")])
                last_r = ctx.transpose(
                    ystg[:, 0:P], ystag[:, i * P:(i + 1) * P],
                    "sp", waits=[t_y], evac_waits=[pg.tok("q_gpsimd_ys")])
            ctx.bank_r["av"] = last_r
            pg.dma("gpsimd",
                   yscr[h, qgi * QGT:(qgi + 1) * QGT].rearrange("t p n -> p t n"),
                   ystag[:, 0:QGT * P].rearrange("p (t n) -> p t n", n=P),
                   waits=[last_r], stream="ys")
    pg.barrier()

    t_wo = pg.dma("gpsimd", wo_sb.rearrange("p (c n) -> p c n", n=D),
                  Wout.ap().rearrange("(c p) n -> p c n", p=P), stream="wo")
    out_toks = []
    for tt in range(TT):
        t_yl = pg.dma("sync", Y_t.rearrange("p (h n) -> p h n", n=P),
                      yscr[:, tt].rearrange("h p n -> p h n"),
                      waits=[pg.tok("s_pe")], stream="yl")
        mm = None
        OBW = min(512, D)
        for dc in range(DCH):
            for nb in range(NOB):
                psn, off = ("ps0", nb * OBW) if nb * OBW < 1024 else ("ps1", nb * OBW - 1024)
                st = dc == 0
                w_ = [t_yl, t_wo]
                if st:
                    w_.append(ctx.bank_r[psn])
                mm = pg.op("tensor",
                           lambda E, dc=dc, nb=nb, psn=psn, off=off, st=st, OBW=OBW:
                           E.matmul(ctx.pt[psn][:, off:off + OBW],
                                    Y_t[:, dc * P:(dc + 1) * P],
                                    wo_sb[:, dc * D + nb * OBW: dc * D + (nb + 1) * OBW],
                                    start=st, stop=(dc == DCH - 1)),
                           waits=w_)
        ctx.bank_w["ps0"] = mm
        if D > 1024:
            ctx.bank_w["ps1"] = mm
        t_rx = pg.dma("sync", xres, xloc[tt * P:(tt + 1) * P, :],
                      waits=[pg.tok("s_dve")], stream="r0")
        t_s = None
        HW = min(D, 1024)
        for halfo in range(ceil_div(D, 1024)):
            psn = ("ps0", "ps1")[halfo]
            t_s = pg.op("vector",
                        lambda E, psn=psn, halfo=halfo: E.tensor_add(
                            sumb[:, halfo * HW:(halfo + 1) * HW],
                            ctx.pt[psn][:, 0:HW],
                            xres[:, halfo * HW:(halfo + 1) * HW]),
                        waits=[mm, t_rx, pg.tok("s_dve")])
            ctx.bank_r[psn] = t_s
        t_o = emit_layernorm(ctx, sumb, D, stat, obuf[:, 0:D],
                             waits=[t_s, out_toks[-1] if out_toks else None],
                             sq_scratch=lnsq)
        t_out = pg.dma("sync", out[tt * P:(tt + 1) * P, :], obuf[:, 0:D],
                       waits=[t_o], stream=f"o{tt % 2}")
        out_toks.append(t_out)

    if standalone:
        return ctx.finish(pg.all_dma_toks())
    return None





def build_l1_ffn(D=2048, NE=32, FE=768, TOPK=6, FSH=1536, T=2048, ctx=None, ext=None):
    DCH = D // P
    TT = T // P
    TOKH = min(1024, T)
    NH_ = T // TOKH
    HT = TOKH // P
    FEC = FE // P
    FSC = FSH // P
    NB = ceil_div(D, 512)
    DBW = min(512, D)
    TB = min(512, TOKH)
    NTB = TOKH // TB

    standalone = ctx is None
    if standalone:
        ctx = Ctx()
    pg = ctx.pg
    EX = (lambda n, sh: ctx.inp(n, sh)) if ext is None else (lambda n, sh: ext[n])

    xs = EX("xslab", (T, D))
    gw = EX("gate_w", (D, NE))
    wg = EX("wg", (NE, D, FE))
    wu = EX("wu", (NE, D, FE))
    wd = EX("wd", (NE, FE, D))
    sg = EX("sg", (D, FSH))
    su = EX("su", (D, FSH))
    sd = EX("sd", (FSH, D))
    out = ctx.outp("out", (T, D)) if ext is None else ext["out"]

    if standalone:
        ctx.setup_consts()

    acc = ctx.sb("acc", [P, HT * D], F32)
    x_fm = ctx.sb("x_fm", [P, DCH * TOKH], BF16)
    ARW = ctx.arena("ARW", max(2 * (DCH * FE) * 3,
                               2 * (FSC * TOKH) + 4 * D * 2 + 2 * DCH * P * 2,
                               16 * D) + 256)
    wgbuf = ctx.view(ARW, 0, DCH * FE, BF16)
    wubuf = ctx.view(ARW, 2 * DCH * FE, DCH * FE, BF16)
    wdbuf = ctx.view(ARW, 4 * DCH * FE, FEC * D, BF16)
    actsh = ctx.view(ARW, 0, FSC * TOKH, BF16)
    shw = [ctx.view(ARW, 2 * FSC * TOKH + i * 2 * DCH * P, DCH * P, BF16)
           for i in range(2)]
    sdrow = [ctx.view(ARW, 2 * FSC * TOKH + 4 * DCH * P + i * 2 * D, D, BF16)
             for i in range(2)]
    xres = ctx.view(ARW, 0, D, F32)
    sumb = ctx.view(ARW, 4 * D, D, F32)
    lnsq = ctx.view(ARW, 8 * D, D, F32)
    obuf = ctx.view(ARW, 12 * D, D, F32)
    act_e = ctx.sb("act_e", [P, FEC * TOKH], BF16)
    gwb = ctx.sb("gwb", [P, DCH * NE], BF16)
    comb = ctx.sb("comb", [P, HT * NE], F32)
    scores = ctx.sb("scores", [P, NE], F32)
    top8 = ctx.sb("top8", [P, 8], F32)
    msk = ctx.sb("msk", [P, NE], F32)
    dnm = ctx.sb("dnm", [P, 2], F32)
    xrow = [ctx.sb(f"xrow2_{i}", [P, D], BF16) for i in range(2)]
    _sg0 = ctx.sb("silu_g0", [P, TB], BF16)
    _sg1 = ctx.sb("silu_g1", [P, TB], BF16)
    silu_g = {"silu0": _sg0, "silu1": _sg1}
    stat = ctx.sb("stat2", [P, 8], F32)

    pg.dma("gpsimd", gwb[:].rearrange("p (c n) -> p c n", n=NE),
           gw.ap().rearrange("(c p) n -> p c n", p=P), stream="w0")
    pg.barrier()

    out_toks = []
    for hf in range(NH_):
        for tt in range(HT):
            g_tt = hf * HT + tt
            xr = xrow[tt % 2]
            t_lx = pg.dma("gpsimd", xr[:, 0:D], xs[g_tt * P:(g_tt + 1) * P, :],
                          waits=[pg.tok("s_pe")], stream=f"x{tt % 2}")
            for dc in range(DCH):
                ctx.transpose(xr[:, dc * P:(dc + 1) * P],
                              x_fm[:, dc * TOKH + tt * P: dc * TOKH + (tt + 1) * P],
                              "pb7", waits=[t_lx], evac_waits=[pg.tok("s_pe")])
        t_xfm = pg.tok("s_dve")
        t_ai = pg.op("gpsimd", lambda E: E.memset(acc[:], 0.0),
                     waits=[pg.tok("s_dve"), pg.tok("q_sync_o0"), pg.tok("q_sync_o1")])
        for tt in range(HT):
            mm = None
            for dc in range(DCH):
                st = dc == 0
                w_ = [t_xfm]
                if st:
                    w_.append(ctx.bank_r["pb6"])
                mm = pg.op("tensor",
                           lambda E, dc=dc, tt=tt, st=st:
                           E.matmul(ctx.pt["pb6"][:, 0:NE],
                                    x_fm[:, dc * TOKH + tt * P: dc * TOKH + (tt + 1) * P],
                                    gwb[:, dc * NE:(dc + 1) * NE],
                                    start=st, stop=(dc == DCH - 1)),
                           waits=w_)
            ctx.bank_w["pb6"] = mm
            t_sg = pg.op("scalar",
                         lambda E: E.activation(scores[:, 0:NE], ctx.pt["pb6"][:, 0:NE],
                                                AF.Sigmoid),
                         waits=[mm, pg.tok("s_dve"), pg.tok("s_act")])
            ctx.bank_r["pb6"] = t_sg
            t_t8 = pg.op("vector", lambda E: E.max(out=top8[:, 0:8], in_=scores[:, 0:NE]),
                         waits=[t_sg])
            t_mk = pg.op("vector",
                         lambda E: E.tensor_single_scalar(msk[:, 0:NE], scores[:, 0:NE],
                                                          top8[:, TOPK - 1:TOPK], ALU.is_ge),
                         waits=[t_t8])
            t_ws = pg.op("vector", lambda E: E.tensor_mul(msk[:, 0:NE], msk[:, 0:NE],
                                                          scores[:, 0:NE]),
                         waits=[t_mk])
            t_dn = pg.op("vector", lambda E: E.reduce_sum(dnm[:, 0:1], msk[:, 0:NE],
                                                          axis=AX.X),
                         waits=[t_ws])
            t_dn = pg.op("vector", lambda E: E.tensor_scalar_add(dnm[:, 0:1], dnm[:, 0:1],
                                                                 1e-20),
                         waits=[t_dn])
            t_dn = pg.op("vector", lambda E: E.reciprocal(dnm[:, 1:2], dnm[:, 0:1]),
                         waits=[t_dn])
            t_cb = pg.op("vector",
                         lambda E, tt=tt: E.tensor_scalar_mul(
                             comb[:, tt * NE:(tt + 1) * NE], msk[:, 0:NE], dnm[:, 1:2]),
                         waits=[t_dn])
        t_comb = pg.tok("s_dve")
        for ft in range(FSC):
            t_lg = pg.dma("gpsimd", shw[0].rearrange("p (c n) -> p c n", n=P),
                          sg[:, ft * P:(ft + 1) * P].rearrange("(c p) n -> p c n", p=P),
                          waits=[pg.tok("s_pe")], stream="sh0")
            t_lu = pg.dma("gpsimd", shw[1].rearrange("p (c n) -> p c n", n=P),
                          su[:, ft * P:(ft + 1) * P].rearrange("(c p) n -> p c n", p=P),
                          waits=[pg.tok("s_pe")], stream="sh1")
            for tb in range(NTB):
                mmg = mmu = None
                for dc in range(DCH):
                    st = dc == 0
                    wg_ = [t_lg, t_xfm]
                    wu_ = [t_lu]
                    if st:
                        wg_.append(ctx.bank_r["pb0"])
                        wu_.append(ctx.bank_r["pb1"])
                    mmg = pg.op("tensor",
                                lambda E, dc=dc, tb=tb, st=st:
                                E.matmul(ctx.pt["pb0"][:, 0:TB],
                                         shw[0][:, dc * P:(dc + 1) * P],
                                         x_fm[:, dc * TOKH + tb * TB: dc * TOKH + (tb + 1) * TB],
                                         start=st, stop=(dc == DCH - 1)),
                                waits=wg_)
                    mmu = pg.op("tensor",
                                lambda E, dc=dc, tb=tb, st=st:
                                E.matmul(ctx.pt["pb1"][:, 0:TB],
                                         shw[1][:, dc * P:(dc + 1) * P],
                                         x_fm[:, dc * TOKH + tb * TB: dc * TOKH + (tb + 1) * TB],
                                         start=st, stop=(dc == DCH - 1)),
                                waits=wu_)
                ctx.bank_w["pb0"] = mmg
                ctx.bank_w["pb1"] = mmu
                sgb = ["silu0", "silu1"][ft % 2]
                t_si = pg.op("scalar",
                             lambda E, sgb=sgb: E.activation(silu_g[sgb][:, 0:TB],
                                                             ctx.pt["pb0"][:, 0:TB],
                                                             AF.Silu),
                             waits=[mmg, pg.tok("s_act")])
                ctx.bank_r["pb0"] = t_si
                t_mu = pg.op("vector",
                             lambda E, ft=ft, tb=tb, sgb=sgb: E.tensor_tensor(
                                 actsh[:, ft * TOKH + tb * TB: ft * TOKH + (tb + 1) * TB],
                                 silu_g[sgb][:, 0:TB], ctx.pt["pb1"][:, 0:TB], ALU.mult),
                             waits=[t_si, mmu])
                ctx.bank_r["pb1"] = t_mu
        t_ash = pg.tok("s_dve")
        BLKT = min(256, TOKH)
        NSUB = BLKT // P
        for blk in range(TOKH // BLKT):
            mm_l = [[None] * NB for _ in range(NSUB)]
            for ft in range(FSC):
                sb_ = sdrow[ft % 2]
                t_ld = pg.dma("gpsimd", sb_, sd[ft * P:(ft + 1) * P, :],
                              waits=[pg.tok("s_pe")], stream=f"sd{ft % 2}")
                for sub in range(NSUB):
                    lt = blk * BLKT + sub * P
                    for nb in range(NB):
                        bank = f"pb{sub * NB + nb}"
                        st = ft == 0
                        w_ = [t_ld, t_ash]
                        if st:
                            w_.append(ctx.bank_r[bank])
                        mm_l[sub][nb] = pg.op(
                            "tensor",
                            lambda E, bank=bank, sb_=sb_, nb=nb, lt=lt, ft=ft, st=st:
                            E.matmul(ctx.pt[bank][:, 0:DBW],
                                     actsh[:, ft * TOKH + lt: ft * TOKH + lt + P],
                                     sb_[:, nb * DBW:(nb + 1) * DBW],
                                     start=st, stop=(ft == FSC - 1)),
                            waits=w_)
            for sub in range(NSUB):
                tile = blk * NSUB + sub
                t_a = None
                for nb in range(NB):
                    bank = f"pb{sub * NB + nb}"
                    t_a = pg.op("vector",
                                lambda E, bank=bank, tile=tile, nb=nb: E.tensor_add(
                                    acc[:, tile * D + nb * DBW: tile * D + (nb + 1) * DBW],
                                    ctx.pt[bank][:, 0:DBW],
                                    acc[:, tile * D + nb * DBW: tile * D + (nb + 1) * DBW]),
                                waits=[mm_l[sub][nb], t_ai, pg.tok("s_dve")])
                    ctx.bank_r[bank] = t_a
        pg.barrier()

        t_wg = pg.dma("gpsimd", wgbuf.rearrange("p (c n) -> p c n", n=FE),
                      wg[0].rearrange("(c p) n -> p c n", p=P), stream="we0")
        t_wu = pg.dma("gpsimd", wubuf.rearrange("p (c n) -> p c n", n=FE),
                      wu[0].rearrange("(c p) n -> p c n", p=P), stream="we1")
        t_wd = pg.dma("gpsimd", wdbuf.rearrange("p (c n) -> p c n", n=D),
                      wd[0].rearrange("(c p) n -> p c n", p=P), stream="we2")
        for e in range(NE):
            for ft in range(FEC):
                gb = ["pb0", "pb2"][ft % 2]
                ub = ["pb1", "pb3"][ft % 2]
                sg_ = ["silu0", "silu1"][ft % 2]
                for tb in range(NTB):
                    mmg = mmu = None
                    for dc in range(DCH):
                        st = dc == 0
                        wg_ = [t_wg]
                        wu_ = [t_wu]
                        if st:
                            wg_.append(ctx.bank_r[gb])
                            wu_.append(ctx.bank_r[ub])
                        mmg = pg.op("tensor",
                                    lambda E, dc=dc, ft=ft, tb=tb, st=st, gb=gb:
                                    E.matmul(ctx.pt[gb][:, 0:TB],
                                             wgbuf[:, dc * FE + ft * P: dc * FE + (ft + 1) * P],
                                             x_fm[:, dc * TOKH + tb * TB: dc * TOKH + (tb + 1) * TB],
                                             start=st, stop=(dc == DCH - 1)),
                                    waits=wg_)
                        mmu = pg.op("tensor",
                                    lambda E, dc=dc, ft=ft, tb=tb, st=st, ub=ub:
                                    E.matmul(ctx.pt[ub][:, 0:TB],
                                             wubuf[:, dc * FE + ft * P: dc * FE + (ft + 1) * P],
                                             x_fm[:, dc * TOKH + tb * TB: dc * TOKH + (tb + 1) * TB],
                                             start=st, stop=(dc == DCH - 1)),
                                    waits=wu_)
                    ctx.bank_w[gb] = mmg
                    ctx.bank_w[ub] = mmu
                    t_si = pg.op("scalar",
                                 lambda E, gb=gb, sgb=sg_: E.activation(
                                     silu_g[sgb][:, 0:TB],
                                     ctx.pt[gb][:, 0:TB], AF.Silu),
                                 waits=[mmg, pg.tok("s_act")])
                    ctx.bank_r[gb] = t_si
                    t_mu = pg.op("vector",
                                 lambda E, ft=ft, tb=tb, ub=ub, sgb=sg_: E.tensor_tensor(
                                     act_e[:, ft * TOKH + tb * TB: ft * TOKH + (tb + 1) * TB],
                                     silu_g[sgb][:, 0:TB], ctx.pt[ub][:, 0:TB], ALU.mult),
                                 waits=[t_si, mmu])
                    ctx.bank_r[ub] = t_mu
            t_act = pg.tok("s_dve")
            if e + 1 < NE:
                t_wg = pg.dma("gpsimd", wgbuf.rearrange("p (c n) -> p c n", n=FE),
                              wg[e + 1].rearrange("(c p) n -> p c n", p=P),
                              waits=[pg.tok("s_pe")], stream="we0")
                t_wu = pg.dma("gpsimd", wubuf.rearrange("p (c n) -> p c n", n=FE),
                              wu[e + 1].rearrange("(c p) n -> p c n", p=P),
                              waits=[pg.tok("s_pe")], stream="we1")
            for blk in range(TOKH // BLKT):
                mm_l = [[None] * NB for _ in range(NSUB)]
                for ft in range(FEC):
                    for sub in range(NSUB):
                        lt = blk * BLKT + sub * P
                        for nb in range(NB):
                            bank = f"pb{sub * NB + nb}"
                            st = ft == 0
                            w_ = [t_wd, t_act]
                            if st:
                                w_.append(ctx.bank_r[bank])
                            mm_l[sub][nb] = pg.op(
                                "tensor",
                                lambda E, bank=bank, ft=ft, nb=nb, lt=lt, st=st:
                                E.matmul(ctx.pt[bank][:, 0:DBW],
                                         act_e[:, ft * TOKH + lt: ft * TOKH + lt + P],
                                         wdbuf[:, ft * D + nb * DBW: ft * D + (nb + 1) * DBW],
                                         start=st, stop=(ft == FEC - 1)),
                                waits=w_)
                for sub in range(NSUB):
                    tile = blk * NSUB + sub
                    t_a = None
                    for nb in range(NB):
                        bank = f"pb{sub * NB + nb}"
                        t_a = pg.op("vector",
                                    lambda E, bank=bank, tile=tile, nb=nb, e=e:
                                    E.scalar_tensor_tensor(
                                        acc[:, tile * D + nb * DBW: tile * D + (nb + 1) * DBW],
                                        ctx.pt[bank][:, 0:DBW],
                                        comb[:, tile * NE + e: tile * NE + e + 1],
                                        acc[:, tile * D + nb * DBW: tile * D + (nb + 1) * DBW],
                                        ALU.mult, ALU.add),
                                    waits=[mm_l[sub][nb], t_comb, pg.tok("s_dve")])
                        ctx.bank_r[bank] = t_a
            if e + 1 < NE:
                t_wd = pg.dma("gpsimd", wdbuf.rearrange("p (c n) -> p c n", n=D),
                              wd[e + 1].rearrange("(c p) n -> p c n", p=P),
                              waits=[pg.tok("s_pe")], stream="we2")
        pg.barrier()

        for tt in range(HT):
            g_tt = hf * HT + tt
            t_rx = pg.dma("sync", xres, xs[g_tt * P:(g_tt + 1) * P, :],
                          waits=[pg.tok("s_dve")], stream="r0")
            t_s = pg.op("vector",
                        lambda E, tt=tt: E.tensor_add(sumb[:, 0:D],
                                                      acc[:, tt * D:(tt + 1) * D],
                                                      xres[:, 0:D]),
                        waits=[t_rx, pg.tok("s_dve")])
            t_o = emit_layernorm(ctx, sumb, D, stat, obuf[:, 0:D],
                                 waits=[t_s, out_toks[-1] if out_toks else None],
                                 sq_scratch=lnsq)
            t_out = pg.dma("sync", out[g_tt * P:(g_tt + 1) * P, :], obuf[:, 0:D],
                           waits=[t_o], stream=f"o{tt % 2}")
            out_toks.append(t_out)
        pg.barrier()

    if standalone:
        return ctx.finish(pg.all_dma_toks())
    return None





def build_fused(B=2, S=8192, D=2048, T=2048, NH0=16, NKV=8, R=32, HID=5120,
                NH1=16, KVC=512, QC=1024, NE=32, FE=768, TOPK=6, FSH=1536):
    ctx = Ctx(psum_spec="both")
    pg = ctx.pg
    KVD = NKV * 128
    QG0 = min(1024, T)

    ins = {}
    def inp(name, shape, dtype=F32):
        ins[name] = ctx.inp(name, shape, dtype)
        return ins[name]

    inp("xb", (S, D))
    for nm, sh in (("W_q", (D, D)), ("W_k", (KVD, D)), ("W_v", (KVD, D)),
                   ("W_o", (D, D)), ("A_q", (R, D)), ("B_q", (D, R)),
                   ("A_k", (R, D)), ("B_k", (KVD, R)), ("A_v", (R, D)),
                   ("B_v", (KVD, R)), ("A_o", (R, D)), ("B_o", (D, R)),
                   ("s_q", (1,)), ("s_k", (1,)), ("s_v", (1,)), ("s_o", (1,)),
                   ("q_gain", (NH0,)), ("cosk", (S, 32)), ("sink", (S, 32)),
                   ("kbias", (P, S // P)), ("bandm", (P, 2 * QG0 - P)),
                   ("w_fc", (HID, D)), ("w_proj", (D, HID)),
                   ("w_kv_down", (D, KVC)), ("w_kv_up_k", (KVC, D)),
                   ("w_kv_up_v", (KVC, D)), ("w_q_down", (D, QC)),
                   ("w_q_up", (QC, D)), ("w_k_rope", (D, 128)),
                   ("w_q_rope", (D, 128)), ("w_out", (D, D)),
                   ("cos1", (S, 64)), ("sin1", (S, 64)),
                   ("cos1q", (T, 64)), ("sin1q", (T, 64)),
                   ("gate_w", (D, NE)), ("wg", (NE, D, FE)), ("wu", (NE, D, FE)),
                   ("wd", (NE, FE, D)), ("sg", (D, FSH)), ("su", (D, FSH)),
                   ("sd", (FSH, D))):
        inp(nm, sh)
    out = ctx.outp("out", (T, D))

    x1 = ctx.scratch("fx1", [T, D], F32)
    x2 = ctx.scratch("fx2", [T, D], F32)
    x2bf = ctx.scratch("fx2bf", [T, D], BF16)
    x2g = ctx.scratch("fx2g", [S, D], BF16)
    x3 = ctx.scratch("fx3", [T, D], F32)

    ctx.setup_consts()
    ctx.mark_persist()

    def phase_gap():
        pg.barrier()
        ctx.reset_pool()
        ctx.reset_banks(None)

    extA = {k: ins[k] for k in ("xb", "W_q", "W_k", "W_v", "W_o", "A_q", "B_q",
                                "A_k", "B_k", "A_v", "B_v", "A_o", "B_o",
                                "s_q", "s_k", "s_v", "s_o", "q_gain",
                                "cosk", "sink", "kbias", "bandm")}
    extA["out"] = x1
    build_l0_attn(D=D, NH=NH0, NKV=NKV, S=S, T=T, R=R, ctx=ctx, ext=extA)
    phase_gap()

    extB = {"xslab": x1, "w_fc": ins["w_fc"], "w_proj": ins["w_proj"], "out": x2}
    build_l0_ffn(D=D, HID=HID, T=T, ctx=ctx, ext=extB, x2bf=x2bf)
    phase_gap()

    pg.sem("cc")
    rg = [[0, 1, 2, 3], [4, 5, 6, 7]]
    t_cc = pg.op("gpsimd",
                 lambda E: E.collective_compute(
                     "AllGather", mybir.AluOpType.bypass,
                     replica_groups=rg,
                     ins=[x2bf.ap().opt()], outs=[x2g.ap().opt()]),
                 waits=pg.all_dma_toks(), inc="cc", inc_amt=1)
    ctx.reset_banks(None)

    extC = {"xloc": x2, "xg": x2g, "out": x3, "cc_tok": t_cc,
            "cos1": ins["cos1"], "sin1": ins["sin1"],
            "cos1q": ins["cos1q"], "sin1q": ins["sin1q"]}
    for k in ("w_kv_down", "w_kv_up_k", "w_kv_up_v", "w_q_down", "w_q_up",
              "w_k_rope", "w_q_rope", "w_out"):
        extC[k] = ins[k]
    build_l1_attn(D=D, NH=NH1, KVC=KVC, QC=QC, S=S, T=T, ctx=ctx, ext=extC,
                  split=True)
    phase_gap()

    extD = {"xslab": x3, "out": out}
    for k in ("gate_w", "wg", "wu", "wd", "sg", "su", "sd"):
        extD[k] = ins[k]
    build_l1_ffn(D=D, NE=NE, FE=FE, TOPK=TOPK, FSH=FSH, T=T, ctx=ctx, ext=extD)

    return ctx.finish(pg.all_dma_toks())



def _ident_np():
    return np.eye(P, dtype=BF16NP)


def make_rope_l1(S, HD=128):
    inv = (1.0 / np.float32(10000.0) ** (np.arange(0, HD, 2, dtype=np.float32) / np.float32(HD))).astype(np.float32)
    ang = (np.arange(S, dtype=np.float32)[:, None] * inv[None, :]).astype(np.float32)
    return np.cos(ang).astype(np.float32), np.sin(ang).astype(np.float32)


def make_rope_l0(S, ROPE=64):
    inv_freq = (1.0 / np.float32(10000.0) ** (np.arange(0, ROPE, 2, dtype=np.float32) / np.float32(ROPE))).astype(np.float32)
    freqs = np.outer(np.arange(S, dtype=np.float32), inv_freq).astype(np.float32)
    return np.cos(freqs).astype(np.float32), np.sin(freqs).astype(np.float32)


def make_kbias(t0, S, T):
    KT = S // P
    kb = np.zeros((P, KT), np.float32)
    for kt in range(KT):
        if kt < T // P:
            continue
        keys = kt * P + np.arange(P)
        kb[:, kt] = np.where(keys >= S - t0, 0.0, -1e9)
    return kb


def make_bandm(QG):
    c = np.arange(2 * QG - P)[None, :]
    p = np.arange(P)[:, None]
    return np.where(c >= p + (QG - P), 0.0, -1e9).astype(np.float32)


_BUILD_CACHE = {}


def _get(name, fn):
    if name not in _BUILD_CACHE:
        _BUILD_CACHE[name] = fn()
    return _BUILD_CACHE[name]


def kernel(**inputs):
    B, S, D = 2, 8192, 2048
    T = S // 4
    NC = 8
    ident = _ident_np()
    identf = np.eye(P, dtype=np.float32)
    f32 = np.float32

    def rolled(xb, t0):
        return np.ascontiguousarray(np.roll(xb, -t0, axis=0), dtype=f32)

    cos0, sin0 = make_rope_l0(S)
    cos1np, sin1np = make_rope_l1(S)
    bandm = make_bandm(1024)
    ncF = _get("F", build_fused)
    x = np.asarray(inputs["L0_attn_x"], f32)
    wmap = {}
    for nm in ("W_q", "W_k", "W_v", "W_o", "A_q", "B_q", "A_k", "B_k",
               "A_v", "B_v", "A_o", "B_o", "s_q", "s_k", "s_v", "s_o", "q_gain"):
        wmap[nm] = np.asarray(inputs["L0_attn_" + nm], f32)
    wmap["w_fc"] = np.asarray(inputs["L0_ffn_w_fc"], f32)
    wmap["w_proj"] = np.asarray(inputs["L0_ffn_w_proj"], f32)
    for nm in ("w_kv_down", "w_kv_up_k", "w_kv_up_v", "w_q_down", "w_q_up",
               "w_k_rope", "w_q_rope", "w_out"):
        wmap[nm] = np.asarray(inputs["L1_attn_" + nm], f32)
    for nm, src in (("gate_w", "L1_ffn_gate_w"), ("wg", "L1_ffn_wg"),
                    ("wu", "L1_ffn_wu"), ("wd", "L1_ffn_wd"),
                    ("sg", "L1_ffn_sg"), ("su", "L1_ffn_su"),
                    ("sd", "L1_ffn_sd")):
        wmap[nm] = np.asarray(inputs[src], f32)
    in_maps = []
    for c in range(NC):
        b, t0 = c // 4, (c % 4) * T
        m = dict(wmap)
        m["xb"] = rolled(x[b], t0)
        m["cosk"] = rolled(cos0, t0)
        m["sink"] = rolled(sin0, t0)
        m["kbias"] = make_kbias(t0, S, T)
        m["bandm"] = bandm
        m["cos1"] = cos1np
        m["sin1"] = sin1np
        m["cos1q"] = np.ascontiguousarray(cos1np[t0:t0 + T])
        m["sin1q"] = np.ascontiguousarray(sin1np[t0:t0 + T])
        m["c_ident"] = ident
        m["c_identf"] = identf
        in_maps.append(m)
    res = run_bass_kernel_spmd(ncF, in_maps, core_ids=list(range(NC)))
    y = np.empty((B, S, D), f32)
    for c in range(NC):
        b, t0 = c // 4, (c % 4) * T
        y[b, t0:t0 + T] = res.results[c]["out"]
    return y


def kernel_multilaunch(**inputs):
    B, S, D = 2, 8192, 2048
    T = S // 4
    NC = 8
    ident = _ident_np()
    identf = np.eye(P, dtype=np.float32)
    f32 = np.float32

    def rolled(xb, t0):
        return np.ascontiguousarray(np.roll(xb, -t0, axis=0), dtype=f32)

    cos0, sin0 = make_rope_l0(S)
    cos1np, sin1np = make_rope_l1(S)
    bandm = make_bandm(1024)

    ncA = _get("A", build_l0_attn)
    x = np.asarray(inputs["L0_attn_x"], f32)
    in_maps = []
    for c in range(NC):
        b, t0 = c // 4, (c % 4) * T
        m = {"xb": rolled(x[b], t0),
             "cosk": rolled(cos0, t0), "sink": rolled(sin0, t0),
             "kbias": make_kbias(t0, S, T), "bandm": bandm, "c_ident": ident,
             "c_identf": identf}
        for nm in ("W_q", "W_k", "W_v", "W_o", "A_q", "B_q", "A_k", "B_k",
                   "A_v", "B_v", "A_o", "B_o", "s_q", "s_k", "s_v", "s_o",
                   "q_gain"):
            m[nm] = np.asarray(inputs["L0_attn_" + nm], f32)
        in_maps.append(m)
    res = run_bass_kernel_spmd(ncA, in_maps, core_ids=list(range(NC)))
    x1 = np.empty((B, S, D), f32)
    for c in range(NC):
        b, t0 = c // 4, (c % 4) * T
        x1[b, t0:t0 + T] = res.results[c]["out"]

    ncB = _get("B", build_l0_ffn)
    in_maps = []
    for c in range(NC):
        b, t0 = c // 4, (c % 4) * T
        in_maps.append({"xslab": np.ascontiguousarray(x1[b, t0:t0 + T]),
                        "w_fc": np.asarray(inputs["L0_ffn_w_fc"], f32),
                        "w_proj": np.asarray(inputs["L0_ffn_w_proj"], f32),
                        "c_ident": ident, "c_identf": identf})
    res = run_bass_kernel_spmd(ncB, in_maps, core_ids=list(range(NC)))
    x2 = np.empty((B, S, D), f32)
    for c in range(NC):
        b, t0 = c // 4, (c % 4) * T
        x2[b, t0:t0 + T] = res.results[c]["out"]

    ncC = _get("C", build_l1_attn)
    in_maps = []
    for c in range(NC):
        b, t0 = c // 4, (c % 4) * T
        m = {"xb": rolled(x2[b], t0),
             "cos1": rolled(cos1np, t0), "sin1": rolled(sin1np, t0),
             "c_ident": ident, "c_identf": identf}
        for nm in ("w_kv_down", "w_kv_up_k", "w_kv_up_v", "w_q_down", "w_q_up",
                   "w_k_rope", "w_q_rope", "w_out"):
            m[nm] = np.asarray(inputs["L1_attn_" + nm], f32)
        in_maps.append(m)
    res = run_bass_kernel_spmd(ncC, in_maps, core_ids=list(range(NC)))
    x3 = np.empty((B, S, D), f32)
    for c in range(NC):
        b, t0 = c // 4, (c % 4) * T
        x3[b, t0:t0 + T] = res.results[c]["out"]

    ncD = _get("D", build_l1_ffn)
    in_maps = []
    for c in range(NC):
        b, t0 = c // 4, (c % 4) * T
        in_maps.append({"xslab": np.ascontiguousarray(x3[b, t0:t0 + T]),
                        "gate_w": np.asarray(inputs["L1_ffn_gate_w"], f32),
                        "wg": np.asarray(inputs["L1_ffn_wg"], f32),
                        "wu": np.asarray(inputs["L1_ffn_wu"], f32),
                        "wd": np.asarray(inputs["L1_ffn_wd"], f32),
                        "sg": np.asarray(inputs["L1_ffn_sg"], f32),
                        "su": np.asarray(inputs["L1_ffn_su"], f32),
                        "sd": np.asarray(inputs["L1_ffn_sd"], f32),
                        "c_ident": ident, "c_identf": identf})
    res = run_bass_kernel_spmd(ncD, in_maps, core_ids=list(range(NC)))
    y = np.empty((B, S, D), f32)
    for c in range(NC):
        b, t0 = c // 4, (c % 4) * T
        y[b, t0:t0 + T] = res.results[c]["out"]
    return y


# version 14 (final)
# speedup vs baseline: 1.4364x; 1.3802x over previous
import numpy as np
import ml_dtypes
from contextlib import ExitStack

import concourse.bass as bass
import concourse.mybir as mybir
from concourse.bass_utils import run_bass_kernel_spmd

F32 = mybir.dt.float32
BF16 = mybir.dt.bfloat16
AF = mybir.ActivationFunctionType
ALU = mybir.AluOpType
AX = mybir.AxisListType
BF16NP = ml_dtypes.bfloat16

P = 128



class Prog:
    ENGINES = ("sync", "scalar", "tensor", "vector", "gpsimd")
    ESEM = {"sync": "s_sp", "scalar": "s_act", "tensor": "s_pe",
            "vector": "s_dve", "gpsimd": "s_pool"}

    def __init__(self, nc, stack):
        self.nc = nc
        self.stack = stack
        self.ops = {e: [] for e in self.ENGINES}
        self.sems = {}
        self.counts = {}
        for e in self.ENGINES:
            self.sem(self.ESEM[e])

    def sem(self, name):
        if name not in self.sems:
            self.sems[name] = self.stack.enter_context(self.nc.semaphore(name))
            self.counts[name] = 0
        return name

    def op(self, eng, fn, waits=(), inc="auto", inc_amt=1):
        if inc == "auto":
            inc = self.ESEM[eng]
        ws = tuple(w for w in waits if w is not None)
        for (s, v) in ws:
            if s.startswith("q_"):
                assert v == self.counts[s], (
                    f"stale DMA token on stream {s}: tok {v} < issued {self.counts[s]}")
        tok = None
        if inc is not None:
            self.counts[inc] += inc_amt
            tok = (inc, self.counts[inc])
        self.ops[eng].append((ws, fn, tok, inc_amt if inc else 0))
        return tok

    def dma(self, eng, out, in_, waits=(), stream="d"):
        s = self.sem(f"q_{eng}_{stream}")
        return self.op(eng, lambda E: E.dma_start(out=out, in_=in_), waits, s, 16)

    def all_dma_toks(self):
        return [self.tok(s) for s in self.counts if s.startswith("q_") and self.counts[s] > 0]

    def tok(self, sname):
        self.sem(sname)
        return (sname, self.counts[sname])

    def barrier(self):
        toks = []
        dts = self.all_dma_toks()
        for e in self.ENGINES:
            be = self.sem("b_" + e)
            if e == "sync":
                toks.append(self.op(e, lambda E: E.nop(), waits=dts, inc=be))
            else:
                toks.append(self.op(e, lambda E: E.drain(), waits=dts, inc=be))
        for e in self.ENGINES:
            self.op(e, lambda E: E.nop(), waits=toks, inc=None)
        return toks

    def emit(self):
        nc = self.nc
        with nc.Block() as block:
            for eng in self.ENGINES:
                ops = self.ops[eng]

                def body(E, ops=ops):
                    waited = {}
                    for ws, fn, tok, amt in ops:
                        for (s, v) in ws:
                            if v > 0 and waited.get(s, 0) < v:
                                E.wait_ge(self.sems[s], v)
                                waited[s] = v
                        ins = fn(E)
                        if tok is not None:
                            ins.then_inc(self.sems[tok[0]], amt)
                getattr(block, eng)(body)


def ceil_div(a, b):
    return -(-a // b)


class _ArenaView:

    def __init__(self, t, off, nbytes):
        self.t = t
        self.off = off
        self.nbytes = nbytes


class _IntAlias:

    def __init__(self, d):
        self.d = d

    def _k(self, k):
        return f"pb{k}" if isinstance(k, int) else k

    def __getitem__(self, k):
        return self.d[self._k(k)]

    def __setitem__(self, k, v):
        self.d[self._k(k)] = v


class Ctx:

    def __init__(self, name="k", psum_spec=None):
        self.nc = bass.Bass("TRN2", target_bir_lowering=False)
        self.stack = ExitStack()
        self.pg = Prog(self.nc, self.stack)
        if psum_spec == "both":
            base = self.stack.enter_context(self.nc.psum_tensor("PS", [P, 4096], F32))
            self.pt = {
                "ps0": base[:, 0:1024], "ps1": base[:, 1024:2048],
                "av": base[:, 2048:3584], "sp": base[:, 3584:4096],
            }
            for i in range(8):
                self.pt[f"pb{i}"] = base[:, i * 512:(i + 1) * 512]
            self.bank_w = {k: None for k in self.pt}
            self.bank_r = {k: None for k in self.pt}
            self.pb = _IntAlias(self.pt)
            self.bank_w = _IntAlias(self.bank_w)
            self.bank_r = _IntAlias(self.bank_r)
            self.inputs = {}
            return
        if psum_spec is None:
            psum_spec = [(f"pb{i}", [P, 512]) for i in range(8)]
        self.pt = {}
        self.bank_w = {}
        self.bank_r = {}
        for nm, shape in psum_spec:
            self.pt[nm] = self.stack.enter_context(self.nc.psum_tensor(nm, shape, F32))
            self.bank_w[nm] = None
            self.bank_r[nm] = None
        self.pb = _IntAlias(self.pt)
        self.bank_w = _IntAlias(self.bank_w)
        self.bank_r = _IntAlias(self.bank_r)
        self.inputs = {}

    POOL_BYTES = 212800

    def _pool(self):
        if not hasattr(self, "_pool_t"):
            self._pool_t = self.stack.enter_context(
                self.nc.sbuf_tensor("POOL", [P, self.POOL_BYTES], mybir.dt.uint8))
            self._pool_off = 0
            self._pool_base = 0
        return self._pool_t

    def sb(self, name, shape, dtype):
        t = self._pool()
        assert len(shape) == 2
        pp, cols = shape
        esz = 2 if dtype in (BF16, mybir.dt.float16) else 4
        off = (self._pool_off + 63) // 64 * 64
        nb = cols * esz
        assert off + nb <= self.POOL_BYTES, (
            f"pool overflow at {name}: need {nb} at {off}")
        self._pool_off = off + nb
        v = t[:].bitcast(dtype)[:, off // esz: off // esz + cols]
        if pp != P:
            v = v[0:pp, :]
        return v

    def reset_banks(self, tok=None):
        keys = self.pt.keys() if isinstance(self.pt, dict) else self.pt.d.keys()
        for k in list(keys):
            self.bank_w[k] = tok
            self.bank_r[k] = tok

    def mark_persist(self):
        self._pool()
        self._pool_base = (self._pool_off + 63) // 64 * 64

    def reset_pool(self):
        self._pool_off = self._pool_base

    def arena(self, name, nbytes):
        nbytes = (nbytes + 63) // 64 * 64
        t = self._pool()
        off = (self._pool_off + 63) // 64 * 64
        assert off + nbytes <= self.POOL_BYTES, (
            f"pool overflow at arena {name}: need {nbytes} at {off}")
        self._pool_off = off + nbytes
        return _ArenaView(t, off, nbytes)

    @staticmethod
    def view(ar, byte_off, cols, dtype):
        esz = 2 if dtype == BF16 else 4
        assert (ar.off + byte_off) % esz == 0, (ar.off, byte_off, esz)
        base = (ar.off + byte_off) // esz
        return ar.t[:].bitcast(dtype)[:, base: base + cols]

    def inp(self, name, shape, dtype=F32):
        t = self.nc.dram_tensor(name, list(shape), dtype, kind="ExternalInput")
        self.inputs[name] = (tuple(shape), dtype)
        return t

    def outp(self, name, shape, dtype=F32):
        return self.nc.dram_tensor(name, list(shape), dtype, kind="ExternalOutput")

    def scratch(self, name, shape, dtype):
        return self.nc.dram_tensor(name, list(shape), dtype, kind="Internal")

    def pbf(self, i, n=1024):
        t = self.pt[i] if isinstance(i, str) else self.pb[i]
        return t[:].bitcast(BF16)[:, 0:n]

    def setup_consts(self):
        pg = self.pg
        identd = self.inp("c_ident", (P, P), BF16)
        identfd = self.inp("c_identf", (P, P), F32)
        self.ident = self.sb("identb", [P, P], BF16)
        self.identf = self.sb("identf", [P, P], F32)
        self.ones_col_b = self.sb("ones_col_b", [P, 1], BF16)
        self.ones_row_b = self.sb("ones_row_b", [1, P], BF16)
        self.ones_col_f = self.sb("ones_col_f", [P, 1], F32)
        self.ones_row_f = self.sb("ones_row_f", [1, P], F32)
        self.zer_b = self.sb("zer_b", [P, 512], BF16)
        t0 = pg.dma("sync", self.ident[:], identd.ap(), stream="c")
        t0b = pg.dma("sync", self.identf[:], identfd.ap(), stream="c")
        tz = pg.op("gpsimd", lambda E: E.memset(self.zer_b[:], 0.0))
        t1 = pg.op("gpsimd", lambda E: E.memset(self.ones_col_b[:], 1.0))
        t2 = pg.op("gpsimd", lambda E: E.memset(self.ones_row_b[:], 1.0),
                   waits=[t1])
        t3 = pg.op("gpsimd", lambda E: E.memset(self.ones_col_f[:], 1.0),
                   waits=[t2])
        t4 = pg.op("gpsimd", lambda E: E.memset(self.ones_row_f[:], 1.0),
                   waits=[t3])
        pg.barrier()

    def transpose(self, src, dst, bank, waits=(), evac="vector", evac_waits=()):
        pg = self.pg
        pp, m = src.shape[0], src.shape[1]
        if src.dtype == BF16:
            view = self.pbf(bank, P)
            idn = self.ident
        else:
            t = self.pt[bank] if isinstance(bank, str) else self.pb[bank]
            view = t[:]
            idn = self.identf
        wts = list(waits) + [self.bank_r[bank]]
        tw = pg.op("tensor",
                   lambda E: E.transpose(view[0:m, 0:pp], src, idn[0:pp, 0:pp]),
                   waits=wts)
        self.bank_w[bank] = tw
        tr = pg.op(evac, lambda E: E.tensor_copy(out=dst, in_=view[0:m, 0:pp]),
                   waits=[tw] + list(evac_waits))
        self.bank_r[bank] = tr
        return tr

    def transpose_batch(self, srcs, dst_ap, bank, waits=(), evac="vector",
                        evac_waits=()):
        pg = self.pg
        n = len(srcs)
        if srcs[0].dtype == BF16:
            view = self.pbf(bank, 1024)
            idn = self.ident
        else:
            t = self.pt[bank] if isinstance(bank, str) else self.pb[bank]
            view = t[:]
            idn = self.identf
        wts = list(waits) + [self.bank_r[bank]]
        tw = None
        for i, src in enumerate(srcs):
            tw = pg.op("tensor",
                       lambda E, i=i, src=src: E.transpose(
                           view[:, i * P:(i + 1) * P], src, idn[:]),
                       waits=wts if i == 0 else list(waits))
        self.bank_w[bank] = tw
        tr = pg.op(evac,
                   lambda E: E.tensor_copy(
                       out=dst_ap,
                       in_=view[:, 0:n * P].rearrange("p (a n) -> p a n", n=P)),
                   waits=[tw] + list(evac_waits))
        self.bank_r[bank] = tr
        return tr

    def finish(self, final_waits):
        pg = self.pg
        pg.op("gpsimd", lambda E: E.nop(), waits=list(final_waits), inc=None)
        pg.emit()
        self.stack.close()
        return self.nc


def emit_layernorm(ctx, sumbuf, D, stat, out_ap, out_dtype_note=None, waits=(),
                   sq_scratch=None):
    pg = ctx.pg
    t_mean = pg.op("vector",
                   lambda E: E.reduce_sum(stat[:, 0:1], sumbuf[:, 0:D], axis=AX.X),
                   waits=list(waits))
    t_sq = pg.op("scalar",
                 lambda E: E.activation(sq_scratch[:, 0:D], sumbuf[:, 0:D], AF.Square,
                                        accum_out=stat[:, 1:2]),
                 waits=list(waits) + [pg.tok("s_act")])
    t = pg.op("vector",
              lambda E: E.tensor_scalar_mul(stat[:, 2:3], stat[:, 0:1], 1.0 / D),
              waits=[t_mean, t_sq])
    t = pg.op("vector", lambda E: E.tensor_mul(stat[:, 3:4], stat[:, 2:3], stat[:, 2:3]),
              waits=[t])
    t = pg.op("vector",
              lambda E: E.tensor_scalar(stat[:, 4:5], stat[:, 1:2], 1.0 / D, None, ALU.mult),
              waits=[t])
    t = pg.op("vector", lambda E: E.tensor_sub(stat[:, 5:6], stat[:, 4:5], stat[:, 3:4]),
              waits=[t])
    t = pg.op("vector", lambda E: E.tensor_scalar_add(stat[:, 6:7], stat[:, 5:6], 1e-5),
              waits=[t])
    t_rt = pg.op("scalar", lambda E: E.sqrt(stat[:, 7:8], stat[:, 6:7]),
                 waits=[t, t_sq])
    t_rc = pg.op("vector", lambda E: E.reciprocal(stat[:, 7:8], stat[:, 7:8]),
                 waits=[t_rt, t])
    t_o = pg.op("vector",
                lambda E: E.tensor_scalar(out_ap, sumbuf[:, 0:D],
                                          stat[:, 2:3], stat[:, 7:8],
                                          ALU.subtract, ALU.mult),
                waits=[t_rc])
    return t_o



def build_l0_ffn(D=2048, HID=5120, T=2048, ctx=None, ext=None, x2bf=None):
    standalone = ctx is None
    if standalone:
        ctx = Ctx()
    pg = ctx.pg
    EX = (lambda n, sh: ctx.inp(n, sh)) if ext is None else (lambda n, sh: ext[n])
    DCH, HG, TT = D // P, HID // P, T // P
    TOKH = min(T, 1024)
    NH = T // TOKH
    TH_T = TOKH // P

    xs = EX("xslab", (T, D))
    wfc = EX("w_fc", (HID, D))
    wpr = EX("w_proj", (D, HID))
    out = ctx.outp("out", (T, D)) if ext is None else ext["out"]
    wfcT = ctx.scratch("wfcT", [HG, P, D], BF16)
    wprT = ctx.scratch("wprT", [HG, P, D], BF16)

    if standalone:
        ctx.setup_consts()

    wbuf = [ctx.sb(f"wbuf_{i}", [P, HID], F32) for i in range(2)]
    red = ctx.sb("red", [P, HG + 2], F32)
    scal = ctx.sb("scal", [P, 8], F32)
    tiny = ctx.sb("tiny", [1, 8], F32)
    x_fm = ctx.sb("x_fm", [P, DCH * TOKH], BF16)
    act = ctx.sb("act", [P, HG * TOKH], BF16)
    tern = act[:, 0:D]
    stage = act[:, D:2 * D]
    wrow = [ctx.sb(f"wrow{i}", [P, D], BF16) for i in range(2)]
    xbrow = [ctx.sb(f"xbrow{i}", [P, D], BF16) for i in range(2)]
    xf32 = [ctx.sb("xf32_0", [P, D], F32)] * 2
    sumbuf = [ctx.sb(f"sumbuf{i}", [P, D], F32) for i in range(2)]
    sqbuf = ctx.sb("sqbuf", [P, D], F32)
    q1 = sumbuf[0]
    q2 = sqbuf
    stat = ctx.sb("stat_b", [P, 8], F32)
    outbuf = [wbuf[0][:, 0:D]] * 2
    relu_t = ctx.sb("relu_t", [P, 512], F32)

    def absmean(w, ncols, which):
        ng = (w.shape[0]) // P
        toks = []
        for g in range(ng):
            b = wbuf[g % 2]
            w_prev = toks[g - 2] if g >= 2 else pg.tok("s_dve")
            t_l = pg.dma("sync", b[:, 0:ncols], w[g * P:(g + 1) * P, :], waits=[w_prev],
                         stream=f"w{g % 2}")
            t_r = pg.op("vector",
                        lambda E, b=b, g=g: E.reduce_sum(
                            red[:, g:g + 1], b[:, 0:ncols],
                            axis=AX.X, apply_absolute_value=True),
                        waits=[t_l, pg.tok("s_dve")])
            toks.append(t_r)
        t_s = pg.op("vector",
                    lambda E, ng=ng: E.reduce_sum(red[:, ng:ng + 1], red[:, 0:ng], axis=AX.X),
                    waits=[toks[-1]])
        t_m = pg.op("tensor",
                    lambda E, which=which, ng=ng: E.matmul(
                        ctx.pb[7][0:1, which:which + 1],
                        ctx.ones_col_f[:],
                        red[:, ng:ng + 1],
                        start=True, stop=True),
                    waits=[t_s, ctx.bank_r[7]])
        ctx.bank_w[7] = t_m
        return t_m

    tm0 = absmean(wfc, D, 0)
    tm1 = absmean(wpr, HID, 2)
    t1 = pg.op("vector",
               lambda E: E.tensor_scalar(tiny[0:1, 0:1], ctx.pb[7][0:1, 0:1],
                                         1.0 / (HID * D), 1e-5, ALU.mult, ALU.max),
               waits=[tm0, tm1])
    t2 = pg.op("vector",
               lambda E: E.tensor_scalar(tiny[0:1, 1:2], ctx.pb[7][0:1, 2:3],
                                         1.0 / (HID * D), 1e-5, ALU.mult, ALU.max),
               waits=[tm0, tm1, t1])
    ctx.bank_r[7] = t2
    t_bc = pg.op("tensor",
                 lambda E: E.matmul(ctx.pb[6][:, 0:2], ctx.ones_row_f[:],
                                    tiny[0:1, 0:2],
                                    start=True, stop=True),
                 waits=[t2, ctx.bank_r[6]])
    ctx.bank_w[6] = t_bc
    t = pg.op("vector", lambda E: E.tensor_copy(scal[:, 0:1], ctx.pb[6][:, 0:1]), waits=[t_bc])
    t = pg.op("vector", lambda E: E.tensor_scalar_mul(scal[:, 1:2], ctx.pb[6][:, 0:1], 0.7),
              waits=[t_bc, t])
    t = pg.op("vector", lambda E: E.tensor_scalar_mul(scal[:, 2:3], ctx.pb[6][:, 0:1], -0.7),
              waits=[t_bc, t])
    t = pg.op("vector", lambda E: E.tensor_copy(scal[:, 3:4], ctx.pb[6][:, 1:2]),
              waits=[t_bc, t])
    t = pg.op("vector", lambda E: E.tensor_scalar_mul(scal[:, 4:5], ctx.pb[6][:, 1:2], 0.7),
              waits=[t_bc, t])
    t = pg.op("vector", lambda E: E.tensor_scalar_mul(scal[:, 5:6], ctx.pb[6][:, 1:2], -0.7),
              waits=[t_bc, t])
    ctx.bank_r[6] = t
    pg.barrier()

    def tern_transpose(w, wT, rows_are_h, scol):
        am = scal[:, scol:scol + 1]
        thr = scal[:, scol + 1:scol + 2]
        nthr = scal[:, scol + 2:scol + 3]
        prev_out = pg.tok(pg.sem("q_gpsimd_wT"))
        last_rd = [None, None]
        for hg in range(HG):
            b = wbuf[hg % 2]
            if rows_are_h:
                src = w[hg * P:(hg + 1) * P, :]
                dst = b[:, 0:D]
            else:
                src = w[:, hg * P:(hg + 1) * P].rearrange("(c p) h -> p c h", p=P)
                dst = b[:, 0:D].rearrange("p (c h) -> p c h", h=P)
            t_l = pg.dma("sync", dst, src,
                         waits=[last_rd[hg % 2] or pg.tok("s_dve")],
                         stream=f"w{hg % 2}")
            t_q = pg.op("vector",
                        lambda E, b=b: E.tensor_single_scalar(q1[:, 0:D], b[:, 0:D], thr, ALU.is_gt),
                        waits=[t_l, pg.tok("s_dve")])
            t_q = pg.op("vector",
                        lambda E, b=b: E.tensor_single_scalar(q2[:, 0:D], b[:, 0:D], nthr, ALU.is_lt),
                        waits=[t_l, t_q])
            last_rd[hg % 2] = t_q
            t_q = pg.op("vector", lambda E: E.tensor_sub(q1[:, 0:D], q1[:, 0:D], q2[:, 0:D]),
                        waits=[t_q])
            t_q = pg.op("vector", lambda E: E.tensor_scalar_mul(tern[:, 0:D], q1[:, 0:D], am),
                        waits=[t_q])
            ev_toks = []
            st3d = stage.rearrange("p (a n) -> p a n", n=P)
            for h8 in range(ceil_div(DCH, 8)):
                nn = min(8, DCH - h8 * 8)
                ev_toks.append(ctx.transpose_batch(
                    [tern[:, (h8 * 8 + i) * P:(h8 * 8 + i + 1) * P]
                     for i in range(nn)],
                    st3d[:, h8 * 8:h8 * 8 + nn, :],
                    bank=h8 % 2, waits=[t_q], evac_waits=[prev_out]))
            prev_out = pg.dma("gpsimd", wT[hg], stage[:, 0:D], waits=ev_toks[-2:],
                              stream="wT")
        return prev_out

    tern_transpose(wfc, wfcT, True, 0)
    tern_transpose(wpr, wprT, False, 3)
    pg.barrier()

    out_toks = []
    for h in range(NH):
        for tt in range(TH_T):
            g_tt = h * TH_T + tt
            b = xbrow[tt % 2]
            t_l = pg.dma("gpsimd", b[:, 0:D], xs[g_tt * P:(g_tt + 1) * P, :],
                         waits=[pg.tok("s_dve") if tt >= 2 else None],
                         stream=f"x{tt % 2}")
            x3d_b = x_fm.rearrange("p (a n) -> p a n", n=TOKH)
            for h8 in range(ceil_div(DCH, 8)):
                nn = min(8, DCH - h8 * 8)
                ctx.transpose_batch(
                    [b[:, (h8 * 8 + i) * P:(h8 * 8 + i + 1) * P]
                     for i in range(nn)],
                    x3d_b[:, h8 * 8:h8 * 8 + nn, tt * P:(tt + 1) * P],
                    bank=h8 % 2, waits=[t_l])
        pg.barrier()

        last_rd = [None, None]
        HD2 = D // 2
        for hg in range(HG):
            b = wrow[hg % 2]
            t_la = pg.dma("sync", b[:, 0:HD2], wfcT[hg][:, 0:HD2],
                          waits=[last_rd[hg % 2] or pg.tok("s_pe")],
                          stream=f"w{hg % 2}")
            t_lb = pg.dma("sync", b[:, HD2:D], wfcT[hg][:, HD2:D],
                          waits=[last_rd[hg % 2] or pg.tok("s_pe")],
                          stream=f"w{2 + hg % 2}")
            TB = min(512, TOKH)
            for tb in range(TOKH // TB):
                bank = tb % 2
                pt = ctx.pb[bank]
                mm = None
                for dc in range(DCH):
                    st, sp = dc == 0, dc == DCH - 1
                    wts = [t_la if dc * P < HD2 else t_lb]
                    if st:
                        wts.append(ctx.bank_r[bank])
                    mm = pg.op("tensor",
                               lambda E, pt=pt, b=b, dc=dc, tb=tb, st=st, sp=sp:
                               E.matmul(pt[:, 0:TB], b[:, dc * P:(dc + 1) * P],
                                        x_fm[:, dc * TOKH + tb * TB: dc * TOKH + (tb + 1) * TB],
                                        start=st, stop=sp),
                               waits=wts)
                ctx.bank_w[bank] = mm
                t_r = pg.op("scalar",
                            lambda E, pt=pt: E.activation(relu_t[:, 0:TB], pt[:, 0:TB], AF.Relu),
                            waits=[mm, pg.tok("s_dve")])
                t_a = pg.op("vector",
                            lambda E, hg=hg, tb=tb: E.tensor_mul(
                                act[:, hg * TOKH + tb * TB: hg * TOKH + (tb + 1) * TB],
                                relu_t[:, 0:TB], relu_t[:, 0:TB]),
                            waits=[t_r])
                ctx.bank_r[bank] = t_a
            last_rd[hg % 2] = mm
        pg.barrier()

        NDB = ceil_div(D, 512)
        DBW = min(512, D)
        BLKT = min(256, TOKH)
        NSUB = BLKT // P
        assert NSUB * NDB <= 8
        last_rd = [None, None]
        HD2 = D // 2
        for blk in range(TOKH // BLKT):
            mm_last = [[None] * NDB for _ in range(NSUB)]
            for hg in range(HG):
                b = wrow[hg % 2]
                t_la = pg.dma("sync", b[:, 0:HD2], wprT[hg][:, 0:HD2],
                              waits=[last_rd[hg % 2] or pg.tok("s_pe")],
                              stream=f"w{hg % 2}")
                t_lb = pg.dma("sync", b[:, HD2:D], wprT[hg][:, HD2:D],
                              waits=[last_rd[hg % 2] or pg.tok("s_pe")],
                              stream=f"w{2 + hg % 2}")
                for sub in range(NSUB):
                    tt = h * TH_T + blk * NSUB + sub
                    lt = blk * BLKT + sub * P
                    for db in range(NDB):
                        bank = sub * NDB + db
                        st = hg == 0
                        wts = ([t_la] if db * DBW < HD2 else []) + \
                              ([t_lb] if (db + 1) * DBW > HD2 else [])
                        if st:
                            wts.append(ctx.bank_r[bank])
                        mm_last[sub][db] = pg.op(
                            "tensor",
                            lambda E, bank=bank, b=b, db=db, hg=hg, lt=lt, st=st:
                            E.matmul(ctx.pb[bank][:, 0:DBW],
                                     act[:, hg * TOKH + lt: hg * TOKH + lt + P],
                                     b[:, db * DBW:(db + 1) * DBW],
                                     start=st, stop=(hg == HG - 1)),
                            waits=wts)
                last_rd[hg % 2] = mm_last[NSUB - 1][NDB - 1]
            for sub in range(NSUB):
                tt = h * TH_T + blk * NSUB + sub
                g_tt = tt
                bx = xf32[sub]
                t_rx = pg.dma("sync", bx[:, 0:D], xs[g_tt * P:(g_tt + 1) * P, :],
                              waits=[pg.tok("s_dve")], stream=f"r{sub}")
                sb_ = sumbuf[sub]
                t_s = None
                for db in range(NDB):
                    bank = sub * NDB + db
                    ctx.bank_w[bank] = mm_last[sub][db]
                    t_s = pg.op("vector",
                                lambda E, bank=bank, db=db, bx=bx, sb_=sb_: E.tensor_add(
                                    sb_[:, db * DBW:(db + 1) * DBW],
                                    ctx.pb[bank][:, 0:DBW],
                                    bx[:, db * DBW:(db + 1) * DBW]),
                                waits=[mm_last[sub][db], t_rx, pg.tok("s_dve")])
                    ctx.bank_r[bank] = t_s
                ob = outbuf[sub]
                t_o = emit_layernorm(ctx, sb_, D, stat, ob[:, 0:D],
                                     waits=[t_s,
                                            out_toks[-1] if out_toks else None],
                                     sq_scratch=sqbuf)
                t_out = pg.dma("sync", out[g_tt * P:(g_tt + 1) * P, :], ob[:, 0:D],
                               waits=[t_o], stream=f"o{sub % 2}")
                out_toks.append(t_out)
                if x2bf is not None:
                    out_toks.append(pg.dma(
                        "gpsimd", x2bf[g_tt * P:(g_tt + 1) * P, :], ob[:, 0:D],
                        waits=[t_o], stream=f"ob{sub % 2}"))
        pg.barrier()

    if standalone:
        return ctx.finish(pg.all_dma_toks())
    return None





def emit_rope(pg, kn3, kr3, e3, cs, sn, t_n, HALF, ROPE, HD):
    t_r1 = pg.op("vector", lambda E: E.tensor_tensor(
        kr3[:, :, 0:HALF], kn3[:, :, 0:HALF], cs, ALU.mult), waits=[t_n, pg.tok("s_dve")])
    t_r2 = pg.op("vector", lambda E: E.tensor_tensor(
        e3, kn3[:, :, HALF:ROPE], sn, ALU.mult), waits=[t_r1])
    t_r3 = pg.op("vector", lambda E: E.tensor_tensor(
        kr3[:, :, 0:HALF], kr3[:, :, 0:HALF], e3, ALU.add), waits=[t_r2])
    t_r4 = pg.op("vector", lambda E: E.tensor_tensor(
        kr3[:, :, HALF:ROPE], kn3[:, :, HALF:ROPE], cs, ALU.mult), waits=[t_r3])
    t_r5 = pg.op("vector", lambda E: E.tensor_tensor(
        e3, kn3[:, :, 0:HALF], sn, ALU.mult), waits=[t_r4])
    t_r6 = pg.op("vector", lambda E: E.tensor_tensor(
        kr3[:, :, HALF:ROPE], kr3[:, :, HALF:ROPE], e3, ALU.subtract), waits=[t_r5])
    t_r7 = pg.op("vector", lambda E: E.tensor_copy(
        kr3[:, :, ROPE:HD], kn3[:, :, ROPE:HD]), waits=[t_r6])
    return t_r7



def build_l0_attn(D=2048, NH=16, NKV=8, HD=128, S=8192, T=2048, R=32,
                  ROPE=64, EPS=1.1920929e-07, ctx=None, ext=None):
    KVD = NKV * HD
    DCH = D // P
    KT = S // P
    TT = T // P
    QG = min(1024, T)
    NQG = T // QG
    QGT = QG // P
    HPH = NH // 2
    HPK = NH // NKV
    HALF = ROPE // 2
    SC = 1.0 / np.sqrt(HD)
    NB = float(np.sqrt(HD))

    standalone = ctx is None
    if standalone:
        ctx = Ctx(psum_spec=[("ps0", [P, 1024]), ("ps1", [P, 1024]),
                             ("av", [P, 1536]), ("sp", [P, 512])])
    pg = ctx.pg
    EX = (lambda n, sh: ctx.inp(n, sh)) if ext is None else (lambda n, sh: ext[n])

    xb = EX("xb", (S, D))
    Wq = EX("W_q", (D, D)); Wk = EX("W_k", (KVD, D))
    Wv = EX("W_v", (KVD, D)); Wo = EX("W_o", (D, D))
    Aq = EX("A_q", (R, D)); Bq = EX("B_q", (D, R))
    Ak = EX("A_k", (R, D)); Bk = EX("B_k", (KVD, R))
    Av = EX("A_v", (R, D)); Bv = EX("B_v", (KVD, R))
    Ao = EX("A_o", (R, D)); Bo = EX("B_o", (D, R))
    sq = EX("s_q", (1,)); sk = EX("s_k", (1,))
    sv = EX("s_v", (1,)); so = EX("s_o", (1,))
    qg_in = EX("q_gain", (NH,))
    cosk = EX("cosk", (S, HALF)); sink = EX("sink", (S, HALF))
    kbias = EX("kbias", (P, KT))
    bandm = EX("bandm", (P, 2 * QG - P))
    out = ctx.outp("out", (T, D)) if ext is None else ext["out"]
    kscr = ctx.scratch("kscr", [NKV, KT, P, P], BF16)
    vscr = ctx.scratch("vscr", [KT, P, NKV * 129], BF16)
    qscr = ctx.scratch("qscr", [NH, TT, P, P], BF16)
    yscr = ctx.scratch("yscr", [NH, TT, P, P], BF16)

    if standalone:
        ctx.setup_consts()

    QW = (NH // 2) * HD
    AR1B = 2 * DCH * max(T, D)
    AR2B = max(2 * DCH * 2 * KVD, 2 * DCH * QW, 2 * (KT * P + KT * 129 + T),
               8192 + 16 * D + 2 * NH * P)
    AR1 = ctx.arena("AR1", AR1B)
    AR2 = ctx.arena("AR2", AR2B)
    x_own = ctx.view(AR1, 0, DCH * T, BF16)
    wo_big = ctx.view(AR1, 0, DCH * D, BF16)
    wbig = ctx.view(AR2, 0, DCH * 2 * KVD, BF16)
    KH0 = ctx.view(AR2, 0, KT * P, BF16)
    VH0 = ctx.view(AR2, 2 * KT * P, KT * 129, BF16)
    QH0 = ctx.view(AR2, 2 * (KT * P + KT * 129), T, BF16)
    cos_sb = ctx.sb("cos_sb", [P, KT * HALF], F32)
    sin_sb = ctx.sb("sin_sb", [P, KT * HALF], F32)
    kb_sb = ctx.sb("kb_sb", [P, KT], F32)
    gbc = ctx.sb("gbc", [P, NH], F32)
    mneg = ctx.sb("mneg", [P, NH], F32)
    lA = {}
    for nm in ("q", "k", "v", "o"):
        lA[nm] = ctx.sb(f"lA_{nm}", [P, DCH * R], BF16)
    lB = {}
    for nm, wdt in (("q", D), ("k", KVD), ("v", KVD), ("o", D)):
        lB[nm] = ctx.sb(f"lB_{nm}", [R, wdt], BF16)
    wld = [ctx.sb("wld0", [P, D], F32)] * 2
    wldb = ctx.sb("wldb", [P, D], BF16)
    xstg = [ctx.sb(f"xstg{i}", [P, DCH * P], BF16) for i in range(2)]
    xrow = [ctx.sb(f"xrow{i}", [P, D], BF16) for i in range(2)]
    epi = ctx.sb("epi", [P, max(KVD, 1024)], F32)
    epi2 = ctx.sb("epi2", [P, 40], F32)
    KNB = 4 * max(KVD, 1024)
    _a4t = ((4 * QG + 2 * P + 2 * (min(1024, T) // P) * P + 63) // 64) * 64
    AR3 = ctx.arena("AR3", max(2 * KNB + 2 * NKV * 129 + 2 * NKV * P,
                               2 * KNB + (NH // 2) * P * 2,
                               _a4t + 2 * (2 * QG - P) + 2 * NH * KT))
    bandm_sb = ctx.view(AR3, _a4t, 2 * QG - P, BF16)
    bias_hk = ctx.view(AR3, _a4t + 2 * (2 * QG - P), NH * KT, BF16)
    knorm = ctx.view(AR3, 0, max(KVD, 1024), F32)
    krot = ctx.view(AR3, KNB, max(KVD, 1024), F32)
    vstg = ctx.view(AR3, 2 * KNB, NKV * 129, BF16)
    ktr = ctx.view(AR3, 2 * KNB + 2 * NKV * 129, NKV * P, BF16)
    tlora = ctx.sb("tlora", [R, 2 * P], BF16)
    qstg = ctx.view(AR3, 2 * KNB, (NH // 2) * P, BF16)
    PTb = [ctx.view(AR3, i * 2 * QG, QG, BF16) for i in range(2)]
    ystg = ctx.view(AR3, 4 * QG, P, BF16)
    ystag = ctx.view(AR3, 4 * QG + 2 * P, min(1024, T) // P * P, BF16)
    rcp = epi2[:, 38:39]
    tiny = epi2[0:1, 0:max(NH, 8)]
    xres = [ctx.view(AR2, 8192, D, F32)] * 2
    sumb = ctx.view(AR2, 8192 + 4 * D, D, F32)
    lnsq = ctx.view(AR2, 8192 + 8 * D, D, F32)
    stat = epi2[:, 32:40]
    obuf = [ctx.view(AR2, 8192 + 12 * D, D, F32)] * 2

    pg.dma("sync", cos_sb[:].rearrange("p (t c) -> p t c", c=HALF),
           cosk.ap().rearrange("(t p) c -> p t c", p=P), stream="c")
    pg.dma("sync", sin_sb[:].rearrange("p (t c) -> p t c", c=HALF),
           sink.ap().rearrange("(t p) c -> p t c", p=P), stream="c")
    pg.dma("sync", kb_sb[:], kbias.ap(), stream="c")
    t_g = pg.dma("sync", tiny[0:1, 0:NH], qg_in.ap().unsqueeze(0), stream="c")
    sc_t = epi2[0:1, 16:20]
    pg.dma("sync", sc_t[0:1, 0:1], sq.ap().unsqueeze(0), stream="c")
    pg.dma("sync", sc_t[0:1, 1:2], sk.ap().unsqueeze(0), stream="c")
    pg.dma("sync", sc_t[0:1, 2:3], sv.ap().unsqueeze(0), stream="c")
    t_sc = pg.dma("sync", sc_t[0:1, 3:4], so.ap().unsqueeze(0), stream="c")
    pg.barrier()
    t_mm = pg.op("tensor", lambda E: E.matmul(ctx.pt["sp"][:, 0:NH], ctx.ones_row_f[:],
                                              tiny[0:1, 0:NH], start=True, stop=True),
                 waits=[pg.tok("q_sync_c")])
    t2 = pg.op("vector", lambda E: E.tensor_copy(gbc[:, 0:NH], ctx.pt["sp"][:, 0:NH]),
               waits=[t_mm])
    t3 = pg.op("scalar", lambda E: E.activation(mneg[:, 0:NH], gbc[:, 0:NH], AF.Abs),
               waits=[t2])
    t4 = pg.op("vector", lambda E: E.tensor_scalar_mul(mneg[:, 0:NH], mneg[:, 0:NH], -NB),
               waits=[t3])
    t_mm2 = pg.op("tensor", lambda E: E.matmul(ctx.pt["sp"][:, 64:68], ctx.ones_row_f[:],
                                               sc_t[0:1, 0:4], start=True, stop=True),
                  waits=[t_sc, t2, t4])
    scbc = ctx.sb("scbc", [P, 4], F32)
    t6 = pg.op("vector", lambda E: E.tensor_copy(scbc[:, 0:4], ctx.pt["sp"][:, 64:68]),
               waits=[t_mm2])
    pg.barrier()

    def prep_lora(Amat, Bmat, nm, outw, scol=None):
        t_l = pg.dma("sync", wld[0][0:R, 0:D], Amat.ap(), waits=[pg.tok("s_dve")],
                     stream="w0")
        t_c = pg.op("vector", lambda E: E.tensor_copy(wldb[0:R, 0:D], wld[0][0:R, 0:D]),
                    waits=[t_l])
        for dc in range(DCH):
            ctx.transpose(wldb[0:R, dc * P:(dc + 1) * P],
                          lA[nm][:, dc * R:(dc + 1) * R], "sp", waits=[t_c])
        t_l2 = pg.dma("sync", wld[1][:, 0:(outw // P) * R].rearrange(
            "p (c r) -> p c r", r=R),
            Bmat.ap().rearrange("(c p) r -> p c r", p=P),
            waits=[pg.tok("s_dve")], stream="w1")
        t_c2 = pg.op("vector", lambda E: E.tensor_copy(
            wldb[:, 0:(outw // P) * R], wld[1][:, 0:(outw // P) * R]),
            waits=[t_l2, pg.tok("s_pe")])
        for c in range(outw // P):
            ctx.transpose(wldb[:, c * R:(c + 1) * R],
                          lB[nm][0:R, c * P:(c + 1) * P], "sp", waits=[t_c2])

    prep_lora(Aq, Bq, "q", D)
    prep_lora(Ak, Bk, "k", KVD)
    prep_lora(Av, Bv, "v", KVD)
    prep_lora(Ao, Bo, "o", D)
    pg.barrier()

    KV2 = 2 * KVD
    def prep_w(Wmat, outw, col_off, stride, scale_col, dest, dest_onecol=None):
        for oc in range(outw // P):
            b = wld[oc % 2]
            t_l = pg.dma("sync", b[:, 0:D], Wmat[oc * P:(oc + 1) * P, :],
                         waits=[pg.tok("s_dve")], stream=f"w{oc % 2}")
            t_c = pg.op("vector",
                        lambda E, b=b: E.tensor_scalar_mul(wldb[:, 0:D], b[:, 0:D],
                                                           scbc[:, scale_col:scale_col + 1]),
                        waits=[t_l, pg.tok("s_dve")])
            d3 = dest.rearrange("p (a n) -> p a n", n=stride)
            for h8 in range(ceil_div(DCH, 8)):
                nn = min(8, DCH - h8 * 8)
                ctx.transpose_batch(
                    [wldb[:, (h8 * 8 + i) * P:(h8 * 8 + i + 1) * P]
                     for i in range(nn)],
                    d3[:, h8 * 8:h8 * 8 + nn,
                       col_off + oc * P:col_off + (oc + 1) * P],
                    "sp", waits=[t_c])

    prep_w(Wk, KVD, 0, KV2, 1, wbig)
    prep_w(Wv, KVD, KVD, KV2, 2, wbig)
    pg.barrier()

    vst_init = pg.op("gpsimd", lambda E: E.memset(vstg[:], 1.0))
    rms_c = 1.0 / HD
    for kt in range(KT):
        own = kt < TT
        xr = xrow[kt % 2]
        t_lx = pg.dma("gpsimd", xr[:, 0:D], xb[kt * P:(kt + 1) * P, :],
                      waits=[pg.tok("s_pe")], stream=f"x{kt % 2}")
        if own:
            xfm = lambda dc, kt=kt: x_own[:, dc * T + kt * P: dc * T + (kt + 1) * P]
            x3d = x_own.rearrange("p (a n) -> p a n", n=T)
        else:
            stg = xstg[kt % 2]
            xfm = lambda dc, stg=stg: stg[:, dc * P:(dc + 1) * P]
            x3d = stg.rearrange("p (a n) -> p a n", n=P)
        for h8 in range(ceil_div(DCH, 8)):
            nn = min(8, DCH - h8 * 8)
            srcs = [xr[:, (h8 * 8 + i) * P:(h8 * 8 + i + 1) * P]
                    for i in range(nn)]
            if own:
                dst = x3d[:, h8 * 8:h8 * 8 + nn, kt * P:(kt + 1) * P]
            else:
                dst = x3d[:, h8 * 8:h8 * 8 + nn, :]
            ctx.transpose_batch(srcs, dst, "sp", waits=[t_lx],
                                evac_waits=[pg.tok("s_pe")])
        t_x = pg.tok("s_dve")
        for j, nm in enumerate(("k", "v")):
            mm = None
            for dc in range(DCH):
                mm = pg.op("tensor",
                           lambda E, dc=dc, j=j, nm=nm, xfm=xfm, st=(dc == 0), sp_=(dc == DCH - 1):
                           E.matmul(ctx.pt["av"][0:R, j * P:j * P + P],
                                    lA[nm][:, dc * R:(dc + 1) * R],
                                    xfm(dc), start=st, stop=sp_),
                           waits=[t_x, ctx.bank_r["av"] if dc == 0 else None])
            ctx.bank_w["av"] = mm
            tev = pg.op("vector",
                        lambda E, j=j: E.tensor_copy(tlora[0:R, j * P:(j + 1) * P],
                                                     ctx.pt["av"][0:R, j * P:j * P + P]),
                        waits=[mm, pg.tok("s_act")])
            ctx.bank_r["av"] = tev
        t_tl = pg.tok("s_dve")
        kps = ["ps0", "ps1"][kt % 2]
        for j, (nm, ps) in enumerate((("k", kps), ("v", "av"))):
            co = j * KVD
            mm = None
            for dc in range(DCH):
                for nb in range(ceil_div(KVD, 512)):
                    st = dc == 0
                    w_ = [t_x]
                    if st and nb == 0:
                        w_.append(ctx.bank_r[ps])
                    po = 512 if j == 1 else 0
                    mm = pg.op("tensor",
                               lambda E, dc=dc, nb=nb, co=co, ps=ps, st=st, xfm=xfm, po=po:
                               E.matmul(ctx.pt[ps][:, po + nb * 512:po + min(KVD, (nb + 1) * 512)],
                                        xfm(dc),
                                        wbig[:, dc * KV2 + co + nb * 512:
                                             dc * KV2 + co + min(KVD, (nb + 1) * 512)],
                                        start=st, stop=False),
                               waits=w_)
            for nb in range(ceil_div(KVD, 512)):
                po = 512 if j == 1 else 0
                mm = pg.op("tensor",
                           lambda E, nb=nb, j=j, nm=nm, ps=ps, po=po:
                           E.matmul(ctx.pt[ps][:, po + nb * 512:po + min(KVD, (nb + 1) * 512)],
                                    tlora[0:R, j * P:(j + 1) * P],
                                    lB[nm][0:R, nb * 512:min(KVD, (nb + 1) * 512)],
                                    start=False, stop=True),
                           waits=[t_tl])
            ctx.bank_w[ps] = mm
        t_sq = pg.op("scalar",
                     lambda E, kps=kps: E.activation(epi[:, 0:KVD], ctx.pt[kps][:, 0:KVD],
                                                     AF.Square),
                     waits=[ctx.bank_w[kps], pg.tok("s_dve")])
        t_red = pg.op("vector",
                      lambda E: E.reduce_sum(
                          epi2[:, 0:NKV],
                          epi[:, 0:KVD].rearrange("p (h d) -> p h d", d=HD), axis=AX.X),
                      waits=[t_sq])
        t_st = pg.op("vector",
                     lambda E: E.tensor_scalar(epi2[:, 8:8 + NKV], epi2[:, 0:NKV],
                                               rms_c, EPS, ALU.mult, ALU.add),
                     waits=[t_red])
        t_rt = pg.op("scalar", lambda E: E.sqrt(epi2[:, 16:16 + NKV], epi2[:, 8:8 + NKV]),
                     waits=[t_st, t_sq])
        t_rc = pg.op("vector", lambda E: E.reciprocal(epi2[:, 16:16 + NKV],
                                                      epi2[:, 16:16 + NKV]),
                     waits=[t_rt])
        t_n = pg.op("vector",
                    lambda E, kps=kps: E.tensor_tensor(
                        knorm[:, 0:KVD].rearrange("p (h d) -> p h d", d=HD),
                        ctx.pt[kps][:, 0:KVD].rearrange("p (h d) -> p h d", d=HD),
                        epi2[:, 16:16 + NKV].unsqueeze(2).broadcast_to(
                            [P, NKV, HD]),
                        ALU.mult),
                    waits=[t_rc])
        ctx.bank_r[kps] = t_n
        kn3 = knorm[:, 0:KVD].rearrange("p (h d) -> p h d", d=HD)
        kr3 = krot[:, 0:KVD].rearrange("p (h d) -> p h d", d=HD)
        cs = cos_sb[:, kt * HALF:(kt + 1) * HALF].unsqueeze(1).broadcast_to([P, NKV, HALF])
        sn = sin_sb[:, kt * HALF:(kt + 1) * HALF].unsqueeze(1).broadcast_to([P, NKV, HALF])
        e3 = epi[:, 0:NKV * HALF].rearrange("p (h c) -> p h c", c=HALF)
        t_r7 = emit_rope(pg, kn3, kr3, e3, cs, sn, t_n, HALF, ROPE, HD)
        tr_toks = []
        kt3 = ktr.rearrange("p (a n) -> p a n", n=P)
        for h4 in range(ceil_div(NKV, 4)):
            nn = min(4, NKV - h4 * 4)
            tr_toks.append(ctx.transpose_batch(
                [krot[:, (h4 * 4 + i) * HD:(h4 * 4 + i) * HD + P]
                 for i in range(nn)],
                kt3[:, h4 * 4:h4 * 4 + nn, :], "sp", waits=[t_r7],
                evac_waits=[pg.tok("q_gpsimd_ks") if kt > 0 else None]))
        pg.dma("gpsimd", kscr[:, kt].rearrange("h p n -> p h n"),
               ktr[:].rearrange("p (h n) -> p h n", n=P),
               waits=[tr_toks[-1]], stream="ks")
        t_v = pg.op("vector",
                    lambda E: E.tensor_copy(
                        vstg[:].rearrange("p (h n) -> p h n", n=129)[:, :, 0:P],
                        ctx.pt["av"][:, 512:512 + KVD].rearrange("p (h d) -> p h d", d=HD)),
                    waits=[ctx.bank_w["av"], vst_init,
                           pg.tok("q_gpsimd_vs") if kt > 0 else None])
        ctx.bank_r["av"] = t_v
        pg.dma("gpsimd", vscr[kt], vstg[:], waits=[t_v], stream="vs")
    pg.barrier()

    for half in range(2):
        QW = HPH * HD
        for oc in range(QW // P):
            b = wld[oc % 2]
            goc = half * (QW // P) + oc
            t_l = pg.dma("sync", b[:, 0:D], Wq[goc * P:(goc + 1) * P, :],
                         waits=[pg.tok("s_dve")], stream=f"w{oc % 2}")
            t_c = pg.op("vector",
                        lambda E, b=b: E.tensor_scalar_mul(wldb[:, 0:D], b[:, 0:D],
                                                           scbc[:, 0:1]),
                        waits=[t_l, pg.tok("s_dve")])
            wb3 = wbig.rearrange("p (a n) -> p a n", n=QW)
            for h8 in range(ceil_div(DCH, 8)):
                nn = min(8, DCH - h8 * 8)
                ctx.transpose_batch(
                    [wldb[:, (h8 * 8 + i) * P:(h8 * 8 + i + 1) * P]
                     for i in range(nn)],
                    wb3[:, h8 * 8:h8 * 8 + nn, oc * P:(oc + 1) * P],
                    "sp", waits=[t_c], evac_waits=[pg.tok("s_pe")])
        pg.barrier()
        for tt in range(TT):
            xfm = lambda dc, tt=tt: x_own[:, dc * T + tt * P: dc * T + (tt + 1) * P]
            mm = None
            for dc in range(DCH):
                mm = pg.op("tensor",
                           lambda E, dc=dc, xfm=xfm, st=(dc == 0), sp_=(dc == DCH - 1):
                           E.matmul(ctx.pt["av"][0:R, 0:P],
                                    lA["q"][:, dc * R:(dc + 1) * R],
                                    xfm(dc), start=st, stop=sp_),
                           waits=[ctx.bank_r["av"] if dc == 0 else None])
            ctx.bank_w["av"] = mm
            tev = pg.op("vector",
                        lambda E: E.tensor_copy(tlora[0:R, 0:P], ctx.pt["av"][0:R, 0:P]),
                        waits=[mm, pg.tok("s_act")])
            ctx.bank_r["av"] = tev
            NBQ = ceil_div(QW, 512)
            QBW = min(QW, 512)
            mm = None
            for dc in range(DCH):
                for nb in range(NBQ):
                    st = dc == 0
                    w_ = []
                    if st and nb == 0:
                        w_.append(ctx.bank_r["ps0"])
                    mm = pg.op("tensor",
                               lambda E, dc=dc, nb=nb, st=st, xfm=xfm:
                               E.matmul(ctx.pt["ps0"][:, nb * QBW:(nb + 1) * QBW],
                                        xfm(dc),
                                        wbig[:, dc * QW + nb * QBW:
                                             dc * QW + (nb + 1) * QBW],
                                        start=st, stop=False),
                               waits=w_)
            for nb in range(NBQ):
                mm = pg.op("tensor",
                           lambda E, nb=nb, half=half:
                           E.matmul(ctx.pt["ps0"][:, nb * QBW:(nb + 1) * QBW],
                                    tlora[0:R, 0:P],
                                    lB["q"][0:R, half * QW + nb * QBW:
                                            half * QW + (nb + 1) * QBW],
                                    start=False, stop=True),
                           waits=[tev])
            ctx.bank_w["ps0"] = mm
            t_sq = pg.op("scalar",
                         lambda E: E.activation(epi[:, 0:QW], ctx.pt["ps0"][:, 0:QW],
                                                AF.Square),
                         waits=[mm, pg.tok("s_dve")])
            t_red = pg.op("vector",
                          lambda E: E.reduce_sum(
                              epi2[:, 0:HPH],
                              epi[:, 0:QW].rearrange("p (h d) -> p h d", d=HD),
                              axis=AX.X),
                          waits=[t_sq])
            t_st = pg.op("vector",
                         lambda E: E.tensor_scalar(epi2[:, 8:8 + HPH], epi2[:, 0:HPH],
                                                   rms_c, EPS, ALU.mult, ALU.add),
                         waits=[t_red])
            t_rt = pg.op("scalar", lambda E: E.sqrt(epi2[:, 16:16 + HPH],
                                                    epi2[:, 8:8 + HPH]),
                         waits=[t_st, t_sq])
            t_rc = pg.op("vector", lambda E: E.reciprocal(epi2[:, 16:16 + HPH],
                                                          epi2[:, 16:16 + HPH]),
                         waits=[t_rt])
            t_gm = pg.op("vector",
                         lambda E, half=half: E.tensor_mul(
                             epi2[:, 24:24 + HPH], epi2[:, 16:16 + HPH],
                             gbc[:, half * HPH:(half + 1) * HPH]),
                         waits=[t_rc])
            t_n = pg.op("vector",
                        lambda E: E.tensor_tensor(
                            knorm[:, 0:QW].rearrange("p (h d) -> p h d", d=HD),
                            ctx.pt["ps0"][:, 0:QW].rearrange("p (h d) -> p h d", d=HD),
                            epi2[:, 24:24 + HPH].unsqueeze(2).broadcast_to(
                                [P, HPH, HD]),
                            ALU.mult),
                        waits=[t_gm])
            ctx.bank_r["ps0"] = t_n
            kn3 = knorm[:, 0:QW].rearrange("p (h d) -> p h d", d=HD)
            kr3 = krot[:, 0:QW].rearrange("p (h d) -> p h d", d=HD)
            cs = cos_sb[:, tt * HALF:(tt + 1) * HALF].unsqueeze(1).broadcast_to([P, HPH, HALF])
            sn = sin_sb[:, tt * HALF:(tt + 1) * HALF].unsqueeze(1).broadcast_to([P, HPH, HALF])
            e3 = epi[:, 0:HPH * HALF].rearrange("p (h c) -> p h c", c=HALF)
            t_r7 = emit_rope(pg, kn3, kr3, e3, cs, sn, t_n, HALF, ROPE, HD)
            q3 = qstg.rearrange("p (a n) -> p a n", n=P)
            for h4 in range(ceil_div(HPH, 4)):
                nn = min(4, HPH - h4 * 4)
                ctx.transpose_batch(
                    [krot[:, (h4 * 4 + i) * HD:(h4 * 4 + i) * HD + P]
                     for i in range(nn)],
                    q3[:, h4 * 4:h4 * 4 + nn, :], "sp", waits=[t_r7],
                    evac_waits=[pg.tok("q_gpsimd_qs")])
            pg.dma("gpsimd", qscr[half * HPH:(half + 1) * HPH, tt].rearrange(
                "h p n -> p h n"),
                qstg[:].rearrange("p (h n) -> p h n", n=P),
                waits=[pg.tok("s_dve")], stream="qs")
        pg.barrier()

    def kt_list(qgi):
        own_band = list(range(qgi * QG // P, (qgi + 1) * QG // P))
        own_full = list(range(0, qgi * QG // P))
        tail = list(range(TT, KT))
        return own_full, own_band, tail

    avbase = lambda i: (i // 3) * 512 + (i % 3) * 129
    av_tok = [None, None]
    t_bm = pg.dma("gpsimd", bandm_sb[:], bandm.ap(), stream="c",
                  waits=[pg.tok("s_dve"), pg.tok("s_pe")])
    tb = pg.tok("s_dve")
    for h in range(NH):
        tb = pg.op("vector",
                   lambda E, h=h: E.tensor_scalar_add(bias_hk[:, h * KT:(h + 1) * KT],
                                                      kb_sb[:, 0:KT], mneg[:, h:h + 1]),
                   waits=[tb, pg.tok("s_pe"), t_bm])
    t_bias = tb

    for kv in range(NKV):
        t_kh = pg.dma("sync", KH0.rearrange("p (t n) -> p t n", n=P),
                      kscr[kv].rearrange("t p n -> p t n"),
                      waits=[pg.tok("s_pe")], stream="kh")
        t_vh = pg.dma("sync", VH0.rearrange("p (t n) -> p t n", n=129),
                      vscr[:, :, kv * 129:(kv + 1) * 129].rearrange("t p n -> p t n"),
                      waits=[pg.tok("s_pe")], stream="vh")
        for hh in range(HPK):
            h = kv * HPK + hh
            t_qh = pg.dma("sync", QH0.rearrange("p (t n) -> p t n", n=P),
                          qscr[h].rearrange("t p n -> p t n"),
                          waits=[pg.tok("s_pe")], stream="qh")
            for qgi in range(NQG):
                own_full, own_band, tail = kt_list(qgi)
                kts = own_full + own_band + tail
                first_kt = kts[0]
                last_kt = kts[-1]
                psn = ["ps0", "ps1"]
                nav_banks = ceil_div(avbase(QGT - 1) + 129, 512)
                mmz = None
                for bkk in range(nav_banks):
                    mmz = pg.op("tensor",
                                lambda E, bkk=bkk: E.matmul(
                                    ctx.pt["av"][:, bkk * 512:(bkk + 1) * 512],
                                    ctx.ident[:], ctx.zer_b[:, 0:512],
                                    start=True, stop=True),
                                waits=[ctx.bank_r["av"] if bkk == 0 else None])
                ctx.bank_w["av"] = mmz

                def emit_qk0(ki, kt, h=h, qgi=qgi):
                    ps = psn[ki % 2]
                    mm = None
                    for nb in range(QG // 512 if QG >= 512 else 1):
                        NBW = min(512, QG)
                        w_ = [t_kh, t_qh, ctx.bank_r[ps] if nb == 0 else None]
                        mm = pg.op("tensor",
                                   lambda E, ps=ps, kt=kt, nb=nb, NBW=NBW, h=h, qgi=qgi:
                                   E.matmul(ctx.pt[ps][:, nb * NBW:(nb + 1) * NBW],
                                            KH0[:, kt * P:(kt + 1) * P],
                                            QH0[:, qgi * QG + nb * NBW:
                                                qgi * QG + (nb + 1) * NBW],
                                            start=True, stop=True),
                                   waits=w_)
                    ctx.bank_w[ps] = mm
                    t_pre = mm
                    if kt in own_band:
                        s = kt * P - qgi * QG
                        off = (QG - P) - s
                        t_pre = pg.op("vector",
                                      lambda E, ps=ps, off=off: E.tensor_add(
                                          ctx.pt[ps][:, 0:QG],
                                          ctx.pt[ps][:, 0:QG],
                                          bandm_sb[:, off:off + QG]),
                                      waits=[mm, t_bm, pg.tok("s_dve")])
                    return t_pre

                pend = emit_qk0(0, kts[0])
                for ki, kt in enumerate(kts):
                    ps = psn[ki % 2]
                    PT = PTb[ki % 2]
                    t_pre = pend
                    if ki + 1 < len(kts):
                        pend = emit_qk0(ki + 1, kts[ki + 1])
                    t_e = pg.op("scalar",
                                lambda E, ps=ps, h=h, kt=kt, PT=PT: E.activation(
                                    PT[:, 0:QG], ctx.pt[ps][:, 0:QG], AF.Exp,
                                    bias=bias_hk[:, h * KT + kt:h * KT + kt + 1],
                                    scale=SC),
                                waits=[t_pre, t_bias, av_tok[ki % 2], pg.tok("s_act")])
                    ctx.bank_r[ps] = t_e
                    for i in range(QGT):
                        w_ = [t_e, t_vh, t_qh]
                        mm2 = pg.op("tensor",
                                    lambda E, i=i, kt=kt, PT=PT: E.matmul(
                                        ctx.pt["av"][:, avbase(i):avbase(i) + 129],
                                        PT[:, i * P:(i + 1) * P],
                                        VH0[:, kt * 129:(kt + 1) * 129],
                                        start=False, stop=(kt == last_kt),
                                        skip_group_check=True),
                                    waits=w_)
                    av_tok[ki % 2] = mm2
                    ctx.bank_w["av"] = mm2
                last_r = None
                for i in range(QGT):
                    t_rc2 = pg.op("vector",
                                  lambda E, i=i: E.reciprocal(
                                      rcp[:, 0:1],
                                      ctx.pt["av"][:, avbase(i) + P:avbase(i) + 129]),
                                  waits=[ctx.bank_w["av"], pg.tok("s_dve")])
                    t_y = pg.op("vector",
                                lambda E, i=i: E.tensor_scalar_mul(
                                    ystg[:, 0:P],
                                    ctx.pt["av"][:, avbase(i):avbase(i) + P],
                                    rcp[:, 0:1]),
                                waits=[t_rc2, pg.tok("s_pe")])
                    last_r = ctx.transpose(
                        ystg[:, 0:P],
                        ystag[:, i * P:(i + 1) * P],
                        "sp", waits=[t_y],
                        evac_waits=[pg.tok("q_gpsimd_ys")])
                ctx.bank_r["av"] = last_r
                pg.dma("gpsimd",
                       yscr[h, qgi * QGT:(qgi + 1) * QGT].rearrange("t p n -> p t n"),
                       ystag[:, 0:QGT * P].rearrange("p (t n) -> p t n", n=P),
                       waits=[last_r], stream="ys")
    pg.barrier()

    for oc in range(D // P):
        b = wld[oc % 2]
        t_l = pg.dma("sync", b[:, 0:D], Wo[oc * P:(oc + 1) * P, :],
                     waits=[pg.tok("s_dve")], stream=f"w{oc % 2}")
        t_c = pg.op("vector",
                    lambda E, b=b: E.tensor_scalar_mul(wldb[:, 0:D], b[:, 0:D],
                                                       scbc[:, 3:4]),
                    waits=[t_l, pg.tok("s_dve")])
        wo3 = wo_big.rearrange("p (a n) -> p a n", n=D)
        for h8 in range(ceil_div(DCH, 8)):
            nn = min(8, DCH - h8 * 8)
            ctx.transpose_batch(
                [wldb[:, (h8 * 8 + i) * P:(h8 * 8 + i + 1) * P]
                 for i in range(nn)],
                wo3[:, h8 * 8:h8 * 8 + nn, oc * P:(oc + 1) * P],
                "sp", waits=[t_c], evac_waits=[pg.tok("s_pe")])
    pg.barrier()
    out_toks = []
    OH = D // 2
    ybufs = [ctx.view(AR2, 8192 + 16 * D, NH * P, BF16)] * 2
    for tt in range(TT):
        yb = ybufs[tt % 2]
        t_yl = pg.dma("sync", yb[:].rearrange("p (h n) -> p h n", n=P),
                      yscr[:, tt].rearrange("h p n -> p h n"),
                      waits=[pg.tok("s_pe")], stream=f"yl{tt % 2}")
        yfm = lambda dc, yb=yb: yb[:, dc * P:(dc + 1) * P]
        mm = None
        for dc in range(DCH):
            mm = pg.op("tensor",
                       lambda E, dc=dc, yfm=yfm, st=(dc == 0), sp_=(dc == DCH - 1):
                       E.matmul(ctx.pt["av"][0:R, 0:P],
                                lA["o"][:, dc * R:(dc + 1) * R],
                                yfm(dc), start=st, stop=sp_),
                       waits=[t_yl, ctx.bank_r["av"] if dc == 0 else None])
        ctx.bank_w["av"] = mm
        tev = pg.op("vector",
                    lambda E: E.tensor_copy(tlora[0:R, 0:P], ctx.pt["av"][0:R, 0:P]),
                    waits=[mm, pg.tok("s_act")])
        ctx.bank_r["av"] = tev
        NBO = ceil_div(OH, 512)
        OBW = min(OH, 512)
        for halfo, ps in enumerate(("ps0", "ps1")):
            mm = None
            for dc in range(DCH):
                for nb in range(NBO):
                    st = dc == 0
                    w_ = [t_yl]
                    if st and nb == 0:
                        w_.append(ctx.bank_r[ps])
                    mm = pg.op("tensor",
                               lambda E, dc=dc, nb=nb, halfo=halfo, ps=ps, st=st, yfm=yfm:
                               E.matmul(ctx.pt[ps][:, nb * OBW:(nb + 1) * OBW],
                                        yfm(dc),
                                        wo_big[:, dc * D + halfo * OH + nb * OBW:
                                               dc * D + halfo * OH + (nb + 1) * OBW],
                                        start=st, stop=False),
                               waits=w_)
            for nb in range(NBO):
                mm = pg.op("tensor",
                           lambda E, nb=nb, halfo=halfo, ps=ps:
                           E.matmul(ctx.pt[ps][:, nb * OBW:(nb + 1) * OBW],
                                    tlora[0:R, 0:P],
                                    lB["o"][0:R, halfo * OH + nb * OBW:
                                            halfo * OH + (nb + 1) * OBW],
                                    start=False, stop=True),
                           waits=[tev])
            ctx.bank_w[ps] = mm
        bx = xres[tt % 2]
        t_rx = pg.dma("sync", bx[:, 0:D], xb[tt * P:(tt + 1) * P, :],
                      waits=[pg.tok("s_dve")], stream=f"r{tt % 2}")
        t_s = None
        for halfo, ps in enumerate(("ps0", "ps1")):
            t_s = pg.op("vector",
                        lambda E, ps=ps, halfo=halfo, bx=bx: E.tensor_add(
                            sumb[:, halfo * OH:(halfo + 1) * OH],
                            ctx.pt[ps][:, 0:OH],
                            bx[:, halfo * OH:(halfo + 1) * OH]),
                        waits=[ctx.bank_w[ps], t_rx, pg.tok("s_dve")])
            ctx.bank_r[ps] = t_s
        ob = obuf[tt % 2]
        t_o = emit_layernorm(ctx, sumb, D, stat, ob[:, 0:D],
                             waits=[t_s, out_toks[-1] if out_toks else None],
                             sq_scratch=lnsq)
        t_out = pg.dma("sync", out[tt * P:(tt + 1) * P, :], ob[:, 0:D],
                       waits=[t_o], stream=f"o{tt % 2}")
        out_toks.append(t_out)

    if standalone:
        return ctx.finish(pg.all_dma_toks())
    return None





def build_l1_attn(D=2048, NH=16, HD=128, KVC=512, QC=1024, S=8192, T=2048,
                  ctx=None, ext=None, split=False):
    DCH = D // P
    KT = S // P
    TT = T // P
    LTC = KVC // P
    QCC = QC // P
    QG = min(1024, T)
    NQG = T // QG
    QGT = QG // P
    HALF1 = HD // 2
    SC = 1.0 / float(np.sqrt(2 * HD))
    NOB = ceil_div(D, 512)

    standalone = ctx is None
    if standalone:
        ctx = Ctx(psum_spec=[("ps0", [P, 1024]), ("ps1", [P, 1024]),
                             ("av", [P, 1536]), ("sp", [P, 512])])
    pg = ctx.pg
    EX = (lambda n, sh: ctx.inp(n, sh)) if ext is None else (lambda n, sh: ext[n])

    if split:
        xloc = ext["xloc"]
        xg = ext["xg"]
        xg_bf = True
        cos1q = ext["cos1q"]
        sin1q = ext["sin1q"]
        cc_tok = ext.get("cc_tok")
    else:
        xb = EX("xb", (S, D))
        xloc = xb
        xg = xb
        xg_bf = False
        cos1q = None
    Wkvd = EX("w_kv_down", (D, KVC))
    Wupk = EX("w_kv_up_k", (KVC, D))
    Wupv = EX("w_kv_up_v", (KVC, D))
    Wqd = EX("w_q_down", (D, QC))
    Wqu = EX("w_q_up", (QC, D))
    Wkr = EX("w_k_rope", (D, HD))
    Wqr = EX("w_q_rope", (D, HD))
    Wout = EX("w_out", (D, D))
    cos1 = EX("cos1", (S, HALF1))
    sin1 = EX("sin1", (S, HALF1))
    out = ctx.outp("out", (T, D)) if ext is None else ext["out"]
    yscr = ctx.scratch("yscr1", [NH, TT, P, P], BF16)

    if standalone:
        ctx.setup_consts()

    ARXB = max(2 * DCH * QC + 2 * DCH * P + 2 * QC + 128,
               2 * DCH * KVC + 2 * DCH * HD + 2 * DCH * 512 + 2048 + 4 * HALF1 + 64,
               2 * KT * P + 2 * KT * 129 + 2 * (2 * LTC + QCC) * HD
               + 4 * QG + 2 * P + 2 * QGT * P + 256,
               16 * D + 2 * NH * P + 256)
    ARX = ctx.arena("ARX1", ARXB)
    ARY = ctx.arena("ARY1", 2 * max(LTC * S, DCH * D))
    lat_fm = ctx.view(ARY, 0, LTC * S, BF16)
    wo_sb = ctx.view(ARY, 0, DCH * D, BF16)
    wqd_sb = ctx.view(ARX, 0, DCH * QC, BF16)
    xstg_c1 = ctx.view(ARX, 2 * DCH * QC, DCH * P, BF16)
    wkvd_sb = ctx.view(ARX, 0, DCH * KVC, BF16)
    wkr_sb = ctx.view(ARX, 2 * DCH * KVC, DCH * HD, BF16)
    xstg_c2 = ctx.view(ARX, 2 * DCH * (KVC + HD), DCH * 512, BF16)
    KH0 = ctx.view(ARX, 0, KT * P, BF16)
    VH0 = ctx.view(ARX, 2 * KT * P, KT * 129, BF16)
    upk_h = ctx.view(ARX, 2 * (KT * P + KT * 129), LTC * HD, BF16)
    upv_h = ctx.view(ARX, 2 * (KT * P + KT * 129) + 2 * LTC * HD, LTC * HD, BF16)
    wqu_h = ctx.view(ARX, 2 * (KT * P + KT * 129) + 4 * LTC * HD, QCC * HD, BF16)
    _c3t = 2 * (KT * P + KT * 129) + 4 * LTC * HD + 2 * QCC * HD
    xres = ctx.view(ARX, 0, D, F32)
    sumb = ctx.view(ARX, 4 * D, D, F32)
    lnsq = ctx.view(ARX, 8 * D, D, F32)
    obuf = ctx.view(ARX, 12 * D, D, F32)
    Y_t = ctx.view(ARX, 16 * D, NH * P, BF16)

    t1fm = ctx.sb("t1fm1", [P, QCC * T], BF16)
    kr_fm = ctx.sb("kr_fm1", [P, S], BF16)
    qr_fm = ctx.sb("qr_fm1", [P, T], BF16)
    qch = ctx.sb("qch1", [P, T], BF16)
    cos_sb = ctx.sb("cos1_sb", [P, KT * HALF1], BF16)
    sin_sb = ctx.sb("sin1_sb", [P, KT * HALF1], BF16)
    cosq_sb, sinq_sb = cos_sb, sin_sb
    wqr_sb = ctx.sb("wqr_sb1", [P, DCH * HD], BF16)
    xrow = [ctx.sb(f"xrow1_{i}", [P, D], BF16) for i in range(2)]
    t1tok = ctx.view(ARX, ((2 * DCH * QC + 2 * DCH * P + 63) // 64) * 64, QC, BF16)
    qrtok = ctx.sb("qrtok1", [P, HD], F32)
    qrrot = ctx.sb("qrrot1", [P, HD], F32)
    krtok = qrtok
    krrot = qrrot
    krstg = ctx.view(ARX, 2 * DCH * (KVC + HD) + 2 * DCH * 512, 512, F32)
    _rop_off = ((max(2 * DCH * QC + 2 * DCH * P,
                     2 * DCH * (KVC + HD) + 2 * DCH * 512 + 2048) + 63) // 64) * 64
    ropes = ctx.view(ARX, _rop_off, HALF1, F32)
    sqrow = ctx.sb("sqrow1", [P, 512], BF16)
    sca = ctx.sb("sca1", [1, 16], F32)
    mb_h = ctx.sb("mb_h1", [P, 1], F32)
    PTb1 = [ctx.view(ARX, _c3t + i * 2 * QG, QG, BF16) for i in range(2)]
    ystg = ctx.view(ARX, _c3t + 4 * QG, P, BF16)
    ystag = ctx.view(ARX, _c3t + 4 * QG + 2 * P, QGT * P, BF16)
    rcp = ctx.sb("rcp1", [P, 1], F32)
    stat = ctx.sb("stat1", [P, 8], F32)

    pg.dma("gpsimd", wqd_sb.rearrange("p (c n) -> p c n", n=QC),
           Wqd.ap().rearrange("(c p) n -> p c n", p=P), stream="w0")
    pg.dma("gpsimd", wqr_sb.rearrange("p (c n) -> p c n", n=HD),
           Wqr.ap().rearrange("(c p) n -> p c n", p=P), stream="w1")
    if split:
        pg.dma("gpsimd", cos_sb[:, 0:TT * HALF1].rearrange("p (t c) -> p t c", c=HALF1),
               cos1q.ap().rearrange("(t p) c -> p t c", p=P), stream="w2")
        pg.dma("gpsimd", sin_sb[:, 0:TT * HALF1].rearrange("p (t c) -> p t c", c=HALF1),
               sin1q.ap().rearrange("(t p) c -> p t c", p=P), stream="w3")
    else:
        pg.dma("gpsimd", cos_sb[:].rearrange("p (t c) -> p t c", c=HALF1),
               cos1.ap().rearrange("(t p) c -> p t c", p=P), stream="w2")
        pg.dma("gpsimd", sin_sb[:].rearrange("p (t c) -> p t c", c=HALF1),
               sin1.ap().rearrange("(t p) c -> p t c", p=P), stream="w3")
    pg.barrier()

    def rope1(src, dst, tl, t_in, tabs=None):
        cs_t, sn_t = tabs if tabs is not None else (cos_sb, sin_sb)
        s2 = src.rearrange("p (c two) -> p c two", two=2)
        d2 = dst.rearrange("p (c two) -> p c two", two=2)
        cs = cs_t[:, tl * HALF1:(tl + 1) * HALF1]
        sn = sn_t[:, tl * HALF1:(tl + 1) * HALF1]
        e2 = ropes[:, 0:HALF1]
        t = pg.op("vector", lambda E: E.tensor_tensor(
            d2[:, :, 0], s2[:, :, 0], cs, ALU.mult), waits=[t_in, pg.tok("s_dve")])
        t = pg.op("vector", lambda E: E.tensor_tensor(
            e2, s2[:, :, 1], sn, ALU.mult), waits=[t])
        t = pg.op("vector", lambda E: E.tensor_tensor(
            d2[:, :, 0], d2[:, :, 0], e2, ALU.subtract), waits=[t])
        t = pg.op("vector", lambda E: E.tensor_tensor(
            d2[:, :, 1], s2[:, :, 1], cs, ALU.mult), waits=[t])
        t = pg.op("vector", lambda E: E.tensor_tensor(
            e2, s2[:, :, 0], sn, ALU.mult), waits=[t])
        t = pg.op("vector", lambda E: E.tensor_tensor(
            d2[:, :, 1], d2[:, :, 1], e2, ALU.add), waits=[t])
        return t

    def max_norm_pass(src_bf, width, out_col, t_in, init):
        t = t_in
        if init:
            t = pg.op("vector", lambda E: E.memset(sca[0:1, out_col:out_col + 1], 0.0),
                      waits=[t_in, pg.tok("s_dve")])
        for off in range(0, width, 512):
            w = min(512, width - off)
            t_sq = pg.op("scalar",
                         lambda E, off=off, w=w: E.activation(
                             sqrow[:, 0:w], src_bf[:, off:off + w], AF.Square),
                         waits=[t, pg.tok("s_act"), pg.tok("s_dve")])
            t_mm = pg.op("tensor",
                         lambda E, w=w: E.matmul(ctx.pt["sp"][0:1, 0:w],
                                                 ctx.ones_col_b[:], sqrow[:, 0:w],
                                                 start=True, stop=True),
                         waits=[t_sq, ctx.bank_r["sp"]])
            ctx.bank_w["sp"] = t_mm
            t_mx = pg.op("vector",
                         lambda E, w=w: E.reduce_max(sca[0:1, 15:16],
                                                     ctx.pt["sp"][0:1, 0:w], axis=AX.X),
                         waits=[t_mm])
            ctx.bank_r["sp"] = t_mx
            t = pg.op("vector",
                      lambda E: E.tensor_max(sca[0:1, out_col:out_col + 1],
                                             sca[0:1, out_col:out_col + 1],
                                             sca[0:1, 15:16]),
                      waits=[t_mx])
        return t

    for tt in range(TT):
        xr = xrow[tt % 2]
        t_lx = pg.dma("gpsimd", xr[:, 0:D], xloc[tt * P:(tt + 1) * P, :],
                      waits=[pg.tok("s_pe")], stream=f"x{tt % 2}")
        x3d_c1 = xstg_c1.rearrange("p (a n) -> p a n", n=P)
        for h8 in range(ceil_div(DCH, 8)):
            nn = min(8, DCH - h8 * 8)
            ctx.transpose_batch(
                [xr[:, (h8 * 8 + i) * P:(h8 * 8 + i + 1) * P] for i in range(nn)],
                x3d_c1[:, h8 * 8:h8 * 8 + nn, :],
                "sp" if h8 % 2 == 0 else "ps1", waits=[t_lx],
                evac_waits=[pg.tok("s_pe")])
        t_x = pg.tok("s_dve")
        NBQ = ceil_div(QC, 512)
        QBW = min(QC, 512)
        mm = None
        for dc in range(DCH):
            for nb in range(NBQ):
                st = dc == 0
                w_ = [t_x]
                if st and nb == 0:
                    w_.append(ctx.bank_r["ps0"])
                mm = pg.op("tensor",
                           lambda E, dc=dc, nb=nb, st=st:
                           E.matmul(ctx.pt["ps0"][:, nb * QBW:(nb + 1) * QBW],
                                    xstg_c1[:, dc * P:(dc + 1) * P],
                                    wqd_sb[:, dc * QC + nb * QBW: dc * QC + (nb + 1) * QBW],
                                    start=st, stop=(dc == DCH - 1)),
                           waits=w_)
        ctx.bank_w["ps0"] = mm
        t_ev = pg.op("vector", lambda E: E.tensor_copy(t1tok[:, 0:QC],
                                                       ctx.pt["ps0"][:, 0:QC]),
                     waits=[mm, pg.tok("s_dve")])
        ctx.bank_r["ps0"] = t_ev
        t13d = t1fm.rearrange("p (a n) -> p a n", n=T)
        for h8 in range(ceil_div(QCC, 8)):
            nn = min(8, QCC - h8 * 8)
            ctx.transpose_batch(
                [t1tok[:, (h8 * 8 + i) * P:(h8 * 8 + i + 1) * P] for i in range(nn)],
                t13d[:, h8 * 8:h8 * 8 + nn, tt * P:(tt + 1) * P],
                "ps1", waits=[t_ev], evac_waits=[pg.tok("s_pe")])
        mm = None
        for dc in range(DCH):
            st = dc == 0
            w_ = [t_x]
            if st:
                w_.append(ctx.bank_r["av"])
            mm = pg.op("tensor",
                       lambda E, dc=dc, st=st:
                       E.matmul(ctx.pt["av"][:, 0:HD], xstg_c1[:, dc * P:(dc + 1) * P],
                                wqr_sb[:, dc * HD:(dc + 1) * HD],
                                start=st, stop=(dc == DCH - 1)),
                       waits=w_)
        ctx.bank_w["av"] = mm
        t_qr = pg.op("vector", lambda E: E.tensor_copy(qrtok[:, 0:HD],
                                                       ctx.pt["av"][:, 0:HD]),
                     waits=[mm, pg.tok("s_dve")])
        ctx.bank_r["av"] = t_qr
        t_ro = rope1(qrtok[:, 0:HD], qrrot[:, 0:HD], tt, t_qr,
                     tabs=(cosq_sb, sinq_sb))
        ctx.transpose(qrrot[:, 0:HD], qr_fm[:, tt * P:(tt + 1) * P], "sp",
                      waits=[t_ro])
    t_q2m = max_norm_pass(qr_fm[:], T, 0, pg.tok("s_dve"), True)
    pg.barrier()

    pg.dma("gpsimd", wkvd_sb.rearrange("p (c n) -> p c n", n=KVC),
           Wkvd.ap().rearrange("(c p) n -> p c n", p=P), stream="w0")
    pg.dma("gpsimd", wkr_sb.rearrange("p (c n) -> p c n", n=HD),
           Wkr.ap().rearrange("(c p) n -> p c n", p=P), stream="w1")
    if split:
        pg.dma("gpsimd", cos_sb[:].rearrange("p (t c) -> p t c", c=HALF1),
               cos1.ap().rearrange("(t p) c -> p t c", p=P), stream="w2")
        pg.dma("gpsimd", sin_sb[:].rearrange("p (t c) -> p t c", c=HALF1),
               sin1.ap().rearrange("(t p) c -> p t c", p=P), stream="w3")
    pg.barrier()
    for blk in range(S // 512):
        for j in range(4):
            kt = blk * 4 + j
            xr = xrow[j % 2]
            t_lx = pg.dma("sync" if xg_bf else "gpsimd",
                          xr[:, 0:D], xg[kt * P:(kt + 1) * P, :],
                          waits=[pg.tok("s_pe"),
                                 cc_tok if split else None],
                          stream=f"x{j % 2}")
            x3d_c2 = xstg_c2.rearrange("p (a n) -> p a n", n=512)
            for h8 in range(ceil_div(DCH, 8)):
                nn = min(8, DCH - h8 * 8)
                ctx.transpose_batch(
                    [xr[:, (h8 * 8 + i) * P:(h8 * 8 + i + 1) * P]
                     for i in range(nn)],
                    x3d_c2[:, h8 * 8:h8 * 8 + nn, j * P:(j + 1) * P],
                    "sp", waits=[t_lx], evac_waits=[pg.tok("s_pe")])
        t_x = pg.tok("s_dve")
        for lt in range(LTC):
            psn, off = [("ps0", 0), ("ps0", 512), ("ps1", 0), ("ps1", 512)][lt % 4]
            mm = None
            for dc in range(DCH):
                st = dc == 0
                w_ = [t_x]
                if st:
                    w_.append(ctx.bank_r[psn])
                mm = pg.op("tensor",
                           lambda E, dc=dc, lt=lt, psn=psn, off=off, st=st:
                           E.matmul(ctx.pt[psn][:, off:off + 512],
                                    wkvd_sb[:, dc * KVC + lt * P: dc * KVC + (lt + 1) * P],
                                    xstg_c2[:, dc * 512:(dc + 1) * 512],
                                    start=st, stop=(dc == DCH - 1)),
                           waits=w_)
            ctx.bank_w[psn] = mm
            t_ev = pg.op("vector",
                         lambda E, lt=lt, blk=blk, psn=psn, off=off: E.tensor_copy(
                             lat_fm[:, lt * S + blk * 512:lt * S + (blk + 1) * 512],
                             ctx.pt[psn][:, off:off + 512]),
                         waits=[mm, pg.tok("s_dve")])
            ctx.bank_r[psn] = t_ev
        mm = None
        for dc in range(DCH):
            st = dc == 0
            w_ = [t_x]
            if st:
                w_.append(ctx.bank_r["av"])
            mm = pg.op("tensor",
                       lambda E, dc=dc, st=st:
                       E.matmul(ctx.pt["av"][:, 0:512],
                                wkr_sb[:, dc * HD:(dc + 1) * HD],
                                xstg_c2[:, dc * 512:(dc + 1) * 512],
                                start=st, stop=(dc == DCH - 1)),
                       waits=w_)
        ctx.bank_w["av"] = mm
        t_evk = pg.op("vector",
                      lambda E: E.tensor_copy(krstg[:, 0:512], ctx.pt["av"][:, 0:512]),
                      waits=[mm, pg.tok("s_dve"), pg.tok("s_pe")])
        ctx.bank_r["av"] = t_evk
        for j in range(4):
            kt = blk * 4 + j
            t_t = ctx.transpose(krstg[:, j * P:(j + 1) * P], krtok[:, 0:P], "sp",
                                waits=[t_evk], evac_waits=[pg.tok("s_pe")])
            t_ro = rope1(krtok[:, 0:P], krrot[:, 0:P], kt, t_t)
            ctx.transpose(krrot[:, 0:P], kr_fm[:, kt * P:(kt + 1) * P], "sp",
                          waits=[t_ro], evac_waits=[pg.tok("s_pe")])
    t_k2m = max_norm_pass(kr_fm[:], S, 1, pg.tok("s_dve"), True)
    pg.barrier()

    avbase = lambda i: (i // 3) * 512 + (i % 3) * 129
    av_tok1 = [None, None]
    for h in range(NH):
        t_wq = pg.dma("gpsimd", wqu_h.rearrange("p (c n) -> p c n", n=HD),
                      Wqu[:, h * HD:(h + 1) * HD].rearrange("(c p) n -> p c n", p=P),
                      waits=[pg.tok("s_pe")], stream="wh0")
        t_uk = pg.dma("gpsimd", upk_h.rearrange("p (c n) -> p c n", n=HD),
                      Wupk[:, h * HD:(h + 1) * HD].rearrange("(c p) n -> p c n", p=P),
                      waits=[pg.tok("s_pe")], stream="wh1")
        t_uv = pg.dma("gpsimd", upv_h.rearrange("p (c n) -> p c n", n=HD),
                      Wupv[:, h * HD:(h + 1) * HD].rearrange("(c p) n -> p c n", p=P),
                      waits=[pg.tok("s_pe")], stream="wh2")
        QB2 = min(512, T)
        t_qn = pg.op("vector", lambda E: E.memset(sca[0:1, 2:3], 0.0),
                     waits=[pg.tok("s_dve")])
        for qb in range(ceil_div(T, 512)):
            psn = ["ps0", "ps1"][qb % 2]
            mm = None
            for c in range(QCC):
                st = c == 0
                w_ = [t_wq]
                if st:
                    w_.append(ctx.bank_r[psn])
                mm = pg.op("tensor",
                           lambda E, c=c, qb=qb, psn=psn, st=st:
                           E.matmul(ctx.pt[psn][:, 0:QB2],
                                    wqu_h[:, c * HD:(c + 1) * HD],
                                    t1fm[:, c * T + qb * QB2: c * T + (qb + 1) * QB2],
                                    start=st, stop=(c == QCC - 1)),
                           waits=w_)
            ctx.bank_w[psn] = mm
            t_ev = pg.op("vector",
                         lambda E, qb=qb, psn=psn: E.tensor_copy(
                             qch[:, qb * QB2:(qb + 1) * QB2], ctx.pt[psn][:, 0:QB2]),
                         waits=[mm, pg.tok("s_dve")])
            ctx.bank_r[psn] = t_ev
            t_qn = max_norm_pass(qch[:, qb * QB2:(qb + 1) * QB2], QB2, 2, t_ev, False)
        t_kn = pg.op("vector", lambda E: E.memset(sca[0:1, 3:4], 0.0),
                     waits=[pg.tok("s_dve")])
        for sb_ in range(S // 512):
            psn = ["ps0", "ps1"][sb_ % 2]
            mm = None
            for lt in range(LTC):
                st = lt == 0
                w_ = [t_uk]
                if st:
                    w_.append(ctx.bank_r[psn])
                mm = pg.op("tensor",
                           lambda E, lt=lt, sb_=sb_, psn=psn, st=st:
                           E.matmul(ctx.pt[psn][:, 0:512],
                                    upk_h[:, lt * HD:(lt + 1) * HD],
                                    lat_fm[:, lt * S + sb_ * 512: lt * S + (sb_ + 1) * 512],
                                    start=st, stop=(lt == LTC - 1)),
                           waits=w_)
            ctx.bank_w[psn] = mm
            t_ev = pg.op("vector",
                         lambda E, sb_=sb_, psn=psn: E.tensor_copy(
                             KH0[:, sb_ * 512:(sb_ + 1) * 512], ctx.pt[psn][:, 0:512]),
                         waits=[mm, pg.tok("s_dve")])
            ctx.bank_r[psn] = t_ev
            t_kn = max_norm_pass(KH0[:, sb_ * 512:(sb_ + 1) * 512], 512, 3, t_ev, False)
        t_m = pg.op("vector", lambda E: E.tensor_add(sca[0:1, 4:5], sca[0:1, 2:3],
                                                     sca[0:1, 0:1]),
                    waits=[t_qn, t_q2m])
        t_m = pg.op("vector", lambda E: E.tensor_add(sca[0:1, 5:6], sca[0:1, 3:4],
                                                     sca[0:1, 1:2]),
                    waits=[t_kn, t_k2m, t_m])
        t_m = pg.op("scalar", lambda E: E.sqrt(sca[0:1, 6:7], sca[0:1, 4:5]),
                    waits=[t_m, pg.tok("s_act")])
        t_m = pg.op("scalar", lambda E: E.sqrt(sca[0:1, 7:8], sca[0:1, 5:6]),
                    waits=[t_m])
        t_m = pg.op("vector", lambda E: E.tensor_mul(sca[0:1, 8:9], sca[0:1, 6:7],
                                                     sca[0:1, 7:8]),
                    waits=[t_m, pg.tok("s_dve")])
        t_m = pg.op("vector", lambda E: E.tensor_scalar_mul(sca[0:1, 9:10],
                                                            sca[0:1, 8:9], -SC),
                    waits=[t_m])
        t_mb = pg.op("tensor", lambda E: E.matmul(ctx.pt["sp"][:, 0:1],
                                                  ctx.ones_row_f[:], sca[0:1, 9:10],
                                                  start=True, stop=True),
                     waits=[t_m, ctx.bank_r["sp"]])
        ctx.bank_w["sp"] = t_mb
        t_mb2 = pg.op("vector", lambda E: E.tensor_copy(mb_h[:, 0:1],
                                                        ctx.pt["sp"][:, 0:1]),
                      waits=[t_mb])
        ctx.bank_r["sp"] = t_mb2
        t_vi = pg.op("gpsimd", lambda E: E.memset(VH0, 1.0),
                     waits=[pg.tok("s_pe"), pg.tok("q_gpsimd_ys")])
        vb_tok = [None, None]
        for kt in range(KT):
            po = [0, 512][kt % 2]
            mm = None
            for lt in range(LTC):
                st = lt == 0
                w_ = [t_uv, t_vi]
                if st:
                    w_.append(vb_tok[kt % 2])
                    if kt < 2:
                        w_.append(ctx.bank_r["av"])
                mm = pg.op("tensor",
                           lambda E, lt=lt, kt=kt, st=st, po=po:
                           E.matmul(ctx.pt["av"][:, po:po + HD],
                                    lat_fm[:, lt * S + kt * P: lt * S + (kt + 1) * P],
                                    upv_h[:, lt * HD:(lt + 1) * HD],
                                    start=st, stop=(lt == LTC - 1)),
                           waits=w_)
            ctx.bank_w["av"] = mm
            t_ev = pg.op("vector",
                         lambda E, kt=kt, po=po: E.tensor_copy(
                             VH0[:, kt * 129:kt * 129 + P], ctx.pt["av"][:, po:po + HD]),
                         waits=[mm, pg.tok("s_dve")])
            vb_tok[kt % 2] = t_ev
            ctx.bank_r["av"] = t_ev
        for qgi in range(NQG):
            nav_banks = ceil_div(avbase(QGT - 1) + 129, 512)
            mmz = None
            for bkk in range(nav_banks):
                mmz = pg.op("tensor",
                            lambda E, bkk=bkk: E.matmul(
                                ctx.pt["av"][:, bkk * 512:(bkk + 1) * 512],
                                ctx.ident[:], ctx.zer_b[:, 0:512],
                                start=True, stop=True),
                            waits=[ctx.bank_r["av"] if bkk == 0 else None])
            ctx.bank_w["av"] = mmz
            psn = ["ps0", "ps1"]

            def emit_qk1(kt, qgi=qgi):
                ps = psn[kt % 2]
                mm = None
                for nb in range(ceil_div(QG, 512)):
                    NBW = min(512, QG)
                    w_ = [pg.tok("s_dve")]
                    if nb == 0:
                        w_.append(ctx.bank_r[ps])
                    mm = pg.op("tensor",
                               lambda E, ps=ps, kt=kt, nb=nb, NBW=NBW, qgi=qgi:
                               E.matmul(ctx.pt[ps][:, nb * NBW:(nb + 1) * NBW],
                                        KH0[:, kt * P:(kt + 1) * P],
                                        qch[:, qgi * QG + nb * NBW:
                                            qgi * QG + (nb + 1) * NBW],
                                        start=True, stop=False),
                               waits=w_)
                    mm = pg.op("tensor",
                               lambda E, ps=ps, kt=kt, nb=nb, NBW=NBW, qgi=qgi:
                               E.matmul(ctx.pt[ps][:, nb * NBW:(nb + 1) * NBW],
                                        kr_fm[:, kt * P:(kt + 1) * P],
                                        qr_fm[:, qgi * QG + nb * NBW:
                                              qgi * QG + (nb + 1) * NBW],
                                        start=False, stop=True),
                               waits=[])
                ctx.bank_w[ps] = mm
                return mm

            pend = emit_qk1(0)
            for kt in range(KT):
                ps = psn[kt % 2]
                PT = PTb1[kt % 2]
                mm = pend
                if kt + 1 < KT:
                    pend = emit_qk1(kt + 1)
                t_e = pg.op("scalar",
                            lambda E, ps=ps, PT=PT: E.activation(
                                PT[:, 0:QG], ctx.pt[ps][:, 0:QG], AF.Exp,
                                bias=mb_h[:, 0:1], scale=SC),
                            waits=[mm, t_mb2, av_tok1[kt % 2], pg.tok("s_act")])
                ctx.bank_r[ps] = t_e
                for i in range(QGT):
                    mm2 = pg.op("tensor",
                                lambda E, i=i, kt=kt, PT=PT: E.matmul(
                                    ctx.pt["av"][:, avbase(i):avbase(i) + 129],
                                    PT[:, i * P:(i + 1) * P],
                                    VH0[:, kt * 129:(kt + 1) * 129],
                                    start=False, stop=(kt == KT - 1),
                                    skip_group_check=True),
                                waits=[t_e])
                av_tok1[kt % 2] = mm2
                ctx.bank_w["av"] = mm2
            last_r = None
            for i in range(QGT):
                t_rc2 = pg.op("vector",
                              lambda E, i=i: E.reciprocal(
                                  rcp[:, 0:1],
                                  ctx.pt["av"][:, avbase(i) + P:avbase(i) + 129]),
                              waits=[ctx.bank_w["av"], pg.tok("s_dve")])
                t_y = pg.op("vector",
                            lambda E, i=i: E.tensor_scalar_mul(
                                ystg[:, 0:P],
                                ctx.pt["av"][:, avbase(i):avbase(i) + P],
                                rcp[:, 0:1]),
                            waits=[t_rc2, pg.tok("s_pe")])
                last_r = ctx.transpose(
                    ystg[:, 0:P], ystag[:, i * P:(i + 1) * P],
                    "sp", waits=[t_y], evac_waits=[pg.tok("q_gpsimd_ys")])
            ctx.bank_r["av"] = last_r
            pg.dma("gpsimd",
                   yscr[h, qgi * QGT:(qgi + 1) * QGT].rearrange("t p n -> p t n"),
                   ystag[:, 0:QGT * P].rearrange("p (t n) -> p t n", n=P),
                   waits=[last_r], stream="ys")
    pg.barrier()

    t_wo = pg.dma("gpsimd", wo_sb.rearrange("p (c n) -> p c n", n=D),
                  Wout.ap().rearrange("(c p) n -> p c n", p=P), stream="wo")
    out_toks = []
    for tt in range(TT):
        t_yl = pg.dma("sync", Y_t.rearrange("p (h n) -> p h n", n=P),
                      yscr[:, tt].rearrange("h p n -> p h n"),
                      waits=[pg.tok("s_pe")], stream="yl")
        mm = None
        OBW = min(512, D)
        for dc in range(DCH):
            for nb in range(NOB):
                psn, off = ("ps0", nb * OBW) if nb * OBW < 1024 else ("ps1", nb * OBW - 1024)
                st = dc == 0
                w_ = [t_yl, t_wo]
                if st:
                    w_.append(ctx.bank_r[psn])
                mm = pg.op("tensor",
                           lambda E, dc=dc, nb=nb, psn=psn, off=off, st=st, OBW=OBW:
                           E.matmul(ctx.pt[psn][:, off:off + OBW],
                                    Y_t[:, dc * P:(dc + 1) * P],
                                    wo_sb[:, dc * D + nb * OBW: dc * D + (nb + 1) * OBW],
                                    start=st, stop=(dc == DCH - 1)),
                           waits=w_)
        ctx.bank_w["ps0"] = mm
        if D > 1024:
            ctx.bank_w["ps1"] = mm
        t_rx = pg.dma("sync", xres, xloc[tt * P:(tt + 1) * P, :],
                      waits=[pg.tok("s_dve")], stream="r0")
        t_s = None
        HW = min(D, 1024)
        for halfo in range(ceil_div(D, 1024)):
            psn = ("ps0", "ps1")[halfo]
            t_s = pg.op("vector",
                        lambda E, psn=psn, halfo=halfo: E.tensor_add(
                            sumb[:, halfo * HW:(halfo + 1) * HW],
                            ctx.pt[psn][:, 0:HW],
                            xres[:, halfo * HW:(halfo + 1) * HW]),
                        waits=[mm, t_rx, pg.tok("s_dve")])
            ctx.bank_r[psn] = t_s
        t_o = emit_layernorm(ctx, sumb, D, stat, obuf[:, 0:D],
                             waits=[t_s, out_toks[-1] if out_toks else None],
                             sq_scratch=lnsq)
        t_out = pg.dma("sync", out[tt * P:(tt + 1) * P, :], obuf[:, 0:D],
                       waits=[t_o], stream=f"o{tt % 2}")
        out_toks.append(t_out)

    if standalone:
        return ctx.finish(pg.all_dma_toks())
    return None





def build_l1_ffn(D=2048, NE=32, FE=768, TOPK=6, FSH=1536, T=2048, ctx=None, ext=None):
    DCH = D // P
    TT = T // P
    TOKH = min(1024, T)
    NH_ = T // TOKH
    HT = TOKH // P
    FEC = FE // P
    FSC = FSH // P
    NB = ceil_div(D, 512)
    DBW = min(512, D)
    TB = min(512, TOKH)
    NTB = TOKH // TB

    standalone = ctx is None
    if standalone:
        ctx = Ctx()
    pg = ctx.pg
    EX = (lambda n, sh: ctx.inp(n, sh)) if ext is None else (lambda n, sh: ext[n])

    xs = EX("xslab", (T, D))
    gw = EX("gate_w", (D, NE))
    wg = EX("wg", (NE, D, FE))
    wu = EX("wu", (NE, D, FE))
    wd = EX("wd", (NE, FE, D))
    sg = EX("sg", (D, FSH))
    su = EX("su", (D, FSH))
    sd = EX("sd", (FSH, D))
    out = ctx.outp("out", (T, D)) if ext is None else ext["out"]

    if standalone:
        ctx.setup_consts()

    acc = ctx.sb("acc", [P, HT * D], F32)
    x_fm = ctx.sb("x_fm", [P, DCH * TOKH], BF16)
    ARW = ctx.arena("ARW", max(2 * (DCH * FE) * 3,
                               2 * (FSC * TOKH) + 4 * D * 2 + 2 * DCH * P * 2,
                               16 * D) + 256)
    wgbuf = ctx.view(ARW, 0, DCH * FE, BF16)
    wubuf = ctx.view(ARW, 2 * DCH * FE, DCH * FE, BF16)
    wdbuf = ctx.view(ARW, 4 * DCH * FE, FEC * D, BF16)
    actsh = ctx.view(ARW, 0, FSC * TOKH, BF16)
    shw = [ctx.view(ARW, 2 * FSC * TOKH + i * 2 * DCH * P, DCH * P, BF16)
           for i in range(2)]
    sdrow = [ctx.view(ARW, 2 * FSC * TOKH + 4 * DCH * P + i * 2 * D, D, BF16)
             for i in range(2)]
    xres = ctx.view(ARW, 0, D, F32)
    sumb = ctx.view(ARW, 4 * D, D, F32)
    lnsq = ctx.view(ARW, 8 * D, D, F32)
    obuf = ctx.view(ARW, 12 * D, D, F32)
    act_e = ctx.sb("act_e", [P, FEC * TOKH], BF16)
    gwb = ctx.sb("gwb", [P, DCH * NE], BF16)
    comb = ctx.sb("comb", [P, HT * NE], F32)
    scores = ctx.sb("scores", [P, NE], F32)
    top8 = ctx.sb("top8", [P, 8], F32)
    msk = ctx.sb("msk", [P, NE], F32)
    dnm = ctx.sb("dnm", [P, 2], F32)
    xrow = [ctx.sb(f"xrow2_{i}", [P, D], BF16) for i in range(2)]
    _sg0 = ctx.sb("silu_g0", [P, TB], BF16)
    _sg1 = ctx.sb("silu_g1", [P, TB], BF16)
    silu_g = {"silu0": _sg0, "silu1": _sg1}
    stat = ctx.sb("stat2", [P, 8], F32)

    pg.dma("gpsimd", gwb[:].rearrange("p (c n) -> p c n", n=NE),
           gw.ap().rearrange("(c p) n -> p c n", p=P), stream="w0")
    pg.barrier()

    out_toks = []
    for hf in range(NH_):
        for tt in range(HT):
            g_tt = hf * HT + tt
            xr = xrow[tt % 2]
            t_lx = pg.dma("gpsimd", xr[:, 0:D], xs[g_tt * P:(g_tt + 1) * P, :],
                          waits=[pg.tok("s_pe")], stream=f"x{tt % 2}")
            x3d_d = x_fm.rearrange("p (a n) -> p a n", n=TOKH)
            for h8 in range(ceil_div(DCH, 8)):
                nn = min(8, DCH - h8 * 8)
                ctx.transpose_batch(
                    [xr[:, (h8 * 8 + i) * P:(h8 * 8 + i + 1) * P]
                     for i in range(nn)],
                    x3d_d[:, h8 * 8:h8 * 8 + nn, tt * P:(tt + 1) * P],
                    "pb7", waits=[t_lx], evac_waits=[pg.tok("s_pe")])
        t_xfm = pg.tok("s_dve")
        t_ai = pg.op("gpsimd", lambda E: E.memset(acc[:], 0.0),
                     waits=[pg.tok("s_dve"), pg.tok("q_sync_o0"), pg.tok("q_sync_o1")])
        for tt in range(HT):
            mm = None
            for dc in range(DCH):
                st = dc == 0
                w_ = [t_xfm]
                if st:
                    w_.append(ctx.bank_r["pb6"])
                mm = pg.op("tensor",
                           lambda E, dc=dc, tt=tt, st=st:
                           E.matmul(ctx.pt["pb6"][:, 0:NE],
                                    x_fm[:, dc * TOKH + tt * P: dc * TOKH + (tt + 1) * P],
                                    gwb[:, dc * NE:(dc + 1) * NE],
                                    start=st, stop=(dc == DCH - 1)),
                           waits=w_)
            ctx.bank_w["pb6"] = mm
            t_sg = pg.op("scalar",
                         lambda E: E.activation(scores[:, 0:NE], ctx.pt["pb6"][:, 0:NE],
                                                AF.Sigmoid),
                         waits=[mm, pg.tok("s_dve"), pg.tok("s_act")])
            ctx.bank_r["pb6"] = t_sg
            t_t8 = pg.op("vector", lambda E: E.max(out=top8[:, 0:8], in_=scores[:, 0:NE]),
                         waits=[t_sg])
            t_mk = pg.op("vector",
                         lambda E: E.tensor_single_scalar(msk[:, 0:NE], scores[:, 0:NE],
                                                          top8[:, TOPK - 1:TOPK], ALU.is_ge),
                         waits=[t_t8])
            t_ws = pg.op("vector", lambda E: E.tensor_mul(msk[:, 0:NE], msk[:, 0:NE],
                                                          scores[:, 0:NE]),
                         waits=[t_mk])
            t_dn = pg.op("vector", lambda E: E.reduce_sum(dnm[:, 0:1], msk[:, 0:NE],
                                                          axis=AX.X),
                         waits=[t_ws])
            t_dn = pg.op("vector", lambda E: E.tensor_scalar_add(dnm[:, 0:1], dnm[:, 0:1],
                                                                 1e-20),
                         waits=[t_dn])
            t_dn = pg.op("vector", lambda E: E.reciprocal(dnm[:, 1:2], dnm[:, 0:1]),
                         waits=[t_dn])
            t_cb = pg.op("vector",
                         lambda E, tt=tt: E.tensor_scalar_mul(
                             comb[:, tt * NE:(tt + 1) * NE], msk[:, 0:NE], dnm[:, 1:2]),
                         waits=[t_dn])
        t_comb = pg.tok("s_dve")
        for ft in range(FSC):
            t_lg = pg.dma("gpsimd", shw[0].rearrange("p (c n) -> p c n", n=P),
                          sg[:, ft * P:(ft + 1) * P].rearrange("(c p) n -> p c n", p=P),
                          waits=[pg.tok("s_pe")], stream="sh0")
            t_lu = pg.dma("gpsimd", shw[1].rearrange("p (c n) -> p c n", n=P),
                          su[:, ft * P:(ft + 1) * P].rearrange("(c p) n -> p c n", p=P),
                          waits=[pg.tok("s_pe")], stream="sh1")
            for tb in range(NTB):
                un = (ft * NTB + tb) % 2
                gb = ["pb0", "pb2"][un]
                ub = ["pb1", "pb3"][un]
                sgb = ["silu0", "silu1"][un]
                mmg = mmu = None
                for dc in range(DCH):
                    st = dc == 0
                    wg_ = [t_lg, t_xfm]
                    wu_ = [t_lu]
                    if st:
                        wg_.append(ctx.bank_r[gb])
                        wu_.append(ctx.bank_r[ub])
                    mmg = pg.op("tensor",
                                lambda E, dc=dc, tb=tb, st=st, gb=gb:
                                E.matmul(ctx.pt[gb][:, 0:TB],
                                         shw[0][:, dc * P:(dc + 1) * P],
                                         x_fm[:, dc * TOKH + tb * TB: dc * TOKH + (tb + 1) * TB],
                                         start=st, stop=(dc == DCH - 1)),
                                waits=wg_)
                    mmu = pg.op("tensor",
                                lambda E, dc=dc, tb=tb, st=st, ub=ub:
                                E.matmul(ctx.pt[ub][:, 0:TB],
                                         shw[1][:, dc * P:(dc + 1) * P],
                                         x_fm[:, dc * TOKH + tb * TB: dc * TOKH + (tb + 1) * TB],
                                         start=st, stop=(dc == DCH - 1)),
                                waits=wu_)
                ctx.bank_w[gb] = mmg
                ctx.bank_w[ub] = mmu
                t_si = pg.op("scalar",
                             lambda E, sgb=sgb, gb=gb: E.activation(
                                 silu_g[sgb][:, 0:TB], ctx.pt[gb][:, 0:TB], AF.Silu),
                             waits=[mmg, pg.tok("s_act")])
                ctx.bank_r[gb] = t_si
                t_mu = pg.op("vector",
                             lambda E, ft=ft, tb=tb, sgb=sgb, ub=ub: E.tensor_tensor(
                                 actsh[:, ft * TOKH + tb * TB: ft * TOKH + (tb + 1) * TB],
                                 silu_g[sgb][:, 0:TB], ctx.pt[ub][:, 0:TB], ALU.mult),
                             waits=[t_si, mmu])
                ctx.bank_r[ub] = t_mu
        t_ash = pg.tok("s_dve")
        BLKT = min(256, TOKH)
        NSUB = BLKT // P
        for blk in range(TOKH // BLKT):
            mm_l = [[None] * NB for _ in range(NSUB)]
            for ft in range(FSC):
                sb_ = sdrow[ft % 2]
                t_ld = pg.dma("gpsimd", sb_, sd[ft * P:(ft + 1) * P, :],
                              waits=[pg.tok("s_pe")], stream=f"sd{ft % 2}")
                for sub in range(NSUB):
                    lt = blk * BLKT + sub * P
                    for nb in range(NB):
                        bank = f"pb{sub * NB + nb}"
                        st = ft == 0
                        w_ = [t_ld, t_ash]
                        if st:
                            w_.append(ctx.bank_r[bank])
                        mm_l[sub][nb] = pg.op(
                            "tensor",
                            lambda E, bank=bank, sb_=sb_, nb=nb, lt=lt, ft=ft, st=st:
                            E.matmul(ctx.pt[bank][:, 0:DBW],
                                     actsh[:, ft * TOKH + lt: ft * TOKH + lt + P],
                                     sb_[:, nb * DBW:(nb + 1) * DBW],
                                     start=st, stop=(ft == FSC - 1)),
                            waits=w_)
            for sub in range(NSUB):
                tile = blk * NSUB + sub
                t_a = None
                for nb in range(NB):
                    bank = f"pb{sub * NB + nb}"
                    t_a = pg.op("vector",
                                lambda E, bank=bank, tile=tile, nb=nb: E.tensor_add(
                                    acc[:, tile * D + nb * DBW: tile * D + (nb + 1) * DBW],
                                    ctx.pt[bank][:, 0:DBW],
                                    acc[:, tile * D + nb * DBW: tile * D + (nb + 1) * DBW]),
                                waits=[mm_l[sub][nb], t_ai, pg.tok("s_dve")])
                    ctx.bank_r[bank] = t_a
        pg.barrier()

        t_wg = pg.dma("gpsimd", wgbuf.rearrange("p (c n) -> p c n", n=FE),
                      wg[0].rearrange("(c p) n -> p c n", p=P), stream="we0")
        t_wu = pg.dma("gpsimd", wubuf.rearrange("p (c n) -> p c n", n=FE),
                      wu[0].rearrange("(c p) n -> p c n", p=P), stream="we1")
        t_wd = pg.dma("gpsimd", wdbuf.rearrange("p (c n) -> p c n", n=D),
                      wd[0].rearrange("(c p) n -> p c n", p=P), stream="we2")
        for e in range(NE):
            for ft in range(FEC):
                for tb in range(NTB):
                    un = (ft * NTB + tb) % 2
                    gb = ["pb0", "pb2"][un]
                    ub = ["pb1", "pb3"][un]
                    sg_ = ["silu0", "silu1"][un]
                    mmg = mmu = None
                    for dc in range(DCH):
                        st = dc == 0
                        wg_ = [t_wg]
                        wu_ = [t_wu]
                        if st:
                            wg_.append(ctx.bank_r[gb])
                            wu_.append(ctx.bank_r[ub])
                        mmg = pg.op("tensor",
                                    lambda E, dc=dc, ft=ft, tb=tb, st=st, gb=gb:
                                    E.matmul(ctx.pt[gb][:, 0:TB],
                                             wgbuf[:, dc * FE + ft * P: dc * FE + (ft + 1) * P],
                                             x_fm[:, dc * TOKH + tb * TB: dc * TOKH + (tb + 1) * TB],
                                             start=st, stop=(dc == DCH - 1)),
                                    waits=wg_)
                        mmu = pg.op("tensor",
                                    lambda E, dc=dc, ft=ft, tb=tb, st=st, ub=ub:
                                    E.matmul(ctx.pt[ub][:, 0:TB],
                                             wubuf[:, dc * FE + ft * P: dc * FE + (ft + 1) * P],
                                             x_fm[:, dc * TOKH + tb * TB: dc * TOKH + (tb + 1) * TB],
                                             start=st, stop=(dc == DCH - 1)),
                                    waits=wu_)
                    ctx.bank_w[gb] = mmg
                    ctx.bank_w[ub] = mmu
                    t_si = pg.op("scalar",
                                 lambda E, gb=gb, sgb=sg_: E.activation(
                                     silu_g[sgb][:, 0:TB],
                                     ctx.pt[gb][:, 0:TB], AF.Silu),
                                 waits=[mmg, pg.tok("s_act")])
                    ctx.bank_r[gb] = t_si
                    t_mu = pg.op("vector",
                                 lambda E, ft=ft, tb=tb, ub=ub, sgb=sg_: E.tensor_tensor(
                                     act_e[:, ft * TOKH + tb * TB: ft * TOKH + (tb + 1) * TB],
                                     silu_g[sgb][:, 0:TB], ctx.pt[ub][:, 0:TB], ALU.mult),
                                 waits=[t_si, mmu])
                    ctx.bank_r[ub] = t_mu
            t_act = pg.tok("s_dve")
            if e + 1 < NE:
                t_wg = pg.dma("gpsimd", wgbuf.rearrange("p (c n) -> p c n", n=FE),
                              wg[e + 1].rearrange("(c p) n -> p c n", p=P),
                              waits=[pg.tok("s_pe")], stream="we0")
                t_wu = pg.dma("gpsimd", wubuf.rearrange("p (c n) -> p c n", n=FE),
                              wu[e + 1].rearrange("(c p) n -> p c n", p=P),
                              waits=[pg.tok("s_pe")], stream="we1")
            for blk in range(TOKH // BLKT):
                mm_l = [[None] * NB for _ in range(NSUB)]
                for ft in range(FEC):
                    for sub in range(NSUB):
                        lt = blk * BLKT + sub * P
                        for nb in range(NB):
                            bank = f"pb{sub * NB + nb}"
                            st = ft == 0
                            w_ = [t_wd, t_act]
                            if st:
                                w_.append(ctx.bank_r[bank])
                            mm_l[sub][nb] = pg.op(
                                "tensor",
                                lambda E, bank=bank, ft=ft, nb=nb, lt=lt, st=st:
                                E.matmul(ctx.pt[bank][:, 0:DBW],
                                         act_e[:, ft * TOKH + lt: ft * TOKH + lt + P],
                                         wdbuf[:, ft * D + nb * DBW: ft * D + (nb + 1) * DBW],
                                         start=st, stop=(ft == FEC - 1)),
                                waits=w_)
                for sub in range(NSUB):
                    tile = blk * NSUB + sub
                    t_a = None
                    for nb in range(NB):
                        bank = f"pb{sub * NB + nb}"
                        t_a = pg.op("vector",
                                    lambda E, bank=bank, tile=tile, nb=nb, e=e:
                                    E.scalar_tensor_tensor(
                                        acc[:, tile * D + nb * DBW: tile * D + (nb + 1) * DBW],
                                        ctx.pt[bank][:, 0:DBW],
                                        comb[:, tile * NE + e: tile * NE + e + 1],
                                        acc[:, tile * D + nb * DBW: tile * D + (nb + 1) * DBW],
                                        ALU.mult, ALU.add),
                                    waits=[mm_l[sub][nb], t_comb, pg.tok("s_dve")])
                        ctx.bank_r[bank] = t_a
            if e + 1 < NE:
                t_wd = pg.dma("gpsimd", wdbuf.rearrange("p (c n) -> p c n", n=D),
                              wd[e + 1].rearrange("(c p) n -> p c n", p=P),
                              waits=[pg.tok("s_pe")], stream="we2")
        pg.barrier()

        for tt in range(HT):
            g_tt = hf * HT + tt
            t_rx = pg.dma("sync", xres, xs[g_tt * P:(g_tt + 1) * P, :],
                          waits=[pg.tok("s_dve")], stream="r0")
            t_s = pg.op("vector",
                        lambda E, tt=tt: E.tensor_add(sumb[:, 0:D],
                                                      acc[:, tt * D:(tt + 1) * D],
                                                      xres[:, 0:D]),
                        waits=[t_rx, pg.tok("s_dve")])
            t_o = emit_layernorm(ctx, sumb, D, stat, obuf[:, 0:D],
                                 waits=[t_s, out_toks[-1] if out_toks else None],
                                 sq_scratch=lnsq)
            t_out = pg.dma("sync", out[g_tt * P:(g_tt + 1) * P, :], obuf[:, 0:D],
                           waits=[t_o], stream=f"o{tt % 2}")
            out_toks.append(t_out)
        pg.barrier()

    if standalone:
        return ctx.finish(pg.all_dma_toks())
    return None





def build_fused(B=2, S=8192, D=2048, T=2048, NH0=16, NKV=8, R=32, HID=5120,
                NH1=16, KVC=512, QC=1024, NE=32, FE=768, TOPK=6, FSH=1536):
    ctx = Ctx(psum_spec="both")
    pg = ctx.pg
    KVD = NKV * 128
    QG0 = min(1024, T)

    ins = {}
    def inp(name, shape, dtype=F32):
        ins[name] = ctx.inp(name, shape, dtype)
        return ins[name]

    inp("xb", (S, D))
    for nm, sh in (("W_q", (D, D)), ("W_k", (KVD, D)), ("W_v", (KVD, D)),
                   ("W_o", (D, D)), ("A_q", (R, D)), ("B_q", (D, R)),
                   ("A_k", (R, D)), ("B_k", (KVD, R)), ("A_v", (R, D)),
                   ("B_v", (KVD, R)), ("A_o", (R, D)), ("B_o", (D, R)),
                   ("s_q", (1,)), ("s_k", (1,)), ("s_v", (1,)), ("s_o", (1,)),
                   ("q_gain", (NH0,)), ("cosk", (S, 32)), ("sink", (S, 32)),
                   ("kbias", (P, S // P)), ("bandm", (P, 2 * QG0 - P)),
                   ("w_fc", (HID, D)), ("w_proj", (D, HID)),
                   ("w_kv_down", (D, KVC)), ("w_kv_up_k", (KVC, D)),
                   ("w_kv_up_v", (KVC, D)), ("w_q_down", (D, QC)),
                   ("w_q_up", (QC, D)), ("w_k_rope", (D, 128)),
                   ("w_q_rope", (D, 128)), ("w_out", (D, D)),
                   ("cos1", (S, 64)), ("sin1", (S, 64)),
                   ("cos1q", (T, 64)), ("sin1q", (T, 64)),
                   ("gate_w", (D, NE)), ("wg", (NE, D, FE)), ("wu", (NE, D, FE)),
                   ("wd", (NE, FE, D)), ("sg", (D, FSH)), ("su", (D, FSH)),
                   ("sd", (FSH, D))):
        inp(nm, sh)
    out = ctx.outp("out", (T, D))

    x1 = ctx.scratch("fx1", [T, D], F32)
    x2 = ctx.scratch("fx2", [T, D], F32)
    x2bf = ctx.scratch("fx2bf", [T, D], BF16)
    x2g = ctx.scratch("fx2g", [S, D], BF16)
    x3 = ctx.scratch("fx3", [T, D], F32)

    ctx.setup_consts()
    ctx.mark_persist()

    def phase_gap():
        pg.barrier()
        ctx.reset_pool()
        ctx.reset_banks(None)

    extA = {k: ins[k] for k in ("xb", "W_q", "W_k", "W_v", "W_o", "A_q", "B_q",
                                "A_k", "B_k", "A_v", "B_v", "A_o", "B_o",
                                "s_q", "s_k", "s_v", "s_o", "q_gain",
                                "cosk", "sink", "kbias", "bandm")}
    extA["out"] = x1
    build_l0_attn(D=D, NH=NH0, NKV=NKV, S=S, T=T, R=R, ctx=ctx, ext=extA)
    phase_gap()

    extB = {"xslab": x1, "w_fc": ins["w_fc"], "w_proj": ins["w_proj"], "out": x2}
    build_l0_ffn(D=D, HID=HID, T=T, ctx=ctx, ext=extB, x2bf=x2bf)
    phase_gap()

    pg.sem("cc")
    rg = [[0, 1, 2, 3], [4, 5, 6, 7]]
    t_cc = pg.op("gpsimd",
                 lambda E: E.collective_compute(
                     "AllGather", mybir.AluOpType.bypass,
                     replica_groups=rg,
                     ins=[x2bf.ap().opt()], outs=[x2g.ap().opt()]),
                 waits=pg.all_dma_toks(), inc="cc", inc_amt=1)
    ctx.reset_banks(None)

    extC = {"xloc": x2, "xg": x2g, "out": x3, "cc_tok": t_cc,
            "cos1": ins["cos1"], "sin1": ins["sin1"],
            "cos1q": ins["cos1q"], "sin1q": ins["sin1q"]}
    for k in ("w_kv_down", "w_kv_up_k", "w_kv_up_v", "w_q_down", "w_q_up",
              "w_k_rope", "w_q_rope", "w_out"):
        extC[k] = ins[k]
    build_l1_attn(D=D, NH=NH1, KVC=KVC, QC=QC, S=S, T=T, ctx=ctx, ext=extC,
                  split=True)
    phase_gap()

    extD = {"xslab": x3, "out": out}
    for k in ("gate_w", "wg", "wu", "wd", "sg", "su", "sd"):
        extD[k] = ins[k]
    build_l1_ffn(D=D, NE=NE, FE=FE, TOPK=TOPK, FSH=FSH, T=T, ctx=ctx, ext=extD)

    return ctx.finish(pg.all_dma_toks())



def _ident_np():
    return np.eye(P, dtype=BF16NP)


def make_rope_l1(S, HD=128):
    inv = (1.0 / np.float32(10000.0) ** (np.arange(0, HD, 2, dtype=np.float32) / np.float32(HD))).astype(np.float32)
    ang = (np.arange(S, dtype=np.float32)[:, None] * inv[None, :]).astype(np.float32)
    return np.cos(ang).astype(np.float32), np.sin(ang).astype(np.float32)


def make_rope_l0(S, ROPE=64):
    inv_freq = (1.0 / np.float32(10000.0) ** (np.arange(0, ROPE, 2, dtype=np.float32) / np.float32(ROPE))).astype(np.float32)
    freqs = np.outer(np.arange(S, dtype=np.float32), inv_freq).astype(np.float32)
    return np.cos(freqs).astype(np.float32), np.sin(freqs).astype(np.float32)


def make_kbias(t0, S, T):
    KT = S // P
    kb = np.zeros((P, KT), np.float32)
    for kt in range(KT):
        if kt < T // P:
            continue
        keys = kt * P + np.arange(P)
        kb[:, kt] = np.where(keys >= S - t0, 0.0, -1e9)
    return kb


def make_bandm(QG):
    c = np.arange(2 * QG - P)[None, :]
    p = np.arange(P)[:, None]
    return np.where(c >= p + (QG - P), 0.0, -1e9).astype(np.float32)


_BUILD_CACHE = {}


def _get(name, fn):
    if name not in _BUILD_CACHE:
        _BUILD_CACHE[name] = fn()
    return _BUILD_CACHE[name]


def kernel(**inputs):
    B, S, D = 2, 8192, 2048
    T = S // 4
    NC = 8
    ident = _ident_np()
    identf = np.eye(P, dtype=np.float32)
    f32 = np.float32

    def rolled(xb, t0):
        return np.ascontiguousarray(np.roll(xb, -t0, axis=0), dtype=f32)

    cos0, sin0 = make_rope_l0(S)
    cos1np, sin1np = make_rope_l1(S)
    bandm = make_bandm(1024)
    ncF = _get("F", build_fused)
    x = np.asarray(inputs["L0_attn_x"], f32)
    wmap = {}
    for nm in ("W_q", "W_k", "W_v", "W_o", "A_q", "B_q", "A_k", "B_k",
               "A_v", "B_v", "A_o", "B_o", "s_q", "s_k", "s_v", "s_o", "q_gain"):
        wmap[nm] = np.asarray(inputs["L0_attn_" + nm], f32)
    wmap["w_fc"] = np.asarray(inputs["L0_ffn_w_fc"], f32)
    wmap["w_proj"] = np.asarray(inputs["L0_ffn_w_proj"], f32)
    for nm in ("w_kv_down", "w_kv_up_k", "w_kv_up_v", "w_q_down", "w_q_up",
               "w_k_rope", "w_q_rope", "w_out"):
        wmap[nm] = np.asarray(inputs["L1_attn_" + nm], f32)
    for nm, src in (("gate_w", "L1_ffn_gate_w"), ("wg", "L1_ffn_wg"),
                    ("wu", "L1_ffn_wu"), ("wd", "L1_ffn_wd"),
                    ("sg", "L1_ffn_sg"), ("su", "L1_ffn_su"),
                    ("sd", "L1_ffn_sd")):
        wmap[nm] = np.asarray(inputs[src], f32)
    in_maps = []
    for c in range(NC):
        b, t0 = c // 4, (c % 4) * T
        m = dict(wmap)
        m["xb"] = rolled(x[b], t0)
        m["cosk"] = rolled(cos0, t0)
        m["sink"] = rolled(sin0, t0)
        m["kbias"] = make_kbias(t0, S, T)
        m["bandm"] = bandm
        m["cos1"] = cos1np
        m["sin1"] = sin1np
        m["cos1q"] = np.ascontiguousarray(cos1np[t0:t0 + T])
        m["sin1q"] = np.ascontiguousarray(sin1np[t0:t0 + T])
        m["c_ident"] = ident
        m["c_identf"] = identf
        in_maps.append(m)
    res = run_bass_kernel_spmd(ncF, in_maps, core_ids=list(range(NC)))
    y = np.empty((B, S, D), f32)
    for c in range(NC):
        b, t0 = c // 4, (c % 4) * T
        y[b, t0:t0 + T] = res.results[c]["out"]
    return y


def kernel_multilaunch(**inputs):
    B, S, D = 2, 8192, 2048
    T = S // 4
    NC = 8
    ident = _ident_np()
    identf = np.eye(P, dtype=np.float32)
    f32 = np.float32

    def rolled(xb, t0):
        return np.ascontiguousarray(np.roll(xb, -t0, axis=0), dtype=f32)

    cos0, sin0 = make_rope_l0(S)
    cos1np, sin1np = make_rope_l1(S)
    bandm = make_bandm(1024)

    ncA = _get("A", build_l0_attn)
    x = np.asarray(inputs["L0_attn_x"], f32)
    in_maps = []
    for c in range(NC):
        b, t0 = c // 4, (c % 4) * T
        m = {"xb": rolled(x[b], t0),
             "cosk": rolled(cos0, t0), "sink": rolled(sin0, t0),
             "kbias": make_kbias(t0, S, T), "bandm": bandm, "c_ident": ident,
             "c_identf": identf}
        for nm in ("W_q", "W_k", "W_v", "W_o", "A_q", "B_q", "A_k", "B_k",
                   "A_v", "B_v", "A_o", "B_o", "s_q", "s_k", "s_v", "s_o",
                   "q_gain"):
            m[nm] = np.asarray(inputs["L0_attn_" + nm], f32)
        in_maps.append(m)
    res = run_bass_kernel_spmd(ncA, in_maps, core_ids=list(range(NC)))
    x1 = np.empty((B, S, D), f32)
    for c in range(NC):
        b, t0 = c // 4, (c % 4) * T
        x1[b, t0:t0 + T] = res.results[c]["out"]

    ncB = _get("B", build_l0_ffn)
    in_maps = []
    for c in range(NC):
        b, t0 = c // 4, (c % 4) * T
        in_maps.append({"xslab": np.ascontiguousarray(x1[b, t0:t0 + T]),
                        "w_fc": np.asarray(inputs["L0_ffn_w_fc"], f32),
                        "w_proj": np.asarray(inputs["L0_ffn_w_proj"], f32),
                        "c_ident": ident, "c_identf": identf})
    res = run_bass_kernel_spmd(ncB, in_maps, core_ids=list(range(NC)))
    x2 = np.empty((B, S, D), f32)
    for c in range(NC):
        b, t0 = c // 4, (c % 4) * T
        x2[b, t0:t0 + T] = res.results[c]["out"]

    ncC = _get("C", build_l1_attn)
    in_maps = []
    for c in range(NC):
        b, t0 = c // 4, (c % 4) * T
        m = {"xb": rolled(x2[b], t0),
             "cos1": rolled(cos1np, t0), "sin1": rolled(sin1np, t0),
             "c_ident": ident, "c_identf": identf}
        for nm in ("w_kv_down", "w_kv_up_k", "w_kv_up_v", "w_q_down", "w_q_up",
                   "w_k_rope", "w_q_rope", "w_out"):
            m[nm] = np.asarray(inputs["L1_attn_" + nm], f32)
        in_maps.append(m)
    res = run_bass_kernel_spmd(ncC, in_maps, core_ids=list(range(NC)))
    x3 = np.empty((B, S, D), f32)
    for c in range(NC):
        b, t0 = c // 4, (c % 4) * T
        x3[b, t0:t0 + T] = res.results[c]["out"]

    ncD = _get("D", build_l1_ffn)
    in_maps = []
    for c in range(NC):
        b, t0 = c // 4, (c % 4) * T
        in_maps.append({"xslab": np.ascontiguousarray(x3[b, t0:t0 + T]),
                        "gate_w": np.asarray(inputs["L1_ffn_gate_w"], f32),
                        "wg": np.asarray(inputs["L1_ffn_wg"], f32),
                        "wu": np.asarray(inputs["L1_ffn_wu"], f32),
                        "wd": np.asarray(inputs["L1_ffn_wd"], f32),
                        "sg": np.asarray(inputs["L1_ffn_sg"], f32),
                        "su": np.asarray(inputs["L1_ffn_su"], f32),
                        "sd": np.asarray(inputs["L1_ffn_sd"], f32),
                        "c_ident": ident, "c_identf": identf})
    res = run_bass_kernel_spmd(ncD, in_maps, core_ids=list(range(NC)))
    y = np.empty((B, S, D), f32)
    for c in range(NC):
        b, t0 = c // 4, (c % 4) * T
        y[b, t0:t0 + T] = res.results[c]["out"]
    return y
